# version 01
# baseline (speedup 1.0000x reference)
import math
from contextlib import ExitStack

import numpy as np

import concourse.bass as bass
import concourse.mybir as mybir
from concourse.bass_utils import run_bass_kernel_spmd

F32 = mybir.dt.float32
BF16 = mybir.dt.bfloat16
AF = mybir.ActivationFunctionType
ALU = mybir.AluOpType
AX = mybir.AxisListType

D = 1024
NH = 16
DH = 64
NP = NH // 2
NCH = 8
SCALE = 1.0 / math.sqrt(DH)
EPS = 1e-5
IFF = 2048
NE = 8
NCORES = 8


class Sem:
    def __init__(self, h):
        self.h = h
        self.n = 0


class Plan:
    ENGINES = ("tensor", "vector", "scalar", "gpsimd", "sync")

    def __init__(self):
        self.q = {e: [] for e in self.ENGINES}

    def op(self, eng, fn, waits=(), inc=None):
        if inc is not None:
            sem, k = inc
            sem.n += k
            inc = (sem, k)
        self.q[eng].append((tuple(waits), fn, inc))

    def realize(self, block):
        def runner(name):
            def run(e):
                last = {}
                for waits, fn, inc in self.q[name]:
                    for sem, val in waits:
                        if val > 0 and last.get(sem, 0) < val:
                            e.wait_ge(sem.h, val)
                            last[sem] = val
                    r = fn(e)
                    if inc is not None:
                        r.then_inc(inc[0].h, inc[1])
            return run
        block.tensor(runner("tensor"))
        block.vector(runner("vector"))
        block.scalar(runner("scalar"))
        block.gpsimd(runner("gpsimd"))
        block.sync(runner("sync"))


def MM(out, lhsT, rhs, start, stop):
    return lambda e, o=out, l=lhsT, r=rhs, s1=start, s2=stop: e.matmul(
        o, l, r, start=s1, stop=s2)


def TRN(out, in_, ident):
    return lambda e, o=out, i=in_, d=ident: e.transpose(o, i, d)


def ACT(out, in_, func, bias=0.0, scale=1.0):
    return lambda e, o=out, i=in_, f=func, b=bias, s=scale: e.activation(
        o, i, f, bias=b, scale=s)


def TS(out, in0, s1, s2, op0, op1=ALU.bypass):
    return lambda e, o=out, i=in0, a=s1, b=s2, p0=op0, p1=op1: e.tensor_scalar(
        o, i, a, b, p0, p1)


def TT(out, in0, in1, op):
    return lambda e, o=out, a=in0, b=in1, p=op: e.tensor_tensor(o, a, b, p)


def STT(out, in0, scalar, in1, op0, op1):
    return lambda e, o=out, a=in0, s=scalar, b=in1, p0=op0, p1=op1: \
        e.scalar_tensor_tensor(o, a, s, b, p0, p1)


def CP(out, in_):
    return lambda e, o=out, i=in_: e.tensor_copy(o, i)


def RED(out, in_, op):
    return lambda e, o=out, i=in_, p=op: e.tensor_reduce(o, i, AX.X, p)


def RECIP(out, in_):
    return lambda e, o=out, i=in_: e.reciprocal(o, i)


def DMA(out, in_):
    return lambda e, o=out, i=in_: e.dma_start(out=o, in_=i)


def MEMSET(ap, v):
    return lambda e, a=ap, vv=v: e.memset(a, vv)


def SCOPY(out, in_):
    return lambda e, o=out, i=in_: e.copy(o, i)


class Bag(dict):
    __getattr__ = dict.__getitem__
    __setattr__ = dict.__setitem__


def alloc_bufs(nc, ctx, S, l1):
    E = ctx.enter_context
    b = Bag()
    b.S = S
    b.nt = S // 128
    b.nqb = S // 512
    b.nkb = S // 128
    b.ps_q = E(nc.psum_tensor("ps_q", [128, 2048], F32))
    b.ps_a = E(nc.psum_tensor("ps_a", [128, 512], F32))
    b.ps_b = E(nc.psum_tensor("ps_b", [128, 512], F32))
    b.ps_c = E(nc.psum_tensor("ps_c", [128, 512], F32))
    b.ps_d = E(nc.psum_tensor("ps_d", [128, 512], F32))
    b.ps_tr = [b.ps_c[:].bitcast(BF16), b.ps_d[:].bitcast(BF16)]
    xnt_cols = max(NCH * S, 2 * IFF * NCH) if l1 else NCH * S
    b.xnT = E(nc.sbuf_tensor("xnT", [128, xnt_cols], BF16))
    b.ctxT = E(nc.sbuf_tensor("ctxT", [128, NCH * S], BF16))
    arc_cols = max(2 * S + (S // 128) * 192, IFF * NCH)
    b.arC = E(nc.sbuf_tensor("arC", [128, arc_cols], BF16))
    b.qp = b.arC[:, 0:S]
    b.kp = b.arC[:, S:2 * S]
    b.vT = b.arC[:, 2 * S:2 * S + (S // 128) * 192]
    b.arE = E(nc.sbuf_tensor("arE", [128, 16 * 512], BF16))
    b.wq = b.arE[:, 0:1024]
    b.wk = b.arE[:, 1024:2048]
    b.wv = b.arE[:, 2048:3072]
    b.pt = b.arE[:, 3072:5120]
    b.g_sb = b.arE[:, 0:4096]
    b.u_sb = b.arE[:, 4096:8192]
    b.arF = E(nc.sbuf_tensor("arF", [128, NCH * D], BF16))
    b.wo = b.arF[:, :]
    b.slab = [b.arF[:, 0:2048], b.arF[:, 2048:4096]]
    b.xsq = b.arF[:].bitcast(F32)[:, 0:D]
    b.xn_tm = [b.arF[:, 2048:3072], b.arF[:, 3072:4096]]
    b.lo_tm = b.arF[:, 4096:5120]
    b.x_t = [E(nc.sbuf_tensor(f"x_t{i}", [128, D], F32)) for i in range(2)]
    b.st = [E(nc.sbuf_tensor(f"st{i}", [128, 8], F32)) for i in range(2)]
    b.vtmp = [E(nc.sbuf_tensor(f"vtmp{i}", [128, 512], BF16)) for i in range(2)]
    b.bc_sb = b.vtmp[0]
    b.bo_bc = E(nc.sbuf_tensor("bo_bc", [128, D], BF16))
    b.bq_s = E(nc.sbuf_tensor("bq_s", [128, NP], F32))
    b.bk_s = E(nc.sbuf_tensor("bk_s", [128, NP], F32))
    b.bv_s = E(nc.sbuf_tensor("bv_s", [128, NP], F32))
    b.rowscr = E(nc.sbuf_tensor("rowscr", [65, 512], F32))
    b.recip_t = b.rowscr
    b.nrm = b.rowscr[0:1, 0:512]
    b.u_t = E(nc.sbuf_tensor("u_t", [1, 16], F32))
    b.biasU = [E(nc.sbuf_tensor(f"biasU{i}", [128, 1], F32)) for i in range(2)]
    b.ident = E(nc.sbuf_tensor("ident", [128, 128], BF16))
    b.ones_colA = E(nc.sbuf_tensor("ones_colA", [128, 1], BF16))
    b.ones_colB = E(nc.sbuf_tensor("ones_colB", [128, 1], BF16))
    b.ones_t = E(nc.sbuf_tensor("ones_t", [65, 128], F32))
    b.colofs = E(nc.sbuf_tensor("colofs", [128, NE], F32))
    b.ident8 = E(nc.sbuf_tensor("ident8", [NE, NE], F32))
    if l1:
        nt = b.nt
        arEf = b.arE[:].bitcast(F32)
        b.rw = b.arE[:, 3072:3072 + NCH * NE]
        b.rwd = b.arE[:, 3072 + NCH * NE:3072 + 2 * NCH * NE]
        b.rwf = arEf[:, 1600:1600 + NCH * NE]
        b.rb = E(nc.sbuf_tensor("rb_sb", [NE, 1], F32))
        b.pbuf = [arEf[0:NE, 512:1024], arEf[0:NE, 1024:1536]]
        b.probs_tm = arEf[:, 0:nt * NE]
        b.t2 = arEf[:, nt * NE:nt * NE + 40]
        b.w_rt = E(nc.sbuf_tensor("w_rt_sb", [128, nt * NE], BF16))
        b.gate_v = b.xnT[:, 0:IFF * NCH]
        b.up_v = b.xnT[:, IFF * NCH:2 * IFF * NCH]
        b.down_v = b.arC[:, 0:IFF * NCH]
    return b


def mksems(nc, ctx, sm):
    E = ctx.enter_context
    for name in ("sLD", "sWD", "sXD", "sT", "sV", "sS", "sG",
                 "sXA", "sXB", "sOA", "sOB", "sSA", "sSB"):
        sm[name] = Sem(E(nc.semaphore(name)))
    sm.sXp = (sm.sXA, sm.sXB)
    sm.sOp = (sm.sOA, sm.sOB)
    sm.sSp = (sm.sSA, sm.sSB)


def emit_consts(p, b, sm):
    sG = sm.sG
    sV = sm.sV
    p.op("gpsimd", MEMSET(b.ident[:], 0.0), inc=(sG, 1))
    p.op("gpsimd", lambda e, o=b.ident[:]: e.affine_select(
        out=o, in_=o, compare_op=ALU.not_equal, fill=1.0, base=0,
        pattern=[[-1, 128]], channel_multiplier=1),
        waits=[(sG, sG.n)], inc=(sG, 1))
    p.op("gpsimd", MEMSET(b.ones_colA[:], 0.0), inc=(sG, 1))
    p.op("gpsimd", MEMSET(b.ones_colA[0:64, :], 1.0),
         waits=[(sG, sG.n)], inc=(sG, 1))
    p.op("gpsimd", MEMSET(b.ones_colB[:], 0.0), inc=(sG, 1))
    p.op("gpsimd", MEMSET(b.ones_colB[64:128, :], 1.0),
         waits=[(sG, sG.n)], inc=(sG, 1))
    p.op("gpsimd", MEMSET(b.ones_t[0:1, :], 1.0), inc=(sG, 1))
    p.op("gpsimd", MEMSET(b.ones_t[32:33, :], 1.0), inc=(sG, 1))
    p.op("gpsimd", MEMSET(b.ones_t[64:65, :], 1.0), inc=(sG, 1))
    p.op("gpsimd", lambda e, o=b.colofs[:]: e.iota(
        o, pattern=[[1, NE]], base=0, channel_multiplier=0,
        allow_small_or_imprecise_dtypes=True), inc=(sG, 1))
    p.op("gpsimd", MEMSET(b.ident8[:], 0.0), inc=(sG, 1))
    p.op("gpsimd", lambda e, o=b.ident8[:]: e.affine_select(
        out=o, in_=o, compare_op=ALU.not_equal, fill=1.0, base=0,
        pattern=[[-1, NE]], channel_multiplier=1),
        waits=[(sG, sG.n)], inc=(sG, 1))
    p.op("vector", TS(b.colofs[:], b.colofs[:], 1e-6, None, ALU.mult),
         waits=[(sG, sG.n)], inc=(sV, 1))
    sm.consts_done = sG.n
    sm.colofs_done = sm.sV.n


def emit_bo_bc(p, b, sm):
    sT, sV = sm.sT, sm.sV
    for half, ps, row in ((0, b.ps_c, 0), (1, b.ps_d, 32)):
        p.op("tensor", MM(ps[0:128, 0:512], b.ones_t[row:row + 1, :],
                          b.rowscr[row:row + 1, 0:512], True, True),
             waits=[(sm.sXD, sm.consts_dma_done), (sm.sG, sm.consts_done)],
             inc=(sT, 1))
        p.op("vector", CP(b.bo_bc[:, half * 512:(half + 1) * 512], ps[0:128, 0:512]),
             waits=[(sT, sT.n)], inc=(sV, 1))
    sm.bo_bc_done = sV.n


def emit_ln(p, b, sm, x_hbm, xn_hbm=None, router=False):
    sXD, sT, sV, sS, sG = sm.sXD, sm.sT, sm.sV, sm.sS, sm.sG
    S, nt = b.S, b.nt
    sq_done, xn_done, evac_done = {}, {}, {}
    rmm_done = {-1: 0, -2: 0}
    sig_done = {}
    tr_evac = {}
    loroll = [b.arC[:, 0:NCH * 512], b.arC[:, NCH * 512:2 * NCH * 512]]
    for i in range(nt):
        bb = i % 2
        sx = sm.sXp[bb]
        waits = []
        if i >= 2:
            waits = [(sS, sq_done[i - 2]), (sV, xn_done[i - 2])]
        p.op("sync", DMA(b.x_t[bb][:], x_hbm.ap()[i * 128:(i + 1) * 128, :]),
             waits=waits, inc=(sx, 16))
        xd = sx.n
        st = b.st[bb]
        p.op("vector", RED(st[:, 0:1], b.x_t[bb][:], ALU.add), waits=[(sx, xd)],
             inc=(sV, 1))
        sqw = [(sx, xd)]
        if i >= 1:
            sqw.append((sV, (xn_done[i - 1] if router else
                             sm.get(("s2", i - 1), 0))))
        p.op("scalar", lambda e, o=b.xsq[:], i_=b.x_t[bb][:]: e.square(o, i_),
             waits=sqw, inc=(sS, 1))
        sq_done[i] = sS.n
        p.op("vector", RED(st[:, 1:2], b.xsq[:], ALU.add),
             waits=[(sS, sq_done[i])], inc=(sV, 1))
        sm[("s2", i)] = sV.n
        for fn in (TS(st[:, 2:3], st[:, 0:1], 1.0 / D, None, ALU.mult),
                   TS(st[:, 3:4], st[:, 1:2], 1.0 / D, None, ALU.mult),
                   TT(st[:, 4:5], st[:, 2:3], st[:, 2:3], ALU.mult),
                   TT(st[:, 5:6], st[:, 3:4], st[:, 4:5], ALU.subtract),
                   TS(st[:, 5:6], st[:, 5:6], EPS, None, ALU.add)):
            p.op("vector", fn, waits=[(sV, sV.n)], inc=(sV, 1))
        var_done = sV.n
        p.op("scalar", ACT(st[:, 6:7], st[:, 5:6], AF.Sqrt),
             waits=[(sV, var_done)], inc=(sS, 1))
        sd_done = sS.n
        p.op("vector", RECIP(st[:, 7:8], st[:, 6:7]), waits=[(sS, sd_done)],
             inc=(sV, 1))
        if not router:
            p.op("vector", TS(b.xn_tm[bb][:], b.x_t[bb][:], st[:, 2:3],
                              st[:, 7:8], ALU.subtract, ALU.mult),
                 waits=[(sV, sV.n)], inc=(sV, 1))
        else:
            p.op("vector", TS(b.xsq[:], b.x_t[bb][:], st[:, 2:3],
                              st[:, 7:8], ALU.subtract, ALU.mult),
                 waits=[(sV, sV.n)], inc=(sV, 1))
            p.op("vector", CP(b.xn_tm[bb][:], b.xsq[:]), waits=[(sV, sV.n)],
                 inc=(sV, 1))
            p.op("vector", TT(b.lo_tm[:], b.xsq[:], b.xn_tm[bb][:],
                              ALU.subtract), waits=[(sV, sV.n)], inc=(sV, 1))
        xn_done[i] = sV.n
        par = (i // 4) % 2
        nsub = 2 if router else 1
        for c in range(NCH):
            for sub in range(nsub):
                key = (i * NCH + c) * nsub + sub - 2
                tw = [(sV, xn_done[i]), (sG, sm.consts_done)]
                if key >= 0:
                    tw.append((sV, evac_done[key]))
                if router and sub == 1 and c == 0 and i % 4 == 0:
                    tw.append((sT, rmm_done[i // 4 - 2]))
                srcb = b.xn_tm[bb] if sub == 0 else b.lo_tm
                p.op("tensor", TRN(b.ps_tr[(c * nsub + sub) % 2][:, 0:128],
                                   srcb[:, c * 128:(c + 1) * 128], b.ident[:]),
                     waits=tw, inc=(sT, 1))
                trd = sT.n
                if sub == 0:
                    dst = b.xnT[:, c * S + i * 128: c * S + (i + 1) * 128]
                else:
                    dst = loroll[par][:, c * 512 + (i % 4) * 128:
                                      c * 512 + (i % 4) * 128 + 128]
                p.op("vector", CP(dst, b.ps_tr[(c * nsub + sub) % 2][:, 0:128]),
                     waits=[(sT, trd)], inc=(sV, 1))
                evac_done[(i * NCH + c) * nsub + sub] = sV.n
        if router and i % 4 == 3:
            j = i // 4
            lo_ready = sV.n
            psl = b.ps_q[0:NE, (j % 2) * 512:(j % 2) * 512 + 512]
            nmm = 3 * NCH
            for ci in range(nmm):
                c = ci % NCH
                if ci < NCH:
                    wsel, xsel = b.rw, b.xnT[:, c * S + j * 512: c * S + j * 512 + 512]
                elif ci < 2 * NCH:
                    wsel, xsel = b.rwd, b.xnT[:, c * S + j * 512: c * S + j * 512 + 512]
                else:
                    wsel, xsel = b.rw, loroll[par][:, c * 512:(c + 1) * 512]
                waits = []
                if ci == 0:
                    waits = [(sV, lo_ready), (sV, sm.rw_ready)]
                    if j >= 2:
                        waits.append((sS, sig_done[j - 2]))
                p.op("tensor", MM(psl, wsel[:, c * NE:(c + 1) * NE], xsel,
                                  ci == 0, ci == nmm - 1),
                     waits=waits, inc=((sT, 1) if ci == nmm - 1 else None))
            rmm_done[j] = sT.n
            sw = [(sT, rmm_done[j]), (sXD, sm.consts_dma_done)]
            if j >= 2:
                sw.append((sT, tr_evac[("tr", j - 2)]))
            p.op("scalar", ACT(b.pbuf[j % 2], psl, AF.Sigmoid, bias=b.rb[:],
                               scale=1.0), waits=sw, inc=(sS, 1))
            sig_done[j] = sS.n
            for tt in range(4):
                ii = j * 4 + tt
                pso = b.ps_q[0:128, 1024 + (ii % 2) * 512: 1024 + (ii % 2) * 512 + NE]
                waits = [(sS, sig_done[j])]
                if ii >= 2:
                    waits.append((sV, tr_evac[ii - 2]))
                p.op("tensor", TRN(pso, b.pbuf[j % 2][0:NE, tt * 128:(tt + 1) * 128],
                                   b.ident8[:]), waits=waits, inc=(sT, 1))
                p.op("vector", CP(b.probs_tm[:, ii * NE:(ii + 1) * NE], pso),
                     waits=[(sT, sT.n)], inc=(sV, 1))
                tr_evac[ii] = sV.n
            tr_evac[("tr", j)] = sT.n
    sm.ln_last_xn = xn_done[nt - 1]
    sm.ln_last_sq = sq_done[nt - 1]
    sm.ln_last_evac = sV.n
    sm.router_tr_evac = tr_evac
    if xn_hbm is not None:
        p.op("sync", DMA(xn_hbm.ap().rearrange("(c r) t -> r c t", r=128),
                         b.xnT[:, 0:NCH * S].rearrange("r (c t) -> r c t", t=S)),
             waits=[(sV, sm.ln_last_evac)], inc=(sXD, 16))
        sm.xn_hbm_done = sXD.n


def emit_attention(p, b, sm, wq_h, wk_h, wv_h, stage="full"):
    sLD, sT, sV, sS = sm.sLD, sm.sT, sm.sV, sm.sS
    S, nqb, nkb = b.S, b.nqb, b.nkb
    nquad = nkb // 4
    wdma_done = {}
    pair_pe_done = {}
    pair_evac_done = {}
    sm.bc_copy_done = sm.get("bc_copy_done", 0)

    def plan_wdma(pp):
        waits = []
        if pp >= 1:
            waits = [(sT, pair_pe_done[pp - 1])]
        else:
            waits = list(sm.get("first_wdma_waits", []))
        for wsb, hbm in ((b.wq, wq_h), (b.wk, wk_h), (b.wv, wv_h)):
            src = hbm.ap()[:, pp * 128:(pp + 1) * 128].rearrange(
                "(c r) o -> r c o", r=128)
            dst = wsb.rearrange("r (c o) -> r c o", o=128)
            p.op("gpsimd", DMA(dst, src), waits=waits, inc=(sLD, 16))
        wdma_done[pp] = sLD.n

    for pp in range(NP):
        plan_wdma(pp)
        p.op("vector", MEMSET(b.vT[:], 0.0),
             waits=[(sV, sV.n), (sT, sT.n), (sS, sm.bc_copy_done)], inc=(sV, 1))
        vT3 = b.vT.rearrange("r (kb c) -> r kb c", c=192)
        p.op("vector", MEMSET(vT3[:, :, 64:65], 1.0), waits=[(sV, sV.n)],
             inc=(sV, 1))
        p.op("vector", MEMSET(vT3[:, :, 96:97], 1.0), waits=[(sV, sV.n)],
             inc=(sV, 1))
        vT_init = sV.n
        evac_done = {}
        vT_evac = {}
        n_mm = 0
        trp = 0
        for xi, (wsb, brow, scl) in enumerate(((b.wq, b.bq_s, SCALE),
                                               (b.wk, b.bk_s, 1.0),
                                               (b.wv, b.bv_s, 1.0))):
            for j in range(S // 512):
                ps = b.ps_a if (n_mm % 2 == 0) else b.ps_b
                mmd = None
                for c in range(NCH):
                    waits = []
                    if c == 0:
                        waits = [(sLD, wdma_done[pp]), (sV, sm.get("ln_last_evac", 0))]
                        if n_mm >= 2:
                            waits.append((sV, evac_done[n_mm - 2]))
                        if n_mm == 0 and pp >= 1:
                            waits.append((sV, pair_evac_done[pp - 1]))
                    p.op("tensor", MM(ps[:], wsb[:, c * 128:(c + 1) * 128],
                                      b.xnT[:, c * S + j * 512: c * S + j * 512 + 512],
                                      c == 0, c == NCH - 1),
                         waits=waits, inc=((sT, 1) if c == NCH - 1 else None))
                mmd = sT.n
                if xi < 2:
                    dst = (b.qp, b.kp)[xi]
                    sl = dst[:, j * 512:(j + 1) * 512]
                else:
                    sl = b.vtmp[j % 2][:]
                if scl == 1.0:
                    fn = TS(sl, ps[:], brow[:, pp:pp + 1], None, ALU.add)
                else:
                    fn = TS(sl, ps[:], brow[:, pp:pp + 1], scl, ALU.add, ALU.mult)
                ew = [(sT, mmd)]
                if xi == 2:
                    ew.append((sV, vT_init))
                    if j >= 2:
                        ew.append((sT, vT_evac[("tr", j - 2)]))
                p.op("vector", fn, waits=ew, inc=(sV, 1))
                evac_done[n_mm] = sV.n
                n_mm += 1
                if xi == 2:
                    for tt in range(4):
                        kb = j * 4 + tt
                        pstr = b.ps_tr[trp % 2]
                        tw = [(sV, evac_done[n_mm - 1])]
                        if trp >= 2:
                            tw.append((sV, vT_evac[trp - 2]))
                        if trp < 2:
                            tw.append((sS, sm.bc_copy_done))
                        p.op("tensor", TRN(pstr[:, 0:128],
                                           b.vtmp[j % 2][:, tt * 128:(tt + 1) * 128],
                                           b.ident[:]), waits=tw, inc=(sT, 1))
                        trd = sT.n
                        p.op("vector", CP(b.vT[:, kb * 192:kb * 192 + 64],
                                          pstr[:, 0:64]), waits=[(sT, trd)])
                        p.op("vector", CP(b.vT[:, kb * 192 + 128:kb * 192 + 192],
                                          pstr[:, 64:128]), inc=(sV, 1))
                        vT_evac[trp] = sV.n
                        trp += 1
                    vT_evac[("tr", j)] = sT.n
        qkv_evac_last = sV.n
        vT_last = sV.n
        if stage == "qkv":
            return

        scr = b.ctxT[:, pp * S:(pp + 1) * S]
        nsq = {}
        jj = 0
        for t_i, src in enumerate((b.qp, b.kp)):
            p.op("vector", TT(scr, src, src, ALU.mult),
                 waits=[(sV, qkv_evac_last), (sT, sT.n), (sV, sV.n)],
                 inc=(sV, 1))
            sq_d = sV.n
            for half, onescol in ((0, b.ones_colA), (1, b.ones_colB)):
                for j in range(S // 512):
                    ps = b.ps_c if jj % 2 == 0 else b.ps_d
                    waits = [(sV, sq_d), (sS, sm.bc_copy_done)]
                    if jj >= 2:
                        waits.append((sV, nsq[jj - 2]))
                    p.op("tensor", MM(ps[0:1, 0:512], onescol[:],
                                      scr[:, j * 512:(j + 1) * 512], True, True),
                         waits=waits, inc=(sT, 1))
                    mmd = sT.n
                    if j == 0:
                        p.op("vector", CP(b.nrm[:], ps[0:1, 0:512]),
                             waits=[(sT, mmd), (sV, sV.n)], inc=(sV, 1))
                    else:
                        p.op("vector", TT(b.nrm[:], b.nrm[:], ps[0:1, 0:512],
                                          ALU.max), waits=[(sT, mmd), (sV, sV.n)],
                             inc=(sV, 1))
                    nsq[jj] = sV.n
                    jj += 1
                p.op("vector", RED(b.u_t[:, (t_i * 2 + half):(t_i * 2 + half) + 1],
                                   b.nrm[:], ALU.max), waits=[(sV, sV.n)],
                     inc=(sV, 1))
        if stage in ("c4a", "c4b", "c4z", "c4y"):
            return
        c4_pe_done = sT.n
        for half in range(2):
            p.op("vector", TS(b.u_t[:, 4 + half:5 + half],
                              b.u_t[:, 2 + half:3 + half], 1.0 / 16.0, None,
                              ALU.mult), waits=[(sV, sV.n)], inc=(sV, 1))
            p.op("vector", STT(b.u_t[:, 6 + half:7 + half],
                               b.u_t[:, half:half + 1], 4.0,
                               b.u_t[:, 4 + half:5 + half], ALU.mult, ALU.add),
                 waits=[(sV, sV.n)], inc=(sV, 1))
            p.op("vector", TS(b.u_t[:, 8 + half:9 + half],
                              b.u_t[:, 6 + half:7 + half], -0.5, None, ALU.mult),
                 waits=[(sV, sV.n)], inc=(sV, 1))
            negU_done = sV.n
            if stage == "c4c":
                continue
            ps = b.ps_c if half == 0 else b.ps_d
            p.op("tensor", MM(ps[0:128, 0:1], b.ones_t[0:1, :],
                              b.u_t[:, 8 + half:9 + half], True, True),
                 waits=[(sV, negU_done), (sV, sV.n)], inc=(sT, 1))
            p.op("scalar", SCOPY(b.biasU[half][:], ps[0:128, 0:1]),
                 waits=[(sT, sT.n)], inc=(sS, 1))
        if stage == "c4c":
            return
        biasU_done = sS.n
        if stage == "c4":
            return

        if stage == "c3":
            return

        exp_done = {}
        avs_done = {}
        mult_done = {}
        hq = 0
        for h in range(2):
            hofs = h * 64
            srow = 64 if h == 0 else 32
            crow = 0 if h == 0 else 64
            for qb in range(nqb):
                po = b.ps_a if hq % 2 == 0 else b.ps_b
                for quad in range(nquad):
                    for k4 in range(4):
                        kb = quad * 4 + k4
                        waits = []
                        if kb == 0:
                            waits += [(sS, biasU_done), (sV, vT_last)]
                            if hq >= 2:
                                waits.append((sV, mult_done[hq - 2]))
                            elif pp >= 1:
                                waits.append((sV, pair_evac_done[pp - 1]))
                        if k4 == 0 and quad > 0:
                            waits.append((sS, exp_done[(hq, quad - 1)]))
                        elif k4 == 0 and hq > 0:
                            waits.append((sS, exp_done[(hq - 1, nquad - 1)]))
                        p.op("tensor", MM(
                            b.ps_q[:, (kb % 4) * 512:(kb % 4 + 1) * 512],
                            b.kp[hofs:hofs + 64, kb * 128:(kb + 1) * 128],
                            b.qp[hofs:hofs + 64, qb * 512:(qb + 1) * 512],
                            True, True), waits=waits, inc=(sT, 1))
                    s_done = sT.n
                    w2 = [(sT, s_done)]
                    if quad > 0:
                        w2.append((sT, avs_done[(hq, quad - 1)]))
                    elif hq > 0:
                        w2.append((sT, avs_done[(hq - 1, nquad - 1)]))
                    p.op("scalar", ACT(b.pt[:, 0:2048], b.ps_q[:, 0:2048],
                                       AF.Exp, bias=b.biasU[h][:], scale=1.0),
                         waits=w2, inc=(sS, 1))
                    exp_done[(hq, quad)] = sS.n
                    for k4 in range(4):
                        kb = quad * 4 + k4
                        if h == 0:
                            lhs = b.vT[:, kb * 192: kb * 192 + 65]
                            out = po[0:65, 0:512]
                        else:
                            lhs = b.vT[:, kb * 192 + 64: kb * 192 + 192]
                            out = po[0:128, 0:512]
                        p.op("tensor", MM(out, lhs,
                                          b.pt[:, (kb % 4) * 512:(kb % 4 + 1) * 512],
                                          kb == 0, kb == nkb - 1),
                             waits=[(sS, exp_done[(hq, quad)])],
                             inc=((sT, 1) if k4 == 3 else None))
                    avs_done[(hq, quad)] = sT.n
                rsl = b.recip_t[srow:srow + 1, 0:512]
                p.op("vector", RECIP(rsl, po[srow:srow + 1, 0:512]),
                     waits=[(sT, avs_done[(hq, nquad - 1)])], inc=(sV, 1))
                rec_done = sV.n
                p.op("tensor", MM(b.ps_c[crow:crow + 64, 0:512],
                                  b.ones_t[srow:srow + 1, 0:64], rsl, True, True),
                     waits=[(sV, rec_done), (sS, sm.bc_copy_done)], inc=(sT, 1))
                bc_done = sT.n
                p.op("scalar", SCOPY(b.bc_sb[crow:crow + 64, :],
                                     b.ps_c[crow:crow + 64, 0:512]),
                     waits=[(sT, bc_done)], inc=(sS, 1))
                sm.bc_copy_done = sS.n
                dst = b.ctxT[hofs:hofs + 64,
                             pp * S + qb * 512: pp * S + qb * 512 + 512]
                p.op("vector", TT(dst, po[crow:crow + 64, 0:512],
                                  b.bc_sb[crow:crow + 64, :], ALU.mult),
                     waits=[(sS, sm.bc_copy_done)], inc=(sV, 1))
                mult_done[hq] = sV.n
                hq += 1
        pair_pe_done[pp] = sT.n
        pair_evac_done[pp] = sV.n
        if stage == "c5":
            return
    sm.attn_pe_done = sT.n
    sm.attn_v_done = sV.n
    sm.attn_s_done = sS.n


def emit_out_proj(p, b, sm, x_hbm, dst_hbm, sign_src=None, sign_ready=0):
    sXD, sT, sV, sS = sm.sXD, sm.sT, sm.sV, sm.sS
    S, nt = b.S, b.nt
    out_done = {}
    ev_done = {}
    for i in range(nt):
        bb = i % 2
        sx = sm.sXp[bb]
        so = sm.sOp[bb]
        waits = [(sV, sm.bo_bc_done), (sV, sm.ln_last_xn)]
        if i >= 2:
            waits.append((so, out_done[i - 2]))
        p.op("sync", DMA(b.x_t[bb][:], x_hbm.ap()[i * 128:(i + 1) * 128, :]),
             waits=waits, inc=(sx, 16))
        xd = sx.n
        for half in range(2):
            ps = (b.ps_a, b.ps_b, b.ps_c, b.ps_d)[2 * (i % 2) + half]
            nmm = NCH * (2 if sign_src is not None else 1)
            k = 0
            for c in range(NCH):
                fw = []
                if k == 0:
                    fw = [(sT, sm.attn_pe_done), (sV, sm.attn_v_done),
                          (sS, sm.bc_copy_done), (sm.sWD, sm.wo_dma)]
                    if i >= 2:
                        fw.append((sV, ev_done[i - 2]))
                p.op("tensor", MM(ps[:],
                                  b.ctxT[:, c * S + i * 128: c * S + (i + 1) * 128],
                                  b.wo[:, c * D + half * 512: c * D + half * 512 + 512],
                                  k == 0, k == nmm - 1),
                     waits=fw, inc=((sT, 1) if k == nmm - 1 else None))
                k += 1
            if sign_src is not None:
                for c in range(NCH):
                    fw = [(sS, sign_ready)] if c == 0 else []
                    p.op("tensor", MM(ps[:],
                                      b.xnT[:, c * S + i * 128: c * S + (i + 1) * 128],
                                      sign_src[:, c * D + half * 512: c * D + half * 512 + 512],
                                      False, k == nmm - 1),
                         waits=fw, inc=((sT, 1) if k == nmm - 1 else None))
                    k += 1
            mmd = sT.n
            sl = b.x_t[bb][:, half * 512:(half + 1) * 512]
            p.op("vector", STT(sl, ps[:], 1.0, sl, ALU.mult, ALU.add),
                 waits=[(sT, mmd), (sx, xd)], inc=(sV, 1))
            p.op("vector", TT(sl, sl, b.bo_bc[:, half * 512:(half + 1) * 512],
                              ALU.add), waits=[(sV, sV.n)], inc=(sV, 1))
        ev_done[i] = sV.n
        p.op("sync", DMA(dst_hbm.ap()[i * 128:(i + 1) * 128, :], b.x_t[bb][:]),
             waits=[(sV, ev_done[i])], inc=(so, 16))
        out_done[i] = so.n
    sm.outproj_oa = sm.sOA.n
    sm.outproj_ob = sm.sOB.n
    sm.outproj_v_done = sV.n
    sm.outproj_pe_done = sT.n


def build_l0(S, stage="full"):
    nc = bass.Bass("TRN2", target_bir_lowering=False, detect_race_conditions=False)
    h = Bag()
    h.x = nc.dram_tensor("x", [S, D], F32, kind="ExternalInput")
    for nm, shp in (("wqT", [D, D]), ("wkT", [D, D]), ("wvT", [D, D]),
                    ("woT", [D, D]), ("wfT", [D, D]),
                    ("bq", [128, NP]), ("bk", [128, NP]), ("bv", [128, NP]),
                    ("bo", [2, 512])):
        h[nm] = nc.dram_tensor(nm, shp, F32, kind="ExternalInput")
    h.x1 = nc.dram_tensor("x1", [S, D], F32, kind="ExternalOutput")

    with ExitStack() as ctx:
        b = alloc_bufs(nc, ctx, S, l1=False)
        E = ctx.enter_context
        b.wf = b.arE[:, 0:8192]
        sm = Bag()
        mksems(nc, ctx, sm)
        block = E(nc.Block())
        p = Plan()
        emit_consts(p, b, sm)
        for dst, src in ((b.bq_s[:], h.bq.ap()), (b.bk_s[:], h.bk.ap()),
                         (b.bv_s[:], h.bv.ap()),
                         (b.rowscr[0:1, 0:512], h.bo.ap()[0:1, :]),
                         (b.rowscr[32:33, 0:512], h.bo.ap()[1:2, :])):
            p.op("sync", DMA(dst, src), inc=(sm.sXD, 16))
        sm.consts_dma_done = sm.sXD.n
        emit_bo_bc(p, b, sm)
        emit_ln(p, b, sm, h.x)
        p.op("gpsimd", DMA(b.wo.rearrange("r (c o) -> r c o", o=D),
                           h.woT.ap().rearrange("(c r) o -> r c o", r=128)),
             waits=[(sm.sV, sm.ln_last_evac), (sm.sT, sm.sT.n)],
             inc=(sm.sWD, 16))
        sm.wo_dma = sm.sWD.n
        if stage == "lntm":
            fl16 = h.x1.ap().rearrange("s d -> (s d)").bitcast(BF16)
            w_all = [(sm.sV, sm.sV.n), (sm.sS, sm.sS.n), (sm.sT, sm.sT.n)]
            p.op("gpsimd", DMA(fl16[0:128 * D].rearrange("(r t) -> r t", r=128),
                               b.xn_tm[0][:]), waits=w_all, inc=(sm.sLD, 16))
            p.op("gpsimd", DMA(fl16[128 * D:2 * 128 * D].rearrange(
                "(r t) -> r t", r=128), b.xn_tm[1][:]), waits=w_all,
                inc=(sm.sLD, 16))
            o32 = 128 * D + 64
            p.op("gpsimd", DMA(h.x1.ap().rearrange("s d -> (s d)")
                               [o32:o32 + 128 * 8].rearrange("(r t) -> r t", r=128),
                               b.st[0][:]), waits=w_all, inc=(sm.sLD, 16))
            p.op("gpsimd", DMA(h.x1.ap().rearrange("s d -> (s d)")
                               [o32 + 1024:o32 + 1024 + 128 * 8].rearrange(
                               "(r t) -> r t", r=128), b.st[1][:]),
                 waits=w_all, inc=(sm.sLD, 16))
            p.op("sync", lambda e: e.nop(), waits=[(sm.sLD, sm.sLD.n)])
            p.realize(block)
            return nc
        if stage == "ln":
            fl16 = h.x1.ap().rearrange("s d -> (s d)").bitcast(BF16)
            p.op("gpsimd", DMA(fl16[0:128 * NCH * S].rearrange(
                "(r c t) -> r c t", r=128, c=NCH),
                b.xnT[:, 0:NCH * S].rearrange("r (c t) -> r c t", t=S)),
                waits=[(sm.sV, sm.ln_last_evac)], inc=(sm.sLD, 16))
            p.op("sync", lambda e: e.nop(), waits=[(sm.sLD, sm.sLD.n)])
            p.realize(block)
            return nc
        emit_attention(p, b, sm, h.wqT, h.wkT, h.wvT, stage=stage)
        if stage == "full":
            p.op("gpsimd", DMA(b.wf.rearrange("r (c o) -> r c o", o=D),
                               h.wfT.ap().rearrange("(c r) o -> r c o", r=128)),
                 waits=[(sm.sT, sm.attn_pe_done), (sm.sS, sm.attn_s_done),
                        (sm.sV, sm.attn_v_done)], inc=(sm.sWD, 16))
            p.op("scalar", ACT(b.wf, b.wf, AF.Sign),
                 waits=[(sm.sWD, sm.sWD.n)], inc=(sm.sS, 1))
            sign_ready = sm.sS.n
        if stage != "full":
            fl16 = h.x1.ap().rearrange("s d -> (s d)").bitcast(BF16)
            fl32 = h.x1.ap().rearrange("s d -> (s d)")
            w_all = [(sm.sV, sm.sV.n), (sm.sT, sm.sT.n), (sm.sS, sm.sS.n)]
            if stage in ("attn", "c5"):
                p.op("gpsimd", DMA(fl16[0:128 * NCH * S].rearrange(
                    "(r c t) -> r c t", r=128, c=NCH),
                    b.ctxT[:].rearrange("r (c t) -> r c t", t=S)),
                    waits=w_all, inc=(sm.sLD, 16))
            else:
                p.op("gpsimd", DMA(fl16[0:128 * S].rearrange("(r t) -> r t", r=128),
                                   b.qp), waits=w_all, inc=(sm.sLD, 16))
                p.op("gpsimd", DMA(fl16[128 * S:2 * 128 * S].rearrange(
                    "(r t) -> r t", r=128), b.kp), waits=w_all, inc=(sm.sLD, 16))

                nv = 128 * (S // 128) * 192
                p.op("gpsimd", DMA(fl16[3 * 128 * S:3 * 128 * S + nv].rearrange(
                    "(r t) -> r t", r=128), b.vT), waits=w_all, inc=(sm.sLD, 16))
                o32 = (3 * 128 * S + nv) // 2 + 64
                p.op("gpsimd", DMA(fl32[o32:o32 + 16].rearrange("(a t) -> a t", a=1),
                                   b.u_t[:]), waits=w_all, inc=(sm.sLD, 16))
                p.op("gpsimd", DMA(fl32[o32 + 16:o32 + 144].rearrange(
                    "(t a) -> t a", a=1), b.biasU[0][:]), waits=w_all,
                    inc=(sm.sLD, 16))
                p.op("gpsimd", DMA(fl32[o32 + 144:o32 + 272].rearrange(
                    "(t a) -> t a", a=1), b.biasU[1][:]), waits=w_all,
                    inc=(sm.sLD, 16))
            p.op("sync", lambda e: e.nop(), waits=[(sm.sLD, sm.sLD.n)])
            p.realize(block)
            return nc
        emit_out_proj(p, b, sm, h.x, h.x1, sign_src=b.wf, sign_ready=sign_ready)
        p.op("sync", lambda e: e.nop(),
             waits=[(sm.sOA, sm.sOA.n), (sm.sOB, sm.sOB.n)])
        p.realize(block)
    return nc


def build_l1(S):
    nc = bass.Bass("TRN2", target_bir_lowering=False, detect_race_conditions=False)
    h = Bag()
    h.x = nc.dram_tensor("x", [S, D], F32, kind="ExternalInput")
    for nm, shp in (("wqT", [D, D]), ("wkT", [D, D]), ("wvT", [D, D]),
                    ("woT", [D, D]), ("bq", [128, NP]), ("bk", [128, NP]),
                    ("bv", [128, NP]), ("bo", [2, 512]),
                    ("sgT", [D, IFF]), ("suT", [D, IFF]), ("sdT", [IFF, D]),
                    ("gw", [NE, D, IFF]), ("uw", [NE, D, IFF]),
                    ("dw", [NE, IFF, D]), ("rwT", [D, NE]), ("rb", [NE, 1])):
        h[nm] = nc.dram_tensor(nm, shp, F32, kind="ExternalInput")
    h.out = nc.dram_tensor("out", [S, D], F32, kind="ExternalOutput")
    xn_hbm = nc.dram_tensor("xn_hbm", [D, S], BF16, kind="Internal")
    part_hbm = nc.dram_tensor("part_hbm", [S, D], F32, kind="Internal")

    with ExitStack() as ctx:
        b = alloc_bufs(nc, ctx, S, l1=True)
        E = ctx.enter_context
        sm = Bag()
        mksems(nc, ctx, sm)
        sLD, sXD, sT, sV, sS = None, None, None, None, None
        block = E(nc.Block())
        p = Plan()
        sLD, sXD, sT, sV, sS = sm.sLD, sm.sXD, sm.sT, sm.sV, sm.sS
        nt, ntb = b.nt, b.S // 256

        emit_consts(p, b, sm)
        for dst, src in ((b.bq_s[:], h.bq.ap()), (b.bk_s[:], h.bk.ap()),
                         (b.bv_s[:], h.bv.ap()),
                         (b.rowscr[0:1, 0:512], h.bo.ap()[0:1, :]),
                         (b.rowscr[32:33, 0:512], h.bo.ap()[1:2, :]),
                         (b.rb[:], h.rb.ap())):
            p.op("sync", DMA(dst, src), inc=(sXD, 16))
        p.op("sync", DMA(b.rwf.rearrange("r (c o) -> r c o", o=NE),
                         h.rwT.ap().rearrange("(c r) o -> r c o", r=128)),
             inc=(sXD, 16))
        sm.consts_dma_done = sXD.n
        rwf_dma = sXD.n
        p.op("vector", CP(b.rw, b.rwf), waits=[(sXD, rwf_dma)], inc=(sV, 1))
        p.op("vector", TT(b.rwd, b.rwf, b.rw, ALU.subtract),
             waits=[(sV, sV.n)], inc=(sV, 1))
        sm.rw_ready = sV.n
        emit_bo_bc(p, b, sm)
        emit_ln(p, b, sm, h.x, xn_hbm=xn_hbm, router=True)
        p.op("gpsimd", DMA(b.wo.rearrange("r (c o) -> r c o", o=D),
                           h.woT.ap().rearrange("(c r) o -> r c o", r=128)),
             waits=[(sm.sV, sm.ln_last_evac), (sm.sT, sm.sT.n)],
             inc=(sm.sWD, 16))
        sm.wo_dma = sm.sWD.n

        for i in range(nt):
            pr = b.probs_tm[:, i * NE:(i + 1) * NE]
            t2 = b.t2
            for fn in (TT(t2[:, 32:40], pr, b.colofs[:], ALU.subtract),
                       RED(t2[:, 0:1], t2[:, 32:40], ALU.max),
                       TS(t2[:, 8:16], t2[:, 32:40], t2[:, 0:1], None,
                          ALU.is_equal),
                       STT(t2[:, 16:24], t2[:, 8:16], -4.0, t2[:, 32:40],
                           ALU.mult, ALU.add),
                       RED(t2[:, 1:2], t2[:, 16:24], ALU.max),
                       TS(t2[:, 24:32], t2[:, 16:24], t2[:, 1:2], None,
                          ALU.is_equal),
                       TT(t2[:, 2:3], t2[:, 0:1], t2[:, 1:2], ALU.add),
                       RECIP(t2[:, 3:4], t2[:, 2:3]),
                       TS(t2[:, 8:16], t2[:, 8:16], t2[:, 0:1], t2[:, 3:4],
                          ALU.mult, ALU.mult),
                       TS(t2[:, 24:32], t2[:, 24:32], t2[:, 1:2], t2[:, 3:4],
                          ALU.mult, ALU.mult),
                       TT(b.w_rt[:, i * NE:(i + 1) * NE], t2[:, 8:16],
                          t2[:, 24:32], ALU.add)):
                p.op("vector", fn, waits=[(sV, sV.n)], inc=(sV, 1))
        sm.w_done = sV.n

        sm.first_wdma_waits = [(sV, sm.w_done), (sS, sS.n)]
        emit_attention(p, b, sm, h.wqT, h.wkT, h.wvT)
        emit_out_proj(p, b, sm, h.x, part_hbm, sign_src=None)

        comb = b.ctxT
        gup_dma = {}
        slab_dma = {}
        gu_pe = {}
        comb_ev = {}
        comb_tile_done = {}

        for ei in range(NE + 1):
            if ei == 0:
                srcs = (h.sgT.ap(), h.suT.ap(), h.sdT.ap())
                waits = [(sT, sT.n), (sV, sV.n), (sS, sS.n),
                         (sXD, sm.xn_hbm_done)]
            else:
                srcs = (h.gw.ap()[ei - 1], h.uw.ap()[ei - 1], h.dw.ap()[ei - 1])
                waits = [(sT, gu_pe[(ei - 1, ntb - 1)])]
            for dst, src, oo in ((b.gate_v, srcs[0], IFF), (b.up_v, srcs[1], IFF),
                                 (b.down_v, srcs[2], D)):
                p.op("gpsimd", DMA(dst.rearrange("r (c o) -> r c o", o=oo),
                                   src.rearrange("(c r) o -> r c o", r=128)),
                     waits=waits, inc=(sLD, 16))
            gup_dma[ei] = sLD.n

            for tb in range(ntb):
                bb = tb % 2
                ssl = sm.sSp[bb]
                waits = [(sXD, sm.xn_hbm_done)]
                if ei == 0 and tb < 2:
                    waits.append((sT, sm.outproj_pe_done))
                pk = (ei, tb - 2) if tb >= 2 else (ei - 1, ntb - 2 + tb)
                if pk in gu_pe:
                    waits.append((sT, gu_pe[pk]))
                p.op("sync", DMA(b.slab[bb].rearrange("r (c t) -> r c t", t=256),
                                 xn_hbm.ap().rearrange("(c r) t -> r c t", r=128)
                                 [:, :, tb * 256:(tb + 1) * 256]),
                     waits=waits, inc=(ssl, 16))
                slab_dma[(ei, tb)] = (ssl, ssl.n)
                ev = {}
                for ic in range(2 * (IFF // 128)):
                    isup = ic % 2
                    icc = ic // 2
                    ps = b.ps_b if isup else b.ps_a
                    wv_ = b.up_v if isup else b.gate_v
                    for c in range(NCH):
                        waits = []
                        if c == 0:
                            waits = [(sLD, gup_dma[ei]), slab_dma[(ei, tb)]]
                            if ic >= 2:
                                waits.append((sS, ev[ic - 2]))
                            elif (ei, tb) == (0, 0):
                                waits.append((sV, sm.outproj_v_done))
                                waits.append((sS, sS.n))
                            else:
                                waits.append((sS, guev_last))
                                pk0 = (ei, tb - 1) if tb >= 1 else (ei - 1, ntb - 1)
                                waits.append((sV, comb_ev[pk0]))
                        p.op("tensor", MM(ps[:, 0:256],
                                          wv_[:, c * IFF + icc * 128: c * IFF + icc * 128 + 128],
                                          b.slab[bb][:, c * 256:(c + 1) * 256],
                                          c == 0, c == NCH - 1),
                             waits=waits, inc=((sT, 1) if c == NCH - 1 else None))
                    mmd = sT.n
                    dst = (b.u_sb if isup else b.g_sb)[:, icc * 256:(icc + 1) * 256]
                    p.op("scalar", ACT(dst, ps[:, 0:256],
                                       AF.Copy if isup else AF.Silu),
                         waits=[(sT, mmd)], inc=(sS, 1))
                    ev[ic] = sS.n
                guev_last = sS.n
                p.op("vector", TT(b.g_sb[:, 0:4096], b.g_sb[:, 0:4096],
                                  b.u_sb[:, 0:4096], ALU.mult),
                     waits=[(sS, ev[2 * (IFF // 128) - 1])], inc=(sV, 1))
                gu_done = sV.n
                dmm = {}
                for tt in range(2):
                    for oh in range(2):
                        psl = b.ps_q[:, (tt * 2 + oh) * 512:(tt * 2 + oh + 1) * 512]
                        for ic in range(IFF // 128):
                            waits = []
                            if ic == 0:
                                waits = [(sV, gu_done)]
                                if (ei, tb) == (0, 0):
                                    waits.append((sS, sm.attn_s_done))
                                pk2 = (ei, tb - 1) if tb >= 1 else (ei - 1, ntb - 1)
                                if pk2 in comb_ev:
                                    waits.append((sV, comb_ev[pk2]))
                            p.op("tensor", MM(psl,
                                              b.g_sb[:, ic * 256 + tt * 128: ic * 256 + tt * 128 + 128],
                                              b.down_v[:, ic * D + oh * 512: ic * D + oh * 512 + 512],
                                              ic == 0, ic == IFF // 128 - 1),
                                 waits=waits,
                                 inc=((sT, 1) if ic == IFF // 128 - 1 else None))
                        dmm[(tt, oh)] = sT.n
                gu_pe[(ei, tb)] = sT.n
                for tt in range(2):
                    ti = tb * 2 + tt
                    for oh in range(2):
                        psl = b.ps_q[:, (tt * 2 + oh) * 512:(tt * 2 + oh + 1) * 512]
                        csl = comb[:, ti * D + oh * 512: ti * D + oh * 512 + 512]
                        waits = [(sT, dmm[(tt, oh)])]
                        if ei == 0:
                            waits.append((sV, sm.outproj_v_done))
                            fn = CP(csl, psl)
                        else:
                            if ei == 1:
                                waits.append((sV, sm.w_done))
                            fn = STT(csl, psl, b.w_rt[:, ti * NE + ei - 1: ti * NE + ei],
                                     csl, ALU.mult, ALU.add)
                        p.op("vector", fn, waits=waits, inc=(sV, 1))
                comb_ev[(ei, tb)] = sV.n
                if ei == NE:
                    comb_tile_done[tb] = sV.n

        fin = {}
        for i in range(nt):
            bb = i % 2
            sx = sm.sXp[bb]
            so = sm.sOp[bb]
            waits = [(sm.sOA, sm.outproj_oa), (sm.sOB, sm.outproj_ob),
                     (sV, sm.outproj_v_done)]
            if i >= 2:
                waits.append((so, fin[i - 2]))
            p.op("sync", DMA(b.x_t[bb][:], part_hbm.ap()[i * 128:(i + 1) * 128, :]),
                 waits=waits, inc=(sx, 16))
            pd = sx.n
            p.op("vector", TT(b.x_t[bb][:], b.x_t[bb][:],
                              comb[:, i * D:(i + 1) * D], ALU.add),
                 waits=[(sx, pd), (sV, comb_tile_done[i // 2])], inc=(sV, 1))
            p.op("sync", DMA(h.out.ap()[i * 128:(i + 1) * 128, :], b.x_t[bb][:]),
                 waits=[(sV, sV.n)], inc=(so, 16))
            fin[i] = so.n
        p.op("sync", lambda e: e.nop(),
             waits=[(sm.sOA, sm.sOA.n), (sm.sOB, sm.sOB.n)])
        p.realize(block)
    return nc


def _l0_inputs(W_qkv, b_qkv, W_out, b_out, Wf):
    idx = (np.arange(NH)[:, None] * 192 + np.arange(DH)[None, :]).ravel()
    d = dict(
        wqT=np.ascontiguousarray(W_qkv[idx].T),
        wkT=np.ascontiguousarray(W_qkv[idx + 64].T),
        wvT=np.ascontiguousarray(W_qkv[idx + 128].T),
        woT=np.ascontiguousarray(W_out.T),
        wfT=np.ascontiguousarray(Wf.T),
        bq=np.ascontiguousarray(b_qkv[idx].reshape(NP, 128).T),
        bk=np.ascontiguousarray(b_qkv[idx + 64].reshape(NP, 128).T),
        bv=np.ascontiguousarray(b_qkv[idx + 128].reshape(NP, 128).T),
        bo=np.ascontiguousarray(b_out.reshape(2, 512)),
    )
    return d


def _l1_inputs(Wq, bq, Wk, bk, Wv, bv, Wo, bo, sg, su, sd, gw, uw, dw, rw, rb):
    return dict(
        wqT=np.ascontiguousarray(Wq.T), wkT=np.ascontiguousarray(Wk.T),
        wvT=np.ascontiguousarray(Wv.T), woT=np.ascontiguousarray(Wo.T),
        bq=np.ascontiguousarray(bq.reshape(NP, 128).T),
        bk=np.ascontiguousarray(bk.reshape(NP, 128).T),
        bv=np.ascontiguousarray(bv.reshape(NP, 128).T),
        bo=np.ascontiguousarray(bo.reshape(2, 512)),
        sgT=np.ascontiguousarray(sg.T), suT=np.ascontiguousarray(su.T),
        sdT=np.ascontiguousarray(sd.T),
        gw=np.ascontiguousarray(gw), uw=np.ascontiguousarray(uw),
        dw=np.ascontiguousarray(dw),
        rwT=np.ascontiguousarray(rw.T),
        rb=np.ascontiguousarray(rb.reshape(NE, 1)),
    )


def kernel(L0_attn_x, L0_attn_W_qkv, L0_attn_b_qkv, L0_attn_W_out, L0_attn_b_out,
           L0_ffn_weight, L1_attn_Wq, L1_attn_bq, L1_attn_Wk, L1_attn_bk,
           L1_attn_Wv, L1_attn_bv, L1_attn_Wo, L1_attn_bo, L1_ffn_shared_gate,
           L1_ffn_shared_up, L1_ffn_shared_down, L1_ffn_gate_w, L1_ffn_up_w,
           L1_ffn_down_w, L1_ffn_router_w, L1_ffn_routing_bias):
    S = L0_attn_x.shape[1]
    x = np.ascontiguousarray(np.asarray(L0_attn_x, dtype=np.float32))
    w0 = _l0_inputs(np.asarray(L0_attn_W_qkv), np.asarray(L0_attn_b_qkv),
                    np.asarray(L0_attn_W_out), np.asarray(L0_attn_b_out),
                    np.asarray(L0_ffn_weight))
    w1 = _l1_inputs(np.asarray(L1_attn_Wq), np.asarray(L1_attn_bq),
                    np.asarray(L1_attn_Wk), np.asarray(L1_attn_bk),
                    np.asarray(L1_attn_Wv), np.asarray(L1_attn_bv),
                    np.asarray(L1_attn_Wo), np.asarray(L1_attn_bo),
                    np.asarray(L1_ffn_shared_gate), np.asarray(L1_ffn_shared_up),
                    np.asarray(L1_ffn_shared_down), np.asarray(L1_ffn_gate_w),
                    np.asarray(L1_ffn_up_w), np.asarray(L1_ffn_down_w),
                    np.asarray(L1_ffn_router_w), np.asarray(L1_ffn_routing_bias))

    nc0 = build_l0(S)
    maps0 = [dict(w0, x=np.ascontiguousarray(x[c])) for c in range(NCORES)]
    res0 = run_bass_kernel_spmd(nc0, maps0, core_ids=list(range(NCORES)))
    x1 = [res0.results[c]["x1"] for c in range(NCORES)]

    nc1 = build_l1(S)
    maps1 = [dict(w1, x=np.ascontiguousarray(x1[c])) for c in range(NCORES)]
    res1 = run_bass_kernel_spmd(nc1, maps1, core_ids=list(range(NCORES)))
    out = np.stack([res1.results[c]["out"] for c in range(NCORES)], axis=0)
    return out.astype(np.float32)


# version 02
# speedup vs baseline: 1.0137x; 1.0137x over previous
import math
from contextlib import ExitStack

import numpy as np

import concourse.bass as bass
import concourse.mybir as mybir
from concourse.bass_utils import run_bass_kernel_spmd

F32 = mybir.dt.float32
BF16 = mybir.dt.bfloat16
AF = mybir.ActivationFunctionType
ALU = mybir.AluOpType
AX = mybir.AxisListType

D = 1024
NH = 16
DH = 64
NP = NH // 2
NCH = 8
SCALE = 1.0 / math.sqrt(DH)
EPS = 1e-5
IFF = 2048
NE = 8
NCORES = 8


class Sem:
    def __init__(self, h):
        self.h = h
        self.n = 0


class Plan:
    ENGINES = ("tensor", "vector", "scalar", "gpsimd", "sync")

    def __init__(self):
        self.q = {e: [] for e in self.ENGINES}

    def op(self, eng, fn, waits=(), inc=None):
        if inc is not None:
            sem, k = inc
            sem.n += k
            inc = (sem, k)
        self.q[eng].append((tuple(waits), fn, inc))

    def realize(self, block):
        def runner(name):
            def run(e):
                last = {}
                for waits, fn, inc in self.q[name]:
                    for sem, val in waits:
                        if val > 0 and last.get(sem, 0) < val:
                            e.wait_ge(sem.h, val)
                            last[sem] = val
                    r = fn(e)
                    if inc is not None:
                        r.then_inc(inc[0].h, inc[1])
            return run
        block.tensor(runner("tensor"))
        block.vector(runner("vector"))
        block.scalar(runner("scalar"))
        block.gpsimd(runner("gpsimd"))
        block.sync(runner("sync"))


def MM(out, lhsT, rhs, start, stop):
    return lambda e, o=out, l=lhsT, r=rhs, s1=start, s2=stop: e.matmul(
        o, l, r, start=s1, stop=s2)


def TRN(out, in_, ident):
    return lambda e, o=out, i=in_, d=ident: e.transpose(o, i, d)


def ACT(out, in_, func, bias=0.0, scale=1.0):
    return lambda e, o=out, i=in_, f=func, b=bias, s=scale: e.activation(
        o, i, f, bias=b, scale=s)


def TS(out, in0, s1, s2, op0, op1=ALU.bypass):
    return lambda e, o=out, i=in0, a=s1, b=s2, p0=op0, p1=op1: e.tensor_scalar(
        o, i, a, b, p0, p1)


def TT(out, in0, in1, op):
    return lambda e, o=out, a=in0, b=in1, p=op: e.tensor_tensor(o, a, b, p)


def STT(out, in0, scalar, in1, op0, op1):
    return lambda e, o=out, a=in0, s=scalar, b=in1, p0=op0, p1=op1: \
        e.scalar_tensor_tensor(o, a, s, b, p0, p1)


def CP(out, in_):
    return lambda e, o=out, i=in_: e.tensor_copy(o, i)


def RED(out, in_, op):
    return lambda e, o=out, i=in_, p=op: e.tensor_reduce(o, i, AX.X, p)


def RECIP(out, in_):
    return lambda e, o=out, i=in_: e.reciprocal(o, i)


def DMA(out, in_):
    return lambda e, o=out, i=in_: e.dma_start(out=o, in_=i)


def MEMSET(ap, v):
    return lambda e, a=ap, vv=v: e.memset(a, vv)


def SCOPY(out, in_):
    return lambda e, o=out, i=in_: e.copy(o, i)


class Bag(dict):
    __getattr__ = dict.__getitem__
    __setattr__ = dict.__setitem__


def alloc_bufs(nc, ctx, S, l1):
    E = ctx.enter_context
    b = Bag()
    b.S = S
    b.nt = S // 128
    b.nqb = S // 512
    b.nkb = S // 128
    b.ps_q = E(nc.psum_tensor("ps_q", [128, 2048], F32))
    b.ps_a = E(nc.psum_tensor("ps_a", [128, 512], F32))
    b.ps_b = E(nc.psum_tensor("ps_b", [128, 512], F32))
    b.ps_c = E(nc.psum_tensor("ps_c", [128, 512], F32))
    b.ps_d = E(nc.psum_tensor("ps_d", [128, 512], F32))
    b.ps_tr = [b.ps_c[:].bitcast(BF16), b.ps_d[:].bitcast(BF16)]
    xnt_cols = max(NCH * S, 2 * IFF * NCH) if l1 else NCH * S
    b.xnT = E(nc.sbuf_tensor("xnT", [128, xnt_cols], BF16))
    b.ctxT = E(nc.sbuf_tensor("ctxT", [128, NCH * S], BF16))
    arc_cols = max(2 * S + (S // 128) * 192, IFF * NCH)
    b.arC = E(nc.sbuf_tensor("arC", [128, arc_cols], BF16))
    b.qp = b.arC[:, 0:S]
    b.kp = b.arC[:, S:2 * S]
    b.vT = b.arC[:, 2 * S:2 * S + (S // 128) * 192]
    b.arE = E(nc.sbuf_tensor("arE", [128, 16 * 512], BF16))
    b.wf = b.arE[:, 0:8192]
    b.wq = b.arE[:, 0:1024]
    b.wk = b.arE[:, 1024:2048]
    b.wv = b.arE[:, 2048:3072]
    b.pt = b.arE[:, 3072:5120]
    b.g_sb = b.arE[:, 0:4096]
    b.u_sb = b.arE[:, 4096:8192]
    b.arF = E(nc.sbuf_tensor("arF", [128, NCH * D], BF16))
    b.wo = b.arF[:, :]
    b.slab = [b.arF[:, 0:2048], b.arF[:, 2048:4096]]
    b.xsq = b.arF[:].bitcast(F32)[:, 0:D]
    b.xn_tm = [b.arF[:, 2048:3072], b.arF[:, 3072:4096]]
    b.lo_tm = b.arF[:, 4096:5120]
    b.x_t = [E(nc.sbuf_tensor(f"x_t{i}", [128, D], F32)) for i in range(2)]
    b.st = [E(nc.sbuf_tensor(f"st{i}", [128, 8], F32)) for i in range(2)]
    b.vtmp = [E(nc.sbuf_tensor(f"vtmp{i}", [128, 512], BF16)) for i in range(2)]
    b.bc_sb = b.vtmp[0]
    b.bo_bc = E(nc.sbuf_tensor("bo_bc", [128, D], BF16))
    b.bq_s = E(nc.sbuf_tensor("bq_s", [128, NP], F32))
    b.bk_s = E(nc.sbuf_tensor("bk_s", [128, NP], F32))
    b.bv_s = E(nc.sbuf_tensor("bv_s", [128, NP], F32))
    b.rowscr = E(nc.sbuf_tensor("rowscr", [65, 512], F32))
    b.recip_t = b.rowscr
    b.nrm = b.rowscr[0:1, 0:512]
    b.u_t = E(nc.sbuf_tensor("u_t", [1, 16], F32))
    b.biasU = [E(nc.sbuf_tensor(f"biasU{i}", [128, 1], F32)) for i in range(2)]
    b.ident = E(nc.sbuf_tensor("ident", [128, 128], BF16))
    b.ones_colA = E(nc.sbuf_tensor("ones_colA", [128, 1], BF16))
    b.ones_colB = E(nc.sbuf_tensor("ones_colB", [128, 1], BF16))
    b.ones_t = E(nc.sbuf_tensor("ones_t", [65, 128], F32))
    b.colofs = E(nc.sbuf_tensor("colofs", [128, NE], F32))
    b.ident8 = E(nc.sbuf_tensor("ident8", [NE, NE], F32))
    if l1:
        nt = b.nt
        arEf = b.arE[:].bitcast(F32)
        b.rw = b.arE[:, 3072:3072 + NCH * NE]
        b.rwd = b.arE[:, 3072 + NCH * NE:3072 + 2 * NCH * NE]
        b.rwf = arEf[:, 1600:1600 + NCH * NE]
        b.rb = E(nc.sbuf_tensor("rb_sb", [NE, 1], F32))
        b.pbuf = [arEf[0:NE, 512:1024], arEf[0:NE, 1024:1536]]
        b.probs_tm = arEf[:, 0:nt * NE]
        b.t2 = arEf[:, nt * NE:nt * NE + 40]
        b.w_rt = E(nc.sbuf_tensor("w_rt_sb", [128, nt * NE], BF16))
        b.gate_v = b.xnT[:, 0:IFF * NCH]
        b.up_v = b.xnT[:, IFF * NCH:2 * IFF * NCH]
        b.down_v = b.arC[:, 0:IFF * NCH]
    return b


def mksems(nc, ctx, sm):
    E = ctx.enter_context
    for name in ("sLD", "sWD", "sXD", "sT", "sV", "sS", "sG",
                 "sXA", "sXB", "sOA", "sOB", "sSA", "sSB"):
        sm[name] = Sem(E(nc.semaphore(name)))
    sm.sXp = (sm.sXA, sm.sXB)
    sm.sOp = (sm.sOA, sm.sOB)
    sm.sSp = (sm.sSA, sm.sSB)


def emit_consts(p, b, sm):
    sG = sm.sG
    sV = sm.sV
    p.op("gpsimd", MEMSET(b.ident[:], 0.0), inc=(sG, 1))
    p.op("gpsimd", lambda e, o=b.ident[:]: e.affine_select(
        out=o, in_=o, compare_op=ALU.not_equal, fill=1.0, base=0,
        pattern=[[-1, 128]], channel_multiplier=1),
        waits=[(sG, sG.n)], inc=(sG, 1))
    p.op("gpsimd", MEMSET(b.ones_colA[:], 0.0), inc=(sG, 1))
    p.op("gpsimd", MEMSET(b.ones_colA[0:64, :], 1.0),
         waits=[(sG, sG.n)], inc=(sG, 1))
    p.op("gpsimd", MEMSET(b.ones_colB[:], 0.0), inc=(sG, 1))
    p.op("gpsimd", MEMSET(b.ones_colB[64:128, :], 1.0),
         waits=[(sG, sG.n)], inc=(sG, 1))
    p.op("gpsimd", MEMSET(b.ones_t[0:1, :], 1.0), inc=(sG, 1))
    p.op("gpsimd", MEMSET(b.ones_t[32:33, :], 1.0), inc=(sG, 1))
    p.op("gpsimd", MEMSET(b.ones_t[64:65, :], 1.0), inc=(sG, 1))
    p.op("gpsimd", lambda e, o=b.colofs[:]: e.iota(
        o, pattern=[[1, NE]], base=0, channel_multiplier=0,
        allow_small_or_imprecise_dtypes=True), inc=(sG, 1))
    p.op("gpsimd", MEMSET(b.ident8[:], 0.0), inc=(sG, 1))
    p.op("gpsimd", lambda e, o=b.ident8[:]: e.affine_select(
        out=o, in_=o, compare_op=ALU.not_equal, fill=1.0, base=0,
        pattern=[[-1, NE]], channel_multiplier=1),
        waits=[(sG, sG.n)], inc=(sG, 1))
    p.op("vector", TS(b.colofs[:], b.colofs[:], 1e-6, None, ALU.mult),
         waits=[(sG, sG.n)], inc=(sV, 1))
    sm.consts_done = sG.n
    sm.colofs_done = sm.sV.n


def emit_bo_bc(p, b, sm):
    sT, sV = sm.sT, sm.sV
    for half, ps, row in ((0, b.ps_c, 0), (1, b.ps_d, 32)):
        p.op("tensor", MM(ps[0:128, 0:512], b.ones_t[row:row + 1, :],
                          b.rowscr[row:row + 1, 0:512], True, True),
             waits=[(sm.sXD, sm.consts_dma_done), (sm.sG, sm.consts_done)],
             inc=(sT, 1))
        p.op("vector", CP(b.bo_bc[:, half * 512:(half + 1) * 512], ps[0:128, 0:512]),
             waits=[(sT, sT.n)], inc=(sV, 1))
    sm.bo_bc_done = sV.n


def emit_ln(p, b, sm, x_hbm, xn_hbm=None, router=False, entry=()):
    sXD, sT, sV, sS, sG = sm.sXD, sm.sT, sm.sV, sm.sS, sm.sG
    S, nt = b.S, b.nt
    sq_done, xn_done, evac_done = {}, {}, {}
    rmm_done = {-1: 0, -2: 0}
    sig_done = {}
    tr_evac = {}
    loroll = [b.arC[:, 0:NCH * 512], b.arC[:, NCH * 512:2 * NCH * 512]]
    for i in range(nt):
        bb = i % 2
        sx = sm.sXp[bb]
        waits = list(entry) if i < 2 else []
        if i >= 2:
            waits = [(sS, sq_done[i - 2]), (sV, xn_done[i - 2])]
        p.op("sync", DMA(b.x_t[bb][:], x_hbm.ap()[i * 128:(i + 1) * 128, :]),
             waits=waits, inc=(sx, 16))
        xd = sx.n
        st = b.st[bb]
        p.op("vector", RED(st[:, 0:1], b.x_t[bb][:], ALU.add), waits=[(sx, xd)],
             inc=(sV, 1))
        sqw = [(sx, xd)] + (list(entry) if i < 2 else [])
        if i >= 1:
            sqw.append((sV, (xn_done[i - 1] if router else
                             sm.get(("s2", i - 1), 0))))
        p.op("scalar", lambda e, o=b.xsq[:], i_=b.x_t[bb][:]: e.square(o, i_),
             waits=sqw, inc=(sS, 1))
        sq_done[i] = sS.n
        p.op("vector", RED(st[:, 1:2], b.xsq[:], ALU.add),
             waits=[(sS, sq_done[i])], inc=(sV, 1))
        sm[("s2", i)] = sV.n
        for fn in (TS(st[:, 2:3], st[:, 0:1], 1.0 / D, None, ALU.mult),
                   TS(st[:, 3:4], st[:, 1:2], 1.0 / D, None, ALU.mult),
                   TT(st[:, 4:5], st[:, 2:3], st[:, 2:3], ALU.mult),
                   TT(st[:, 5:6], st[:, 3:4], st[:, 4:5], ALU.subtract),
                   TS(st[:, 5:6], st[:, 5:6], EPS, None, ALU.add)):
            p.op("vector", fn, waits=[(sV, sV.n)], inc=(sV, 1))
        var_done = sV.n
        p.op("scalar", ACT(st[:, 6:7], st[:, 5:6], AF.Sqrt),
             waits=[(sV, var_done)], inc=(sS, 1))
        sd_done = sS.n
        p.op("vector", RECIP(st[:, 7:8], st[:, 6:7]), waits=[(sS, sd_done)],
             inc=(sV, 1))
        if not router:
            p.op("vector", TS(b.xn_tm[bb][:], b.x_t[bb][:], st[:, 2:3],
                              st[:, 7:8], ALU.subtract, ALU.mult),
                 waits=[(sV, sV.n)], inc=(sV, 1))
        else:
            p.op("vector", TS(b.xsq[:], b.x_t[bb][:], st[:, 2:3],
                              st[:, 7:8], ALU.subtract, ALU.mult),
                 waits=[(sV, sV.n)], inc=(sV, 1))
            p.op("vector", CP(b.xn_tm[bb][:], b.xsq[:]), waits=[(sV, sV.n)],
                 inc=(sV, 1))
            p.op("vector", TT(b.lo_tm[:], b.xsq[:], b.xn_tm[bb][:],
                              ALU.subtract), waits=[(sV, sV.n)], inc=(sV, 1))
        xn_done[i] = sV.n
        par = (i // 4) % 2
        nsub = 2 if router else 1
        for c in range(NCH):
            for sub in range(nsub):
                key = (i * NCH + c) * nsub + sub - 2
                tw = [(sV, xn_done[i]), (sG, sm.consts_done)]
                if key >= 0:
                    tw.append((sV, evac_done[key]))
                if router and sub == 1 and c == 0 and i % 4 == 0:
                    tw.append((sT, rmm_done[i // 4 - 2]))
                srcb = b.xn_tm[bb] if sub == 0 else b.lo_tm
                p.op("tensor", TRN(b.ps_tr[(c * nsub + sub) % 2][:, 0:128],
                                   srcb[:, c * 128:(c + 1) * 128], b.ident[:]),
                     waits=tw, inc=(sT, 1))
                trd = sT.n
                if sub == 0:
                    dst = b.xnT[:, c * S + i * 128: c * S + (i + 1) * 128]
                else:
                    dst = loroll[par][:, c * 512 + (i % 4) * 128:
                                      c * 512 + (i % 4) * 128 + 128]
                p.op("vector", CP(dst, b.ps_tr[(c * nsub + sub) % 2][:, 0:128]),
                     waits=[(sT, trd)], inc=(sV, 1))
                evac_done[(i * NCH + c) * nsub + sub] = sV.n
        if router and i % 4 == 3:
            j = i // 4
            lo_ready = sV.n
            psl = b.ps_q[0:NE, (j % 2) * 512:(j % 2) * 512 + 512]
            nmm = 3 * NCH
            for ci in range(nmm):
                c = ci % NCH
                if ci < NCH:
                    wsel, xsel = b.rw, b.xnT[:, c * S + j * 512: c * S + j * 512 + 512]
                elif ci < 2 * NCH:
                    wsel, xsel = b.rwd, b.xnT[:, c * S + j * 512: c * S + j * 512 + 512]
                else:
                    wsel, xsel = b.rw, loroll[par][:, c * 512:(c + 1) * 512]
                waits = []
                if ci == 0:
                    waits = [(sV, lo_ready), (sV, sm.rw_ready)]
                    if j >= 2:
                        waits.append((sS, sig_done[j - 2]))
                p.op("tensor", MM(psl, wsel[:, c * NE:(c + 1) * NE], xsel,
                                  ci == 0, ci == nmm - 1),
                     waits=waits, inc=((sT, 1) if ci == nmm - 1 else None))
            rmm_done[j] = sT.n
            sw = [(sT, rmm_done[j]), (sXD, sm.consts_dma_done)]
            if j >= 2:
                sw.append((sT, tr_evac[("tr", j - 2)]))
            p.op("scalar", ACT(b.pbuf[j % 2], psl, AF.Sigmoid, bias=b.rb[:],
                               scale=1.0), waits=sw, inc=(sS, 1))
            sig_done[j] = sS.n
            for tt in range(4):
                ii = j * 4 + tt
                pso = b.ps_q[0:128, 1024 + (ii % 2) * 512: 1024 + (ii % 2) * 512 + NE]
                waits = [(sS, sig_done[j])]
                if ii >= 2:
                    waits.append((sV, tr_evac[ii - 2]))
                p.op("tensor", TRN(pso, b.pbuf[j % 2][0:NE, tt * 128:(tt + 1) * 128],
                                   b.ident8[:]), waits=waits, inc=(sT, 1))
                p.op("vector", CP(b.probs_tm[:, ii * NE:(ii + 1) * NE], pso),
                     waits=[(sT, sT.n)], inc=(sV, 1))
                tr_evac[ii] = sV.n
            tr_evac[("tr", j)] = sT.n
    sm.ln_last_xn = xn_done[nt - 1]
    sm.ln_last_sq = sq_done[nt - 1]
    sm.ln_last_evac = sV.n
    sm.router_tr_evac = tr_evac
    if xn_hbm is not None:
        p.op("sync", DMA(xn_hbm.ap().rearrange("(c r) t -> r c t", r=128),
                         b.xnT[:, 0:NCH * S].rearrange("r (c t) -> r c t", t=S)),
             waits=[(sV, sm.ln_last_evac)], inc=(sXD, 16))
        sm.xn_hbm_done = sXD.n


def emit_attention(p, b, sm, wq_h, wk_h, wv_h, stage="full"):
    sLD, sT, sV, sS = sm.sLD, sm.sT, sm.sV, sm.sS
    S, nqb, nkb = b.S, b.nqb, b.nkb
    nquad = nkb // 4
    wdma_done = {}
    pair_pe_done = {}
    pair_evac_done = {}
    sm.bc_copy_done = sm.get("bc_copy_done", 0)

    def plan_wdma(pp):
        waits = []
        if pp >= 1:
            waits = [(sT, pair_pe_done[pp - 1])]
        else:
            waits = list(sm.get("first_wdma_waits", []))
        for wsb, hbm in ((b.wq, wq_h), (b.wk, wk_h), (b.wv, wv_h)):
            src = hbm.ap()[:, pp * 128:(pp + 1) * 128].rearrange(
                "(c r) o -> r c o", r=128)
            dst = wsb.rearrange("r (c o) -> r c o", o=128)
            p.op("gpsimd", DMA(dst, src), waits=waits, inc=(sLD, 16))
        wdma_done[pp] = sLD.n

    for pp in range(NP):
        plan_wdma(pp)
        p.op("vector", MEMSET(b.vT[:], 0.0),
             waits=[(sV, sV.n), (sT, sT.n), (sS, sm.bc_copy_done)], inc=(sV, 1))
        vT3 = b.vT.rearrange("r (kb c) -> r kb c", c=192)
        p.op("vector", MEMSET(vT3[:, :, 64:65], 1.0), waits=[(sV, sV.n)],
             inc=(sV, 1))
        p.op("vector", MEMSET(vT3[:, :, 96:97], 1.0), waits=[(sV, sV.n)],
             inc=(sV, 1))
        vT_init = sV.n
        evac_done = {}
        vT_evac = {}
        n_mm = 0
        trp = 0
        for xi, (wsb, brow, scl) in enumerate(((b.wq, b.bq_s, SCALE),
                                               (b.wk, b.bk_s, 1.0),
                                               (b.wv, b.bv_s, 1.0))):
            for j in range(S // 512):
                ps = b.ps_a if (n_mm % 2 == 0) else b.ps_b
                mmd = None
                for c in range(NCH):
                    waits = []
                    if c == 0:
                        waits = [(sLD, wdma_done[pp]), (sV, sm.get("ln_last_evac", 0))]
                        if n_mm >= 2:
                            waits.append((sV, evac_done[n_mm - 2]))
                        if n_mm == 0 and pp >= 1:
                            waits.append((sV, pair_evac_done[pp - 1]))
                    p.op("tensor", MM(ps[:], wsb[:, c * 128:(c + 1) * 128],
                                      b.xnT[:, c * S + j * 512: c * S + j * 512 + 512],
                                      c == 0, c == NCH - 1),
                         waits=waits, inc=((sT, 1) if c == NCH - 1 else None))
                mmd = sT.n
                if xi < 2:
                    dst = (b.qp, b.kp)[xi]
                    sl = dst[:, j * 512:(j + 1) * 512]
                else:
                    sl = b.vtmp[j % 2][:]
                if scl == 1.0:
                    fn = TS(sl, ps[:], brow[:, pp:pp + 1], None, ALU.add)
                else:
                    fn = TS(sl, ps[:], brow[:, pp:pp + 1], scl, ALU.add, ALU.mult)
                ew = [(sT, mmd)]
                if xi == 2:
                    ew.append((sV, vT_init))
                    if j >= 2:
                        ew.append((sT, vT_evac[("tr", j - 2)]))
                p.op("vector", fn, waits=ew, inc=(sV, 1))
                evac_done[n_mm] = sV.n
                n_mm += 1
                if xi == 2:
                    for tt in range(4):
                        kb = j * 4 + tt
                        pstr = b.ps_tr[trp % 2]
                        tw = [(sV, evac_done[n_mm - 1])]
                        if trp >= 2:
                            tw.append((sV, vT_evac[trp - 2]))
                        if trp < 2:
                            tw.append((sS, sm.bc_copy_done))
                        p.op("tensor", TRN(pstr[:, 0:128],
                                           b.vtmp[j % 2][:, tt * 128:(tt + 1) * 128],
                                           b.ident[:]), waits=tw, inc=(sT, 1))
                        trd = sT.n
                        p.op("vector", CP(b.vT[:, kb * 192:kb * 192 + 64],
                                          pstr[:, 0:64]), waits=[(sT, trd)])
                        p.op("vector", CP(b.vT[:, kb * 192 + 128:kb * 192 + 192],
                                          pstr[:, 64:128]), inc=(sV, 1))
                        vT_evac[trp] = sV.n
                        trp += 1
                    vT_evac[("tr", j)] = sT.n
        qkv_evac_last = sV.n
        vT_last = sV.n
        if stage == "qkv":
            return

        scr = b.ctxT[:, pp * S:(pp + 1) * S]
        nsq = {}
        jj = 0
        for t_i, src in enumerate((b.qp, b.kp)):
            p.op("vector", TT(scr, src, src, ALU.mult),
                 waits=[(sV, qkv_evac_last), (sT, sT.n), (sV, sV.n)],
                 inc=(sV, 1))
            sq_d = sV.n
            for half, onescol in ((0, b.ones_colA), (1, b.ones_colB)):
                for j in range(S // 512):
                    ps = b.ps_c if jj % 2 == 0 else b.ps_d
                    waits = [(sV, sq_d), (sS, sm.bc_copy_done)]
                    if jj >= 2:
                        waits.append((sV, nsq[jj - 2]))
                    p.op("tensor", MM(ps[0:1, 0:512], onescol[:],
                                      scr[:, j * 512:(j + 1) * 512], True, True),
                         waits=waits, inc=(sT, 1))
                    mmd = sT.n
                    if j == 0:
                        p.op("vector", CP(b.nrm[:], ps[0:1, 0:512]),
                             waits=[(sT, mmd), (sV, sV.n)], inc=(sV, 1))
                    else:
                        p.op("vector", TT(b.nrm[:], b.nrm[:], ps[0:1, 0:512],
                                          ALU.max), waits=[(sT, mmd), (sV, sV.n)],
                             inc=(sV, 1))
                    nsq[jj] = sV.n
                    jj += 1
                p.op("vector", RED(b.u_t[:, (t_i * 2 + half):(t_i * 2 + half) + 1],
                                   b.nrm[:], ALU.max), waits=[(sV, sV.n)],
                     inc=(sV, 1))
        if stage in ("c4a", "c4b", "c4z", "c4y"):
            return
        c4_pe_done = sT.n
        for half in range(2):
            p.op("vector", TS(b.u_t[:, 4 + half:5 + half],
                              b.u_t[:, 2 + half:3 + half], 1.0 / 16.0, None,
                              ALU.mult), waits=[(sV, sV.n)], inc=(sV, 1))
            p.op("vector", STT(b.u_t[:, 6 + half:7 + half],
                               b.u_t[:, half:half + 1], 4.0,
                               b.u_t[:, 4 + half:5 + half], ALU.mult, ALU.add),
                 waits=[(sV, sV.n)], inc=(sV, 1))
            p.op("vector", TS(b.u_t[:, 8 + half:9 + half],
                              b.u_t[:, 6 + half:7 + half], -0.5, None, ALU.mult),
                 waits=[(sV, sV.n)], inc=(sV, 1))
            negU_done = sV.n
            if stage == "c4c":
                continue
            ps = b.ps_c if half == 0 else b.ps_d
            p.op("tensor", MM(ps[0:128, 0:1], b.ones_t[0:1, :],
                              b.u_t[:, 8 + half:9 + half], True, True),
                 waits=[(sV, negU_done), (sV, sV.n)], inc=(sT, 1))
            p.op("scalar", SCOPY(b.biasU[half][:], ps[0:128, 0:1]),
                 waits=[(sT, sT.n)], inc=(sS, 1))
        if stage == "c4c":
            return
        biasU_done = sS.n
        if stage == "c4":
            return

        if stage == "c3":
            return

        exp_done = {}
        avs_done = {}
        mult_done = {}
        hq = 0
        for h in range(2):
            hofs = h * 64
            srow = 64 if h == 0 else 32
            crow = 0 if h == 0 else 64
            for qb in range(nqb):
                po = b.ps_a if hq % 2 == 0 else b.ps_b
                for quad in range(nquad):
                    for k4 in range(4):
                        kb = quad * 4 + k4
                        waits = []
                        if kb == 0:
                            waits += [(sS, biasU_done), (sV, vT_last)]
                            if hq >= 2:
                                waits.append((sV, mult_done[hq - 2]))
                            elif pp >= 1:
                                waits.append((sV, pair_evac_done[pp - 1]))
                        if k4 == 0 and quad > 0:
                            waits.append((sS, exp_done[(hq, quad - 1)]))
                        elif k4 == 0 and hq > 0:
                            waits.append((sS, exp_done[(hq - 1, nquad - 1)]))
                        p.op("tensor", MM(
                            b.ps_q[:, (kb % 4) * 512:(kb % 4 + 1) * 512],
                            b.kp[hofs:hofs + 64, kb * 128:(kb + 1) * 128],
                            b.qp[hofs:hofs + 64, qb * 512:(qb + 1) * 512],
                            True, True), waits=waits, inc=(sT, 1))
                    s_done = sT.n
                    w2 = [(sT, s_done)]
                    if quad > 0:
                        w2.append((sT, avs_done[(hq, quad - 1)]))
                    elif hq > 0:
                        w2.append((sT, avs_done[(hq - 1, nquad - 1)]))
                    p.op("scalar", ACT(b.pt[:, 0:2048], b.ps_q[:, 0:2048],
                                       AF.Exp, bias=b.biasU[h][:], scale=1.0),
                         waits=w2, inc=(sS, 1))
                    exp_done[(hq, quad)] = sS.n
                    for k4 in range(4):
                        kb = quad * 4 + k4
                        if h == 0:
                            lhs = b.vT[:, kb * 192: kb * 192 + 65]
                            out = po[0:65, 0:512]
                        else:
                            lhs = b.vT[:, kb * 192 + 64: kb * 192 + 192]
                            out = po[0:128, 0:512]
                        p.op("tensor", MM(out, lhs,
                                          b.pt[:, (kb % 4) * 512:(kb % 4 + 1) * 512],
                                          kb == 0, kb == nkb - 1),
                             waits=[(sS, exp_done[(hq, quad)])],
                             inc=((sT, 1) if k4 == 3 else None))
                    avs_done[(hq, quad)] = sT.n
                rsl = b.recip_t[srow:srow + 1, 0:512]
                p.op("vector", RECIP(rsl, po[srow:srow + 1, 0:512]),
                     waits=[(sT, avs_done[(hq, nquad - 1)])], inc=(sV, 1))
                rec_done = sV.n
                p.op("tensor", MM(b.ps_c[crow:crow + 64, 0:512],
                                  b.ones_t[srow:srow + 1, 0:64], rsl, True, True),
                     waits=[(sV, rec_done), (sS, sm.bc_copy_done)], inc=(sT, 1))
                bc_done = sT.n
                p.op("scalar", SCOPY(b.bc_sb[crow:crow + 64, :],
                                     b.ps_c[crow:crow + 64, 0:512]),
                     waits=[(sT, bc_done)], inc=(sS, 1))
                sm.bc_copy_done = sS.n
                dst = b.ctxT[hofs:hofs + 64,
                             pp * S + qb * 512: pp * S + qb * 512 + 512]
                p.op("vector", TT(dst, po[crow:crow + 64, 0:512],
                                  b.bc_sb[crow:crow + 64, :], ALU.mult),
                     waits=[(sS, sm.bc_copy_done)], inc=(sV, 1))
                mult_done[hq] = sV.n
                hq += 1
        pair_pe_done[pp] = sT.n
        pair_evac_done[pp] = sV.n
        if stage == "c5":
            return
    sm.attn_pe_done = sT.n
    sm.attn_v_done = sV.n
    sm.attn_s_done = sS.n


def emit_out_proj(p, b, sm, x_hbm, dst_hbm, sign_src=None, sign_ready=0):
    sXD, sT, sV, sS = sm.sXD, sm.sT, sm.sV, sm.sS
    S, nt = b.S, b.nt
    out_done = {}
    ev_done = {}
    for i in range(nt):
        bb = i % 2
        sx = sm.sXp[bb]
        so = sm.sOp[bb]
        waits = [(sV, sm.bo_bc_done), (sV, sm.ln_last_xn)]
        if i >= 2:
            waits.append((so, out_done[i - 2]))
        p.op("sync", DMA(b.x_t[bb][:], x_hbm.ap()[i * 128:(i + 1) * 128, :]),
             waits=waits, inc=(sx, 16))
        xd = sx.n
        for half in range(2):
            ps = (b.ps_a, b.ps_b, b.ps_c, b.ps_d)[2 * (i % 2) + half]
            nmm = NCH * (2 if sign_src is not None else 1)
            k = 0
            for c in range(NCH):
                fw = []
                if k == 0:
                    fw = [(sT, sm.attn_pe_done), (sV, sm.attn_v_done),
                          (sS, sm.bc_copy_done), (sm.sWD, sm.wo_dma)]
                    if i >= 2:
                        fw.append((sV, ev_done[i - 2]))
                p.op("tensor", MM(ps[:],
                                  b.ctxT[:, c * S + i * 128: c * S + (i + 1) * 128],
                                  b.wo[:, c * D + half * 512: c * D + half * 512 + 512],
                                  k == 0, k == nmm - 1),
                     waits=fw, inc=((sT, 1) if k == nmm - 1 else None))
                k += 1
            if sign_src is not None:
                for c in range(NCH):
                    fw = [(sS, sign_ready)] if c == 0 else []
                    p.op("tensor", MM(ps[:],
                                      b.xnT[:, c * S + i * 128: c * S + (i + 1) * 128],
                                      sign_src[:, c * D + half * 512: c * D + half * 512 + 512],
                                      False, k == nmm - 1),
                         waits=fw, inc=((sT, 1) if k == nmm - 1 else None))
                    k += 1
            mmd = sT.n
            sl = b.x_t[bb][:, half * 512:(half + 1) * 512]
            p.op("vector", STT(sl, ps[:], 1.0, sl, ALU.mult, ALU.add),
                 waits=[(sT, mmd), (sx, xd)], inc=(sV, 1))
            p.op("vector", TT(sl, sl, b.bo_bc[:, half * 512:(half + 1) * 512],
                              ALU.add), waits=[(sV, sV.n)], inc=(sV, 1))
        ev_done[i] = sV.n
        p.op("sync", DMA(dst_hbm.ap()[i * 128:(i + 1) * 128, :], b.x_t[bb][:]),
             waits=[(sV, ev_done[i])], inc=(so, 16))
        out_done[i] = so.n
    sm.outproj_oa = sm.sOA.n
    sm.outproj_ob = sm.sOB.n
    sm.outproj_v_done = sV.n
    sm.outproj_pe_done = sT.n


def emit_l0_body(p, b, sm, h, x_out, stage="full"):
    for dst, src in ((b.bq_s[:], h.bq.ap()), (b.bk_s[:], h.bk.ap()),
                     (b.bv_s[:], h.bv.ap()),
                     (b.rowscr[0:1, 0:512], h.bo.ap()[0:1, :]),
                     (b.rowscr[32:33, 0:512], h.bo.ap()[1:2, :])):
        p.op("sync", DMA(dst, src), inc=(sm.sXD, 16))
    sm.consts_dma_done = sm.sXD.n
    emit_bo_bc(p, b, sm)
    emit_ln(p, b, sm, h.x)
    p.op("gpsimd", DMA(b.wo.rearrange("r (c o) -> r c o", o=D),
                       h.woT.ap().rearrange("(c r) o -> r c o", r=128)),
         waits=[(sm.sV, sm.ln_last_evac), (sm.sT, sm.sT.n)],
         inc=(sm.sWD, 16))
    sm.wo_dma = sm.sWD.n
    emit_attention(p, b, sm, h.wqT, h.wkT, h.wvT, stage=stage)
    p.op("gpsimd", DMA(b.wf.rearrange("r (c o) -> r c o", o=D),
                       h.wfT.ap().rearrange("(c r) o -> r c o", r=128)),
         waits=[(sm.sT, sm.attn_pe_done), (sm.sS, sm.attn_s_done),
                (sm.sV, sm.attn_v_done)], inc=(sm.sWD, 16))
    p.op("scalar", ACT(b.wf, b.wf, AF.Sign),
         waits=[(sm.sWD, sm.sWD.n)], inc=(sm.sS, 1))
    sign_ready = sm.sS.n
    emit_out_proj(p, b, sm, h.x, x_out, sign_src=b.wf, sign_ready=sign_ready)


def build_l0(S, stage="full"):
    nc = bass.Bass("TRN2", target_bir_lowering=False, detect_race_conditions=False)
    h = Bag()
    h.x = nc.dram_tensor("x", [S, D], F32, kind="ExternalInput")
    for nm, shp in (("wqT", [D, D]), ("wkT", [D, D]), ("wvT", [D, D]),
                    ("woT", [D, D]), ("wfT", [D, D]),
                    ("bq", [128, NP]), ("bk", [128, NP]), ("bv", [128, NP]),
                    ("bo", [2, 512])):
        h[nm] = nc.dram_tensor(nm, shp, F32, kind="ExternalInput")
    h.x1 = nc.dram_tensor("x1", [S, D], F32, kind="ExternalOutput")
    with ExitStack() as ctx:
        b = alloc_bufs(nc, ctx, S, l1=False)
        E = ctx.enter_context
        sm = Bag()
        mksems(nc, ctx, sm)
        block = E(nc.Block())
        p = Plan()
        emit_consts(p, b, sm)
        emit_l0_body(p, b, sm, h, h.x1, stage=stage)
        p.op("sync", lambda e: e.nop(),
             waits=[(sm.sOA, sm.sOA.n), (sm.sOB, sm.sOB.n)])
        p.realize(block)
    return nc


def emit_l1_body(p, b, sm, h, x_in, out_h, xn_hbm, part_hbm):
    sLD, sXD, sT, sV, sS = sm.sLD, sm.sXD, sm.sT, sm.sV, sm.sS
    nt, ntb = b.nt, b.S // 256
    ent = [(sm.sT, sm.sT.n), (sm.sV, sm.sV.n), (sm.sS, sm.sS.n),
           (sm.sOA, sm.sOA.n), (sm.sOB, sm.sOB.n)]
    for dst, src in ((b.bq_s[:], h.bq.ap()), (b.bk_s[:], h.bk.ap()),
                     (b.bv_s[:], h.bv.ap()),
                     (b.rowscr[0:1, 0:512], h.bo.ap()[0:1, :]),
                     (b.rowscr[32:33, 0:512], h.bo.ap()[1:2, :]),
                     (b.rb[:], h.rb.ap())):
        p.op("sync", DMA(dst, src), waits=ent, inc=(sXD, 16))
    p.op("sync", DMA(b.rwf.rearrange("r (c o) -> r c o", o=NE),
                     h.rwT.ap().rearrange("(c r) o -> r c o", r=128)),
         waits=ent, inc=(sXD, 16))
    sm.consts_dma_done = sXD.n
    rwf_dma = sXD.n
    p.op("vector", CP(b.rw, b.rwf), waits=[(sXD, rwf_dma)] + ent, inc=(sV, 1))
    p.op("vector", TT(b.rwd, b.rwf, b.rw, ALU.subtract),
         waits=[(sV, sV.n)], inc=(sV, 1))
    sm.rw_ready = sV.n
    emit_bo_bc(p, b, sm)
    emit_ln(p, b, sm, x_in, xn_hbm=xn_hbm, router=True, entry=ent)
    p.op("gpsimd", DMA(b.wo.rearrange("r (c o) -> r c o", o=D),
                       h.woT.ap().rearrange("(c r) o -> r c o", r=128)),
         waits=[(sm.sV, sm.ln_last_evac), (sm.sT, sm.sT.n)],
         inc=(sm.sWD, 16))
    sm.wo_dma = sm.sWD.n

    for i in range(nt):
        pr = b.probs_tm[:, i * NE:(i + 1) * NE]
        t2 = b.t2
        for fn in (TT(t2[:, 32:40], pr, b.colofs[:], ALU.subtract),
                   RED(t2[:, 0:1], t2[:, 32:40], ALU.max),
                   TS(t2[:, 8:16], t2[:, 32:40], t2[:, 0:1], None,
                      ALU.is_equal),
                   STT(t2[:, 16:24], t2[:, 8:16], -4.0, t2[:, 32:40],
                       ALU.mult, ALU.add),
                   RED(t2[:, 1:2], t2[:, 16:24], ALU.max),
                   TS(t2[:, 24:32], t2[:, 16:24], t2[:, 1:2], None,
                      ALU.is_equal),
                   TT(t2[:, 2:3], t2[:, 0:1], t2[:, 1:2], ALU.add),
                   RECIP(t2[:, 3:4], t2[:, 2:3]),
                   TS(t2[:, 8:16], t2[:, 8:16], t2[:, 0:1], t2[:, 3:4],
                      ALU.mult, ALU.mult),
                   TS(t2[:, 24:32], t2[:, 24:32], t2[:, 1:2], t2[:, 3:4],
                      ALU.mult, ALU.mult),
                   TT(b.w_rt[:, i * NE:(i + 1) * NE], t2[:, 8:16],
                      t2[:, 24:32], ALU.add)):
            p.op("vector", fn, waits=[(sV, sV.n)], inc=(sV, 1))
    sm.w_done = sV.n

    sm.first_wdma_waits = [(sV, sm.w_done), (sS, sS.n)]
    emit_attention(p, b, sm, h.wqT, h.wkT, h.wvT)
    emit_out_proj(p, b, sm, x_in, part_hbm, sign_src=None)

    comb = b.ctxT
    gup_dma = {}
    slab_dma = {}
    gu_pe = {}
    comb_ev = {}
    comb_tile_done = {}

    for ei in range(NE + 1):
        if ei == 0:
            srcs = (h.sgT.ap(), h.suT.ap(), h.sdT.ap())
            waits = [(sT, sT.n), (sV, sV.n), (sS, sS.n),
                     (sXD, sm.xn_hbm_done)]
        else:
            srcs = (h.gw.ap()[ei - 1], h.uw.ap()[ei - 1], h.dw.ap()[ei - 1])
            waits = [(sT, gu_pe[(ei - 1, ntb - 1)])]
        for dst, src, oo in ((b.gate_v, srcs[0], IFF), (b.up_v, srcs[1], IFF),
                             (b.down_v, srcs[2], D)):
            p.op("gpsimd", DMA(dst.rearrange("r (c o) -> r c o", o=oo),
                               src.rearrange("(c r) o -> r c o", r=128)),
                 waits=waits, inc=(sLD, 16))
        gup_dma[ei] = sLD.n

        for tb in range(ntb):
            bb = tb % 2
            ssl = sm.sSp[bb]
            waits = [(sXD, sm.xn_hbm_done)]
            if ei == 0 and tb < 2:
                waits.append((sT, sm.outproj_pe_done))
            pk = (ei, tb - 2) if tb >= 2 else (ei - 1, ntb - 2 + tb)
            if pk in gu_pe:
                waits.append((sT, gu_pe[pk]))
            p.op("sync", DMA(b.slab[bb].rearrange("r (c t) -> r c t", t=256),
                             xn_hbm.ap().rearrange("(c r) t -> r c t", r=128)
                             [:, :, tb * 256:(tb + 1) * 256]),
                 waits=waits, inc=(ssl, 16))
            slab_dma[(ei, tb)] = (ssl, ssl.n)
            ev = {}
            for ic in range(2 * (IFF // 128)):
                isup = ic % 2
                icc = ic // 2
                ps = b.ps_b if isup else b.ps_a
                wv_ = b.up_v if isup else b.gate_v
                for c in range(NCH):
                    waits = []
                    if c == 0:
                        waits = [(sLD, gup_dma[ei]), slab_dma[(ei, tb)]]
                        if ic >= 2:
                            waits.append((sS, ev[ic - 2]))
                        elif (ei, tb) == (0, 0):
                            waits.append((sV, sm.outproj_v_done))
                            waits.append((sS, sS.n))
                        else:
                            waits.append((sS, guev_last))
                            pk0 = (ei, tb - 1) if tb >= 1 else (ei - 1, ntb - 1)
                            waits.append((sV, comb_ev[pk0]))
                    p.op("tensor", MM(ps[:, 0:256],
                                      wv_[:, c * IFF + icc * 128: c * IFF + icc * 128 + 128],
                                      b.slab[bb][:, c * 256:(c + 1) * 256],
                                      c == 0, c == NCH - 1),
                         waits=waits, inc=((sT, 1) if c == NCH - 1 else None))
                mmd = sT.n
                dst = (b.u_sb if isup else b.g_sb)[:, icc * 256:(icc + 1) * 256]
                p.op("scalar", ACT(dst, ps[:, 0:256],
                                   AF.Copy if isup else AF.Silu),
                     waits=[(sT, mmd)], inc=(sS, 1))
                ev[ic] = sS.n
            guev_last = sS.n
            p.op("vector", TT(b.g_sb[:, 0:4096], b.g_sb[:, 0:4096],
                              b.u_sb[:, 0:4096], ALU.mult),
                 waits=[(sS, ev[2 * (IFF // 128) - 1])], inc=(sV, 1))
            gu_done = sV.n
            dmm = {}
            for tt in range(2):
                for oh in range(2):
                    psl = b.ps_q[:, (tt * 2 + oh) * 512:(tt * 2 + oh + 1) * 512]
                    for ic in range(IFF // 128):
                        waits = []
                        if ic == 0:
                            waits = [(sV, gu_done)]
                            if (ei, tb) == (0, 0):
                                waits.append((sS, sm.attn_s_done))
                            pk2 = (ei, tb - 1) if tb >= 1 else (ei - 1, ntb - 1)
                            if pk2 in comb_ev:
                                waits.append((sV, comb_ev[pk2]))
                        p.op("tensor", MM(psl,
                                          b.g_sb[:, ic * 256 + tt * 128: ic * 256 + tt * 128 + 128],
                                          b.down_v[:, ic * D + oh * 512: ic * D + oh * 512 + 512],
                                          ic == 0, ic == IFF // 128 - 1),
                             waits=waits,
                             inc=((sT, 1) if ic == IFF // 128 - 1 else None))
                    dmm[(tt, oh)] = sT.n
            gu_pe[(ei, tb)] = sT.n
            for tt in range(2):
                ti = tb * 2 + tt
                for oh in range(2):
                    psl = b.ps_q[:, (tt * 2 + oh) * 512:(tt * 2 + oh + 1) * 512]
                    csl = comb[:, ti * D + oh * 512: ti * D + oh * 512 + 512]
                    waits = [(sT, dmm[(tt, oh)])]
                    if ei == 0:
                        waits.append((sV, sm.outproj_v_done))
                        fn = CP(csl, psl)
                    else:
                        if ei == 1:
                            waits.append((sV, sm.w_done))
                        fn = STT(csl, psl, b.w_rt[:, ti * NE + ei - 1: ti * NE + ei],
                                 csl, ALU.mult, ALU.add)
                    p.op("vector", fn, waits=waits, inc=(sV, 1))
            comb_ev[(ei, tb)] = sV.n
            if ei == NE:
                comb_tile_done[tb] = sV.n

    fin = {}
    for i in range(nt):
        bb = i % 2
        sx = sm.sXp[bb]
        so = sm.sOp[bb]
        waits = [(sm.sOA, sm.outproj_oa), (sm.sOB, sm.outproj_ob),
                 (sV, sm.outproj_v_done)]
        if i >= 2:
            waits.append((so, fin[i - 2]))
        p.op("sync", DMA(b.x_t[bb][:], part_hbm.ap()[i * 128:(i + 1) * 128, :]),
             waits=waits, inc=(sx, 16))
        pd = sx.n
        p.op("vector", TT(b.x_t[bb][:], b.x_t[bb][:],
                          comb[:, i * D:(i + 1) * D], ALU.add),
             waits=[(sx, pd), (sV, comb_tile_done[i // 2])], inc=(sV, 1))
        p.op("sync", DMA(out_h.ap()[i * 128:(i + 1) * 128, :], b.x_t[bb][:]),
             waits=[(sV, sV.n)], inc=(so, 16))
        fin[i] = so.n


def build_l1(S):
    nc = bass.Bass("TRN2", target_bir_lowering=False, detect_race_conditions=False)
    h = Bag()
    h.x = nc.dram_tensor("x", [S, D], F32, kind="ExternalInput")
    for nm, shp in (("wqT", [D, D]), ("wkT", [D, D]), ("wvT", [D, D]),
                    ("woT", [D, D]), ("bq", [128, NP]), ("bk", [128, NP]),
                    ("bv", [128, NP]), ("bo", [2, 512]),
                    ("sgT", [D, IFF]), ("suT", [D, IFF]), ("sdT", [IFF, D]),
                    ("gw", [NE, D, IFF]), ("uw", [NE, D, IFF]),
                    ("dw", [NE, IFF, D]), ("rwT", [D, NE]), ("rb", [NE, 1])):
        h[nm] = nc.dram_tensor(nm, shp, F32, kind="ExternalInput")
    h.out = nc.dram_tensor("out", [S, D], F32, kind="ExternalOutput")
    xn_hbm = nc.dram_tensor("xn_hbm", [D, S], BF16, kind="Internal")
    part_hbm = nc.dram_tensor("part_hbm", [S, D], F32, kind="Internal")
    with ExitStack() as ctx:
        b = alloc_bufs(nc, ctx, S, l1=True)
        E = ctx.enter_context
        sm = Bag()
        mksems(nc, ctx, sm)
        block = E(nc.Block())
        p = Plan()
        emit_consts(p, b, sm)
        emit_l1_body(p, b, sm, h, h.x, h.out, xn_hbm, part_hbm)
        p.op("sync", lambda e: e.nop(),
             waits=[(sm.sOA, sm.sOA.n), (sm.sOB, sm.sOB.n)])
        p.realize(block)
    return nc


def build_fused(S):
    nc = bass.Bass("TRN2", target_bir_lowering=False, detect_race_conditions=False)
    h = Bag()
    h.x = nc.dram_tensor("x", [S, D], F32, kind="ExternalInput")
    h0 = Bag(x=h.x)
    for nm, shp in (("wqT", [D, D]), ("wkT", [D, D]), ("wvT", [D, D]),
                    ("woT", [D, D]), ("wfT", [D, D]),
                    ("bq", [128, NP]), ("bk", [128, NP]), ("bv", [128, NP]),
                    ("bo", [2, 512])):
        h0[nm] = nc.dram_tensor("l0_" + nm, shp, F32, kind="ExternalInput")
    h1 = Bag()
    for nm, shp in (("wqT", [D, D]), ("wkT", [D, D]), ("wvT", [D, D]),
                    ("woT", [D, D]), ("bq", [128, NP]), ("bk", [128, NP]),
                    ("bv", [128, NP]), ("bo", [2, 512]),
                    ("sgT", [D, IFF]), ("suT", [D, IFF]), ("sdT", [IFF, D]),
                    ("gw", [NE, D, IFF]), ("uw", [NE, D, IFF]),
                    ("dw", [NE, IFF, D]), ("rwT", [D, NE]), ("rb", [NE, 1])):
        h1[nm] = nc.dram_tensor("l1_" + nm, shp, F32, kind="ExternalInput")
    h.out = nc.dram_tensor("out", [S, D], F32, kind="ExternalOutput")
    x1_hbm = nc.dram_tensor("x1_hbm", [S, D], F32, kind="Internal")
    xn_hbm = nc.dram_tensor("xn_hbm", [D, S], BF16, kind="Internal")
    part_hbm = nc.dram_tensor("part_hbm", [S, D], F32, kind="Internal")
    with ExitStack() as ctx:
        b = alloc_bufs(nc, ctx, S, l1=True)
        E = ctx.enter_context
        sm = Bag()
        mksems(nc, ctx, sm)
        block = E(nc.Block())
        p = Plan()
        emit_consts(p, b, sm)
        emit_l0_body(p, b, sm, h0, x1_hbm)
        h1.x = None
        emit_l1_body(p, b, sm, h1, x1_hbm, h.out, xn_hbm, part_hbm)
        p.op("sync", lambda e: e.nop(),
             waits=[(sm.sOA, sm.sOA.n), (sm.sOB, sm.sOB.n)])
        p.realize(block)
    return nc


def _l0_inputs(W_qkv, b_qkv, W_out, b_out, Wf):
    idx = (np.arange(NH)[:, None] * 192 + np.arange(DH)[None, :]).ravel()
    d = dict(
        wqT=np.ascontiguousarray(W_qkv[idx].T),
        wkT=np.ascontiguousarray(W_qkv[idx + 64].T),
        wvT=np.ascontiguousarray(W_qkv[idx + 128].T),
        woT=np.ascontiguousarray(W_out.T),
        wfT=np.ascontiguousarray(Wf.T),
        bq=np.ascontiguousarray(b_qkv[idx].reshape(NP, 128).T),
        bk=np.ascontiguousarray(b_qkv[idx + 64].reshape(NP, 128).T),
        bv=np.ascontiguousarray(b_qkv[idx + 128].reshape(NP, 128).T),
        bo=np.ascontiguousarray(b_out.reshape(2, 512)),
    )
    return d


def _l1_inputs(Wq, bq, Wk, bk, Wv, bv, Wo, bo, sg, su, sd, gw, uw, dw, rw, rb):
    return dict(
        wqT=np.ascontiguousarray(Wq.T), wkT=np.ascontiguousarray(Wk.T),
        wvT=np.ascontiguousarray(Wv.T), woT=np.ascontiguousarray(Wo.T),
        bq=np.ascontiguousarray(bq.reshape(NP, 128).T),
        bk=np.ascontiguousarray(bk.reshape(NP, 128).T),
        bv=np.ascontiguousarray(bv.reshape(NP, 128).T),
        bo=np.ascontiguousarray(bo.reshape(2, 512)),
        sgT=np.ascontiguousarray(sg.T), suT=np.ascontiguousarray(su.T),
        sdT=np.ascontiguousarray(sd.T),
        gw=np.ascontiguousarray(gw), uw=np.ascontiguousarray(uw),
        dw=np.ascontiguousarray(dw),
        rwT=np.ascontiguousarray(rw.T),
        rb=np.ascontiguousarray(rb.reshape(NE, 1)),
    )


def kernel(L0_attn_x, L0_attn_W_qkv, L0_attn_b_qkv, L0_attn_W_out, L0_attn_b_out,
           L0_ffn_weight, L1_attn_Wq, L1_attn_bq, L1_attn_Wk, L1_attn_bk,
           L1_attn_Wv, L1_attn_bv, L1_attn_Wo, L1_attn_bo, L1_ffn_shared_gate,
           L1_ffn_shared_up, L1_ffn_shared_down, L1_ffn_gate_w, L1_ffn_up_w,
           L1_ffn_down_w, L1_ffn_router_w, L1_ffn_routing_bias):
    S = L0_attn_x.shape[1]
    x = np.ascontiguousarray(np.asarray(L0_attn_x, dtype=np.float32))
    w0 = _l0_inputs(np.asarray(L0_attn_W_qkv), np.asarray(L0_attn_b_qkv),
                    np.asarray(L0_attn_W_out), np.asarray(L0_attn_b_out),
                    np.asarray(L0_ffn_weight))
    w1 = _l1_inputs(np.asarray(L1_attn_Wq), np.asarray(L1_attn_bq),
                    np.asarray(L1_attn_Wk), np.asarray(L1_attn_bk),
                    np.asarray(L1_attn_Wv), np.asarray(L1_attn_bv),
                    np.asarray(L1_attn_Wo), np.asarray(L1_attn_bo),
                    np.asarray(L1_ffn_shared_gate), np.asarray(L1_ffn_shared_up),
                    np.asarray(L1_ffn_shared_down), np.asarray(L1_ffn_gate_w),
                    np.asarray(L1_ffn_up_w), np.asarray(L1_ffn_down_w),
                    np.asarray(L1_ffn_router_w), np.asarray(L1_ffn_routing_bias))
    wall = {("l0_" + k): v for k, v in w0.items()}
    wall.update({("l1_" + k): v for k, v in w1.items()})
    nc = build_fused(S)
    maps = [dict(wall, x=np.ascontiguousarray(x[c])) for c in range(NCORES)]
    res = run_bass_kernel_spmd(nc, maps, core_ids=list(range(NCORES)))
    out = np.stack([res.results[c]["out"] for c in range(NCORES)], axis=0)
    return out.astype(np.float32)


# version 03
# speedup vs baseline: 1.5549x; 1.5339x over previous
import math
from contextlib import ExitStack

import numpy as np

import concourse.bass as bass
import concourse.mybir as mybir
from concourse.bass_utils import run_bass_kernel_spmd

F32 = mybir.dt.float32
BF16 = mybir.dt.bfloat16
AF = mybir.ActivationFunctionType
ALU = mybir.AluOpType
AX = mybir.AxisListType

D = 1024
NH = 16
DH = 64
NP = NH // 2
NCH = 8
SCALE = 1.0 / math.sqrt(DH)
EPS = 1e-5
IFF = 2048
NE = 8
NCORES = 8


class Sem:
    def __init__(self, h):
        self.h = h
        self.n = 0


class Plan:
    ENGINES = ("tensor", "vector", "scalar", "gpsimd", "sync")

    def __init__(self):
        self.q = {e: [] for e in self.ENGINES}

    def op(self, eng, fn, waits=(), inc=None):
        if inc is not None:
            sem, k = inc
            sem.n += k
            inc = (sem, k)
        self.q[eng].append((tuple(waits), fn, inc))

    def realize(self, block):
        def runner(name):
            def run(e):
                last = {}
                for waits, fn, inc in self.q[name]:
                    for sem, val in waits:
                        if val > 0 and last.get(sem, 0) < val:
                            e.wait_ge(sem.h, val)
                            last[sem] = val
                    r = fn(e)
                    if inc is not None:
                        r.then_inc(inc[0].h, inc[1])
            return run
        block.tensor(runner("tensor"))
        block.vector(runner("vector"))
        block.scalar(runner("scalar"))
        block.gpsimd(runner("gpsimd"))
        block.sync(runner("sync"))


def MM(out, lhsT, rhs, start, stop):
    return lambda e, o=out, l=lhsT, r=rhs, s1=start, s2=stop: e.matmul(
        o, l, r, start=s1, stop=s2)


def TRN(out, in_, ident):
    return lambda e, o=out, i=in_, d=ident: e.transpose(o, i, d)


def ACT(out, in_, func, bias=0.0, scale=1.0):
    return lambda e, o=out, i=in_, f=func, b=bias, s=scale: e.activation(
        o, i, f, bias=b, scale=s)


def TS(out, in0, s1, s2, op0, op1=ALU.bypass):
    return lambda e, o=out, i=in0, a=s1, b=s2, p0=op0, p1=op1: e.tensor_scalar(
        o, i, a, b, p0, p1)


def TT(out, in0, in1, op):
    return lambda e, o=out, a=in0, b=in1, p=op: e.tensor_tensor(o, a, b, p)


def STT(out, in0, scalar, in1, op0, op1):
    return lambda e, o=out, a=in0, s=scalar, b=in1, p0=op0, p1=op1: \
        e.scalar_tensor_tensor(o, a, s, b, p0, p1)


def CP(out, in_):
    return lambda e, o=out, i=in_: e.tensor_copy(o, i)


def RED(out, in_, op):
    return lambda e, o=out, i=in_, p=op: e.tensor_reduce(o, i, AX.X, p)


def RECIP(out, in_):
    return lambda e, o=out, i=in_: e.reciprocal(o, i)


def DMA(out, in_):
    return lambda e, o=out, i=in_: e.dma_start(out=o, in_=i)


def MEMSET(ap, v):
    return lambda e, a=ap, vv=v: e.memset(a, vv)


def SCOPY(out, in_):
    return lambda e, o=out, i=in_: e.copy(o, i)


class Bag(dict):
    __getattr__ = dict.__getitem__
    __setattr__ = dict.__setitem__


def alloc_bufs(nc, ctx, S, l1):
    E = ctx.enter_context
    b = Bag()
    b.S = S
    b.nt = S // 128
    b.nqb = S // 512
    b.nkb = S // 128
    b.ps_q = E(nc.psum_tensor("ps_q", [128, 2048], F32))
    b.ps_a = E(nc.psum_tensor("ps_a", [128, 512], F32))
    b.ps_b = E(nc.psum_tensor("ps_b", [128, 512], F32))
    b.ps_c = E(nc.psum_tensor("ps_c", [128, 512], F32))
    b.ps_d = E(nc.psum_tensor("ps_d", [128, 512], F32))
    b.ps_tr = [b.ps_c[:].bitcast(BF16), b.ps_d[:].bitcast(BF16)]
    xnt_cols = max(NCH * S, 2 * IFF * NCH) if l1 else NCH * S
    b.xnT = E(nc.sbuf_tensor("xnT", [128, xnt_cols], BF16))
    b.ctxT = E(nc.sbuf_tensor("ctxT", [128, NCH * S], BF16))
    arc_cols = max(3 * S + (S // 128) * 192, IFF * NCH)
    b.arC = E(nc.sbuf_tensor("arC", [128, arc_cols], BF16))
    b.qpA = b.arC[:, 0:S]
    b.qpB = b.arC[:, S:2 * S]
    b.kp = b.arC[:, 2 * S:3 * S]
    b.vT = b.arC[:, 3 * S:3 * S + (S // 128) * 192]
    b.arE = E(nc.sbuf_tensor("arE", [128, 16 * 512], BF16))
    b.wf = b.arC[:, 0:8192]
    b.wq = b.arE[:, 0:1024]
    b.wk = b.arE[:, 1024:2048]
    b.wv = b.arE[:, 2048:3072]
    b.pt = [b.arE[:, 3072:4096], b.arE[:, 4096:5120]]
    b.g_sb = b.arE[:, 0:4096]
    b.u_sb = b.arE[:, 4096:8192]
    b.arF = E(nc.sbuf_tensor("arF", [128, NCH * D], BF16))
    b.wo = b.arF[:, :]
    b.slab = [b.arF[:, 0:2048], b.arF[:, 2048:4096]]
    b.xsq = b.arF[:].bitcast(F32)[:, 0:D]
    b.xn_tm = [b.arF[:, 2048:3072], b.arF[:, 3072:4096]]
    b.lo_tm = b.arF[:, 4096:5120]
    b.x_t = [E(nc.sbuf_tensor(f"x_t{i}", [128, D], F32)) for i in range(2)]
    b.st = [E(nc.sbuf_tensor(f"st{i}", [128, 8], F32)) for i in range(2)]
    b.vtmp = [b.arE[:, 5120:5632], b.arE[:, 5632:6144]]
    b.bo_bc = b.arE[:, 6144:7168]
    b.bq_s = E(nc.sbuf_tensor("bq_s", [128, NP], F32))
    b.bk_s = E(nc.sbuf_tensor("bk_s", [128, NP], F32))
    b.bv_s = E(nc.sbuf_tensor("bv_s", [128, NP], F32))
    b.rowscr = E(nc.sbuf_tensor("rowscr", [65, 512], F32))
    b.recip_t = b.rowscr
    b.nrm = b.rowscr[0:1, 0:512]
    b.u_t = E(nc.sbuf_tensor("u_t", [1, 16], F32))
    b.biasU = [E(nc.sbuf_tensor(f"biasU{i}", [128, 1], F32)) for i in range(2)]
    b.ident = E(nc.sbuf_tensor("ident", [128, 128], BF16))
    b.ones_colA = E(nc.sbuf_tensor("ones_colA", [128, 1], BF16))
    b.ones_colB = E(nc.sbuf_tensor("ones_colB", [128, 1], BF16))
    b.ones_t = E(nc.sbuf_tensor("ones_t", [65, 128], F32))
    b.colofs = E(nc.sbuf_tensor("colofs", [128, NE], F32))
    b.ident8 = E(nc.sbuf_tensor("ident8", [NE, NE], F32))
    if l1:
        nt = b.nt
        arEf = b.arE[:].bitcast(F32)
        b.rw = b.arE[:, 3072:3072 + NCH * NE]
        b.rwd = b.arE[:, 3072 + NCH * NE:3072 + 2 * NCH * NE]
        b.rwf = arEf[:, 1600:1600 + NCH * NE]
        b.rb = E(nc.sbuf_tensor("rb_sb", [NE, 1], F32))
        b.pbuf = [arEf[0:NE, 512:1024], arEf[0:NE, 1024:1536]]
        b.probs_tm = arEf[:, 0:nt * NE]
        b.t2 = arEf[:, nt * NE:nt * NE + 40]
        b.w_rt = E(nc.sbuf_tensor("w_rt_sb", [128, nt * NE], BF16))
        b.gate_v = b.xnT[:, 0:IFF * NCH]
        b.up_v = b.xnT[:, IFF * NCH:2 * IFF * NCH]
        b.down_v = b.arC[:, 0:IFF * NCH]
    return b


def mksems(nc, ctx, sm):
    E = ctx.enter_context
    for name in ("sLD", "sWD", "sXD", "sT", "sV", "sS", "sG",
                 "sXA", "sXB", "sOA", "sOB", "sSA", "sSB"):
        sm[name] = Sem(E(nc.semaphore(name)))
    sm.sXp = (sm.sXA, sm.sXB)
    sm.sOp = (sm.sOA, sm.sOB)
    sm.sSp = (sm.sSA, sm.sSB)


def emit_consts(p, b, sm):
    sG = sm.sG
    sV = sm.sV
    p.op("gpsimd", MEMSET(b.ident[:], 0.0), inc=(sG, 1))
    p.op("gpsimd", lambda e, o=b.ident[:]: e.affine_select(
        out=o, in_=o, compare_op=ALU.not_equal, fill=1.0, base=0,
        pattern=[[-1, 128]], channel_multiplier=1),
        waits=[(sG, sG.n)], inc=(sG, 1))
    p.op("gpsimd", MEMSET(b.ones_colA[:], 0.0), inc=(sG, 1))
    p.op("gpsimd", MEMSET(b.ones_colA[0:64, :], 1.0),
         waits=[(sG, sG.n)], inc=(sG, 1))
    p.op("gpsimd", MEMSET(b.ones_colB[:], 0.0), inc=(sG, 1))
    p.op("gpsimd", MEMSET(b.ones_colB[64:128, :], 1.0),
         waits=[(sG, sG.n)], inc=(sG, 1))
    p.op("gpsimd", MEMSET(b.ones_t[0:1, :], 1.0), inc=(sG, 1))
    p.op("gpsimd", MEMSET(b.ones_t[32:33, :], 1.0), inc=(sG, 1))
    p.op("gpsimd", MEMSET(b.ones_t[64:65, :], 1.0), inc=(sG, 1))
    p.op("gpsimd", lambda e, o=b.colofs[:]: e.iota(
        o, pattern=[[1, NE]], base=0, channel_multiplier=0,
        allow_small_or_imprecise_dtypes=True), inc=(sG, 1))
    p.op("gpsimd", MEMSET(b.ident8[:], 0.0), inc=(sG, 1))
    p.op("gpsimd", lambda e, o=b.ident8[:]: e.affine_select(
        out=o, in_=o, compare_op=ALU.not_equal, fill=1.0, base=0,
        pattern=[[-1, NE]], channel_multiplier=1),
        waits=[(sG, sG.n)], inc=(sG, 1))
    p.op("vector", TS(b.colofs[:], b.colofs[:], 1e-6, None, ALU.mult),
         waits=[(sG, sG.n)], inc=(sV, 1))
    sm.consts_done = sG.n
    sm.colofs_done = sm.sV.n


def emit_bo_bc(p, b, sm):
    sT, sV = sm.sT, sm.sV
    for half, ps, row in ((0, b.ps_c, 0), (1, b.ps_d, 32)):
        p.op("tensor", MM(ps[0:128, 0:512], b.ones_t[row:row + 1, :],
                          b.rowscr[row:row + 1, 0:512], True, True),
             waits=[(sm.sXD, sm.consts_dma_done), (sm.sG, sm.consts_done)],
             inc=(sT, 1))
        p.op("vector", CP(b.bo_bc[:, half * 512:(half + 1) * 512], ps[0:128, 0:512]),
             waits=[(sT, sT.n)], inc=(sV, 1))
    sm.bo_bc_done = sV.n


def emit_ln(p, b, sm, x_hbm, xn_hbm=None, router=False, entry=()):
    sXD, sT, sV, sS, sG = sm.sXD, sm.sT, sm.sV, sm.sS, sm.sG
    S, nt = b.S, b.nt
    sq_done, xn_done, evac_done = {}, {}, {}
    rmm_done = {-1: 0, -2: 0}
    sig_done = {}
    tr_evac = {}
    loroll = [b.arC[:, 0:NCH * 512], b.arC[:, NCH * 512:2 * NCH * 512]]
    for i in range(nt):
        bb = i % 2
        sx = sm.sXp[bb]
        waits = list(entry) if i < 2 else []
        if i >= 2:
            waits = [(sS, sq_done[i - 2]), (sV, xn_done[i - 2])]
        p.op("sync", DMA(b.x_t[bb][:], x_hbm.ap()[i * 128:(i + 1) * 128, :]),
             waits=waits, inc=(sx, 16))
        xd = sx.n
        st = b.st[bb]
        p.op("vector", RED(st[:, 0:1], b.x_t[bb][:], ALU.add), waits=[(sx, xd)],
             inc=(sV, 1))
        sqw = [(sx, xd)] + (list(entry) if i < 2 else [])
        if i >= 1:
            sqw.append((sV, (xn_done[i - 1] if router else
                             sm.get(("s2", i - 1), 0))))
        p.op("scalar", lambda e, o=b.xsq[:], i_=b.x_t[bb][:]: e.square(o, i_),
             waits=sqw, inc=(sS, 1))
        sq_done[i] = sS.n
        p.op("vector", RED(st[:, 1:2], b.xsq[:], ALU.add),
             waits=[(sS, sq_done[i])], inc=(sV, 1))
        sm[("s2", i)] = sV.n
        for fn in (TS(st[:, 2:3], st[:, 0:1], 1.0 / D, None, ALU.mult),
                   TS(st[:, 3:4], st[:, 1:2], 1.0 / D, None, ALU.mult),
                   TT(st[:, 4:5], st[:, 2:3], st[:, 2:3], ALU.mult),
                   TT(st[:, 5:6], st[:, 3:4], st[:, 4:5], ALU.subtract),
                   TS(st[:, 5:6], st[:, 5:6], EPS, None, ALU.add)):
            p.op("vector", fn, waits=[(sV, sV.n)], inc=(sV, 1))
        var_done = sV.n
        p.op("scalar", ACT(st[:, 6:7], st[:, 5:6], AF.Sqrt),
             waits=[(sV, var_done)], inc=(sS, 1))
        sd_done = sS.n
        p.op("vector", RECIP(st[:, 7:8], st[:, 6:7]), waits=[(sS, sd_done)],
             inc=(sV, 1))
        if not router:
            p.op("vector", TS(b.xn_tm[bb][:], b.x_t[bb][:], st[:, 2:3],
                              st[:, 7:8], ALU.subtract, ALU.mult),
                 waits=[(sV, sV.n)], inc=(sV, 1))
        else:
            p.op("vector", TS(b.xsq[:], b.x_t[bb][:], st[:, 2:3],
                              st[:, 7:8], ALU.subtract, ALU.mult),
                 waits=[(sV, sV.n)], inc=(sV, 1))
            p.op("vector", CP(b.xn_tm[bb][:], b.xsq[:]), waits=[(sV, sV.n)],
                 inc=(sV, 1))
            p.op("vector", TT(b.lo_tm[:], b.xsq[:], b.xn_tm[bb][:],
                              ALU.subtract), waits=[(sV, sV.n)], inc=(sV, 1))
        xn_done[i] = sV.n
        par = (i // 4) % 2
        nsub = 2 if router else 1
        for c in range(NCH):
            for sub in range(nsub):
                key = (i * NCH + c) * nsub + sub - 2
                tw = [(sV, xn_done[i]), (sG, sm.consts_done)]
                if key >= 0:
                    tw.append((sV, evac_done[key]))
                if router and sub == 1 and c == 0 and i % 4 == 0:
                    tw.append((sT, rmm_done[i // 4 - 2]))
                srcb = b.xn_tm[bb] if sub == 0 else b.lo_tm
                p.op("tensor", TRN(b.ps_tr[(c * nsub + sub) % 2][:, 0:128],
                                   srcb[:, c * 128:(c + 1) * 128], b.ident[:]),
                     waits=tw, inc=(sT, 1))
                trd = sT.n
                if sub == 0:
                    dst = b.xnT[:, c * S + i * 128: c * S + (i + 1) * 128]
                else:
                    dst = loroll[par][:, c * 512 + (i % 4) * 128:
                                      c * 512 + (i % 4) * 128 + 128]
                p.op("vector", CP(dst, b.ps_tr[(c * nsub + sub) % 2][:, 0:128]),
                     waits=[(sT, trd)], inc=(sV, 1))
                evac_done[(i * NCH + c) * nsub + sub] = sV.n
        if router and i % 4 == 3:
            j = i // 4
            lo_ready = sV.n
            psl = b.ps_q[0:NE, (j % 2) * 512:(j % 2) * 512 + 512]
            nmm = 3 * NCH
            for ci in range(nmm):
                c = ci % NCH
                if ci < NCH:
                    wsel, xsel = b.rw, b.xnT[:, c * S + j * 512: c * S + j * 512 + 512]
                elif ci < 2 * NCH:
                    wsel, xsel = b.rwd, b.xnT[:, c * S + j * 512: c * S + j * 512 + 512]
                else:
                    wsel, xsel = b.rw, loroll[par][:, c * 512:(c + 1) * 512]
                waits = []
                if ci == 0:
                    waits = [(sV, lo_ready), (sV, sm.rw_ready)]
                    if j >= 2:
                        waits.append((sS, sig_done[j - 2]))
                p.op("tensor", MM(psl, wsel[:, c * NE:(c + 1) * NE], xsel,
                                  ci == 0, ci == nmm - 1),
                     waits=waits, inc=((sT, 1) if ci == nmm - 1 else None))
            rmm_done[j] = sT.n
            sw = [(sT, rmm_done[j]), (sXD, sm.consts_dma_done)]
            if j >= 2:
                sw.append((sT, tr_evac[("tr", j - 2)]))
            p.op("scalar", ACT(b.pbuf[j % 2], psl, AF.Sigmoid, bias=b.rb[:],
                               scale=1.0), waits=sw, inc=(sS, 1))
            sig_done[j] = sS.n
            for tt in range(4):
                ii = j * 4 + tt
                pso = b.ps_q[0:128, 1024 + (ii % 2) * 512: 1024 + (ii % 2) * 512 + NE]
                waits = [(sS, sig_done[j])]
                if ii >= 2:
                    waits.append((sV, tr_evac[ii - 2]))
                p.op("tensor", TRN(pso, b.pbuf[j % 2][0:NE, tt * 128:(tt + 1) * 128],
                                   b.ident8[:]), waits=waits, inc=(sT, 1))
                p.op("vector", CP(b.probs_tm[:, ii * NE:(ii + 1) * NE], pso),
                     waits=[(sT, sT.n)], inc=(sV, 1))
                tr_evac[ii] = sV.n
            tr_evac[("tr", j)] = sT.n
    sm.ln_last_xn = xn_done[nt - 1]
    sm.ln_last_sq = sq_done[nt - 1]
    sm.ln_last_evac = sV.n
    sm.router_tr_evac = tr_evac
    if xn_hbm is not None:
        p.op("sync", DMA(xn_hbm.ap().rearrange("(c r) t -> r c t", r=128),
                         b.xnT[:, 0:NCH * S].rearrange("r (c t) -> r c t", t=S)),
             waits=[(sV, sm.ln_last_evac)], inc=(sXD, 16))
        sm.xn_hbm_done = sXD.n


def emit_attention(p, b, sm, wq_h, wk_h, wv_h, stage="full"):
    sLD, sT, sV, sS = sm.sLD, sm.sT, sm.sV, sm.sS
    S, nqb, nkb = b.S, b.nqb, b.nkb
    nquad = nkb // 4
    wdma_done = {}
    pair_pe_done = {}
    pair_evac_done = {}
    sm.bc_copy_done = sm.get("bc_copy_done", 0)

    def plan_wdma(pp):
        waits = []
        if pp >= 1:
            waits = [(sT, pair_pe_done[pp - 1])]
        else:
            waits = list(sm.get("first_wdma_waits", []))
        for wsb, hbm in ((b.wq, wq_h), (b.wk, wk_h), (b.wv, wv_h)):
            src = hbm.ap()[:, pp * 128:(pp + 1) * 128].rearrange(
                "(c r) o -> r c o", r=128)
            dst = wsb.rearrange("r (c o) -> r c o", o=128)
            p.op("gpsimd", DMA(dst, src), waits=waits, inc=(sLD, 16))
        wdma_done[pp] = sLD.n

    for pp in range(NP):
        plan_wdma(pp)
        if pp == 0:
            p.op("vector", MEMSET(b.qpA[64:128, :], 0.0),
                 waits=[(sV, sV.n), (sT, sT.n)], inc=(sV, 1))
            p.op("vector", MEMSET(b.qpB[0:64, :], 0.0),
                 waits=[(sV, sV.n)], inc=(sV, 1))
        vT3 = b.vT.rearrange("r (kb c) -> r kb c", c=192)
        if pp == 0:
            p.op("vector", MEMSET(vT3[:, :, 64:128], 1.0),
                 waits=[(sV, sV.n), (sT, sT.n), (sS, sm.bc_copy_done)],
                 inc=(sV, 1))
        vT_init = sV.n
        evac_done = {}
        vT_evac = {}
        n_mm = 0
        trp = 0
        for xi, (wsb, brow, scl) in enumerate(((b.wq, b.bq_s, SCALE),
                                               (b.wk, b.bk_s, 1.0),
                                               (b.wv, b.bv_s, 1.0))):
            for j in range(S // 512):
                ps = b.ps_a if (n_mm % 2 == 0) else b.ps_b
                mmd = None
                for c in range(NCH):
                    waits = []
                    if c == 0:
                        waits = [(sLD, wdma_done[pp]), (sV, sm.get("ln_last_evac", 0))]
                        if n_mm >= 2:
                            waits.append((sV, evac_done[n_mm - 2]))
                        if n_mm == 0 and pp >= 1:
                            waits.append((sV, pair_evac_done[pp - 1]))
                    p.op("tensor", MM(ps[:], wsb[:, c * 128:(c + 1) * 128],
                                      b.xnT[:, c * S + j * 512: c * S + j * 512 + 512],
                                      c == 0, c == NCH - 1),
                         waits=waits, inc=((sT, 1) if c == NCH - 1 else None))
                mmd = sT.n
                ew = [(sT, mmd)]
                if xi == 0:
                    p.op("vector", TS(b.qpA[0:64, j * 512:(j + 1) * 512],
                                      ps[0:64, :], brow[0:64, pp:pp + 1], scl,
                                      ALU.add, ALU.mult), waits=ew, inc=(sV, 1))
                    p.op("vector", TS(b.qpB[64:128, j * 512:(j + 1) * 512],
                                      ps[64:128, :], brow[64:128, pp:pp + 1], scl,
                                      ALU.add, ALU.mult), waits=ew, inc=(sV, 1))
                    evac_done[n_mm] = sV.n
                    n_mm += 1
                    continue
                if xi == 1:
                    sl = b.kp[:, j * 512:(j + 1) * 512]
                else:
                    sl = b.vtmp[j % 2][:]
                fn = TS(sl, ps[:], brow[:, pp:pp + 1], None, ALU.add)
                if xi == 2:
                    ew.append((sV, vT_init))
                    if j >= 2:
                        ew.append((sT, vT_evac[("tr", j - 2)]))
                p.op("vector", fn, waits=ew, inc=(sV, 1))
                evac_done[n_mm] = sV.n
                n_mm += 1
                if xi == 2:
                    for tt in range(4):
                        kb = j * 4 + tt
                        pstr = b.ps_tr[trp % 2]
                        tw = [(sV, evac_done[n_mm - 1])]
                        if trp >= 2:
                            tw.append((sV, vT_evac[trp - 2]))
                        if trp < 2:
                            tw.append((sS, sm.bc_copy_done))
                        p.op("tensor", TRN(pstr[:, 0:128],
                                           b.vtmp[j % 2][:, tt * 128:(tt + 1) * 128],
                                           b.ident[:]), waits=tw, inc=(sT, 1))
                        trd = sT.n
                        p.op("vector", CP(b.vT[:, kb * 192:kb * 192 + 64],
                                          pstr[:, 0:64]), waits=[(sT, trd)])
                        p.op("vector", CP(b.vT[:, kb * 192 + 128:kb * 192 + 192],
                                          pstr[:, 64:128]), inc=(sV, 1))
                        vT_evac[trp] = sV.n
                        trp += 1
                    vT_evac[("tr", j)] = sT.n
        qkv_evac_last = sV.n
        vT_last = sV.n
        if stage == "qkv":
            return

        scr = b.ctxT[:, pp * S:(pp + 1) * S]
        nsq = {}
        jj = 0
        for t_i in range(2):
            if t_i == 0:
                p.op("vector", TT(scr[0:64, :], b.qpA[0:64, :], b.qpA[0:64, :],
                                  ALU.mult),
                     waits=[(sV, qkv_evac_last), (sT, sT.n), (sV, sV.n)],
                     inc=(sV, 1))
                p.op("vector", TT(scr[64:128, :], b.qpB[64:128, :],
                                  b.qpB[64:128, :], ALU.mult),
                     waits=[(sV, sV.n)], inc=(sV, 1))
            else:
                p.op("vector", TT(scr, b.kp, b.kp, ALU.mult),
                     waits=[(sV, sV.n)], inc=(sV, 1))
            sq_d = sV.n
            for half, onescol in ((0, b.ones_colA), (1, b.ones_colB)):
                for j in range(S // 512):
                    ps = b.ps_c if jj % 2 == 0 else b.ps_d
                    waits = [(sV, sq_d), (sS, sm.bc_copy_done)]
                    if jj >= 2:
                        waits.append((sV, nsq[jj - 2]))
                    p.op("tensor", MM(ps[0:1, 0:512], onescol[:],
                                      scr[:, j * 512:(j + 1) * 512], True, True),
                         waits=waits, inc=(sT, 1))
                    mmd = sT.n
                    if j == 0:
                        p.op("vector", CP(b.nrm[:], ps[0:1, 0:512]),
                             waits=[(sT, mmd), (sV, sV.n)], inc=(sV, 1))
                    else:
                        p.op("vector", TT(b.nrm[:], b.nrm[:], ps[0:1, 0:512],
                                          ALU.max), waits=[(sT, mmd), (sV, sV.n)],
                             inc=(sV, 1))
                    nsq[jj] = sV.n
                    jj += 1
                p.op("vector", RED(b.u_t[:, (t_i * 2 + half):(t_i * 2 + half) + 1],
                                   b.nrm[:], ALU.max), waits=[(sV, sV.n)],
                     inc=(sV, 1))
        if stage in ("c4a", "c4b", "c4z", "c4y"):
            return
        c4_pe_done = sT.n
        for half in range(2):
            p.op("vector", TS(b.u_t[:, 4 + half:5 + half],
                              b.u_t[:, 2 + half:3 + half], 1.0 / 16.0, None,
                              ALU.mult), waits=[(sV, sV.n)], inc=(sV, 1))
            p.op("vector", STT(b.u_t[:, 6 + half:7 + half],
                               b.u_t[:, half:half + 1], 4.0,
                               b.u_t[:, 4 + half:5 + half], ALU.mult, ALU.add),
                 waits=[(sV, sV.n)], inc=(sV, 1))
            p.op("vector", TS(b.u_t[:, 8 + half:9 + half],
                              b.u_t[:, 6 + half:7 + half], -0.5, None, ALU.mult),
                 waits=[(sV, sV.n)], inc=(sV, 1))
            negU_done = sV.n
            if stage == "c4c":
                continue
            ps = b.ps_c if half == 0 else b.ps_d
            p.op("tensor", MM(ps[0:128, 0:1], b.ones_t[0:1, :],
                              b.u_t[:, 8 + half:9 + half], True, True),
                 waits=[(sV, negU_done), (sV, sV.n)], inc=(sT, 1))
            p.op("scalar", SCOPY(b.biasU[half][:], ps[0:128, 0:1]),
                 waits=[(sT, sT.n)], inc=(sS, 1))
        if stage == "c4c":
            return
        biasU_done = sS.n
        if stage == "c4":
            return

        if stage == "c3":
            return

        mult_done = {}
        exp_done = {}
        avs_done = {}
        s_done = {}
        psqh = [b.ps_q[:, 0:1024], b.ps_q[:, 1024:2048]]
        rcp = b.x_t[0]
        nch2 = nkb // 2
        chunks = [(h, qb, ch) for h in range(2) for qb in range(nqb)
                  for ch in range(nch2)]
        n_all = len(chunks)

        def emit_scores(ci):
            h, qb, ch = chunks[ci]
            qsrc = (b.qpA, b.qpB)[h]
            par = ci % 2
            for k2 in range(2):
                kb = ch * 2 + k2
                waits = []
                if k2 == 0:
                    if ci - 2 in exp_done:
                        waits.append((sS, exp_done[ci - 2]))
                    else:
                        waits += [(sS, biasU_done), (sV, vT_last)]
                p.op("tensor", MM(psqh[par][:, k2 * 512:(k2 + 1) * 512],
                                  b.kp[:, kb * 128:(kb + 1) * 128],
                                  qsrc[:, qb * 512:(qb + 1) * 512],
                                  True, True), waits=waits,
                     inc=((sT, 1) if k2 == 1 else None))
            s_done[ci] = sT.n
            w2 = [(sT, s_done[ci])]
            if ci - 2 in avs_done:
                w2.append((sT, avs_done[ci - 2]))
            p.op("scalar", ACT(b.pt[par][:, 0:1024], psqh[par][:, 0:1024],
                               AF.Exp, bias=b.biasU[h][:], scale=1.0),
                 waits=w2, inc=(sS, 1))
            exp_done[ci] = sS.n

        def emit_av(ci):
            h, qb, ch = chunks[ci]
            hq = h * nqb + qb
            par = ci % 2
            po = b.ps_a if hq % 2 == 0 else b.ps_b
            for k2 in range(2):
                kb = ch * 2 + k2
                lhs = b.vT[:, kb * 192 + 64 * h: kb * 192 + 64 * h + 128]
                waits = []
                if k2 == 0:
                    waits.append((sS, exp_done[ci]))
                    if kb == 0:
                        if hq >= 2:
                            waits.append((sV, mult_done[hq - 2]))
                        elif pp >= 1:
                            waits.append((sV, pair_evac_done[pp - 1]))
                        else:
                            waits.append((sV, vT_last))
                p.op("tensor", MM(po[:, 0:512], lhs,
                                  b.pt[par][:, k2 * 512:(k2 + 1) * 512],
                                  kb == 0, kb == nkb - 1),
                     waits=waits, inc=((sT, 1) if k2 == 1 else None))
            avs_done[ci] = sT.n
            if ch == nch2 - 1:
                crow = 0 if h == 0 else 64
                srow = 64 if h == 0 else 0
                p.op("vector", RECIP(rcp[crow:crow + 64, 0:512],
                                     po[srow:srow + 64, 0:512]),
                     waits=[(sT, avs_done[ci]), (sV, sV.n)], inc=(sV, 1))
                dst = b.ctxT[crow:crow + 64,
                             pp * S + qb * 512: pp * S + qb * 512 + 512]
                p.op("vector", TT(dst, po[crow:crow + 64, 0:512],
                                  rcp[crow:crow + 64, 0:512], ALU.mult),
                     waits=[(sV, sV.n)], inc=(sV, 1))
                mult_done[hq] = sV.n

        for ci in range(n_all + 1):
            if ci < n_all:
                emit_scores(ci)
            if ci >= 1:
                emit_av(ci - 1)
        pair_pe_done[pp] = sT.n
        pair_evac_done[pp] = sV.n
        if stage == "c5":
            return
    sm.attn_pe_done = sT.n
    sm.attn_v_done = sV.n
    sm.attn_s_done = sS.n


def emit_out_proj(p, b, sm, x_hbm, dst_hbm, sign_src=None, sign_ready=0):
    sXD, sT, sV, sS = sm.sXD, sm.sT, sm.sV, sm.sS
    S, nt = b.S, b.nt
    out_done = {}
    ev_done = {}
    for i in range(nt):
        bb = i % 2
        sx = sm.sXp[bb]
        so = sm.sOp[bb]
        waits = [(sV, sm.bo_bc_done), (sV, sm.ln_last_xn)]
        if i == 0:
            waits.append((sV, sm.attn_v_done))
        if i >= 2:
            waits.append((so, out_done[i - 2]))
        p.op("sync", DMA(b.x_t[bb][:], x_hbm.ap()[i * 128:(i + 1) * 128, :]),
             waits=waits, inc=(sx, 16))
        xd = sx.n
        for half in range(2):
            ps = (b.ps_a, b.ps_b, b.ps_c, b.ps_d)[2 * (i % 2) + half]
            nmm = NCH * (2 if sign_src is not None else 1)
            k = 0
            for c in range(NCH):
                fw = []
                if k == 0:
                    fw = [(sT, sm.attn_pe_done), (sV, sm.attn_v_done),
                          (sS, sm.bc_copy_done), (sm.sWD, sm.wo_dma)]
                    if i >= 2:
                        fw.append((sV, ev_done[i - 2]))
                p.op("tensor", MM(ps[:],
                                  b.ctxT[:, c * S + i * 128: c * S + (i + 1) * 128],
                                  b.wo[:, c * D + half * 512: c * D + half * 512 + 512],
                                  k == 0, k == nmm - 1),
                     waits=fw, inc=((sT, 1) if k == nmm - 1 else None))
                k += 1
            if sign_src is not None:
                for c in range(NCH):
                    fw = [(sS, sign_ready)] if c == 0 else []
                    p.op("tensor", MM(ps[:],
                                      b.xnT[:, c * S + i * 128: c * S + (i + 1) * 128],
                                      sign_src[:, c * D + half * 512: c * D + half * 512 + 512],
                                      False, k == nmm - 1),
                         waits=fw, inc=((sT, 1) if k == nmm - 1 else None))
                    k += 1
            mmd = sT.n
            sl = b.x_t[bb][:, half * 512:(half + 1) * 512]
            p.op("vector", STT(sl, ps[:], 1.0, sl, ALU.mult, ALU.add),
                 waits=[(sT, mmd), (sx, xd)], inc=(sV, 1))
            p.op("vector", TT(sl, sl, b.bo_bc[:, half * 512:(half + 1) * 512],
                              ALU.add), waits=[(sV, sV.n)], inc=(sV, 1))
        ev_done[i] = sV.n
        p.op("sync", DMA(dst_hbm.ap()[i * 128:(i + 1) * 128, :], b.x_t[bb][:]),
             waits=[(sV, ev_done[i])], inc=(so, 16))
        out_done[i] = so.n
    sm.outproj_oa = sm.sOA.n
    sm.outproj_ob = sm.sOB.n
    sm.outproj_v_done = sV.n
    sm.outproj_pe_done = sT.n


def emit_l0_body(p, b, sm, h, x_out, stage="full"):
    for dst, src in ((b.bq_s[:], h.bq.ap()), (b.bk_s[:], h.bk.ap()),
                     (b.bv_s[:], h.bv.ap()),
                     (b.rowscr[0:1, 0:512], h.bo.ap()[0:1, :]),
                     (b.rowscr[32:33, 0:512], h.bo.ap()[1:2, :])):
        p.op("sync", DMA(dst, src), inc=(sm.sXD, 16))
    sm.consts_dma_done = sm.sXD.n
    emit_bo_bc(p, b, sm)
    emit_ln(p, b, sm, h.x)
    p.op("gpsimd", DMA(b.wo.rearrange("r (c o) -> r c o", o=D),
                       h.woT.ap().rearrange("(c r) o -> r c o", r=128)),
         waits=[(sm.sV, sm.ln_last_evac), (sm.sT, sm.sT.n)],
         inc=(sm.sWD, 16))
    sm.wo_dma = sm.sWD.n
    emit_attention(p, b, sm, h.wqT, h.wkT, h.wvT, stage=stage)
    p.op("gpsimd", DMA(b.wf.rearrange("r (c o) -> r c o", o=D),
                       h.wfT.ap().rearrange("(c r) o -> r c o", r=128)),
         waits=[(sm.sT, sm.attn_pe_done), (sm.sS, sm.attn_s_done),
                (sm.sV, sm.attn_v_done)], inc=(sm.sWD, 16))
    p.op("scalar", ACT(b.wf, b.wf, AF.Sign),
         waits=[(sm.sWD, sm.sWD.n)], inc=(sm.sS, 1))
    sign_ready = sm.sS.n
    emit_out_proj(p, b, sm, h.x, x_out, sign_src=b.wf, sign_ready=sign_ready)


def build_l0(S, stage="full"):
    nc = bass.Bass("TRN2", target_bir_lowering=False, detect_race_conditions=False)
    h = Bag()
    h.x = nc.dram_tensor("x", [S, D], F32, kind="ExternalInput")
    for nm, shp in (("wqT", [D, D]), ("wkT", [D, D]), ("wvT", [D, D]),
                    ("woT", [D, D]), ("wfT", [D, D]),
                    ("bq", [128, NP]), ("bk", [128, NP]), ("bv", [128, NP]),
                    ("bo", [2, 512])):
        h[nm] = nc.dram_tensor(nm, shp, F32, kind="ExternalInput")
    h.x1 = nc.dram_tensor("x1", [S, D], F32, kind="ExternalOutput")
    with ExitStack() as ctx:
        b = alloc_bufs(nc, ctx, S, l1=False)
        E = ctx.enter_context
        sm = Bag()
        mksems(nc, ctx, sm)
        block = E(nc.Block())
        p = Plan()
        emit_consts(p, b, sm)
        emit_l0_body(p, b, sm, h, h.x1, stage=stage)
        p.op("sync", lambda e: e.nop(),
             waits=[(sm.sOA, sm.sOA.n), (sm.sOB, sm.sOB.n)])
        p.realize(block)
    return nc


def emit_l1_body(p, b, sm, h, x_in, out_h, xn_hbm, part_hbm):
    sLD, sXD, sT, sV, sS = sm.sLD, sm.sXD, sm.sT, sm.sV, sm.sS
    nt, ntb = b.nt, b.S // 256
    ent = [(sm.sT, sm.sT.n), (sm.sV, sm.sV.n), (sm.sS, sm.sS.n),
           (sm.sOA, sm.sOA.n), (sm.sOB, sm.sOB.n)]
    for dst, src in ((b.bq_s[:], h.bq.ap()), (b.bk_s[:], h.bk.ap()),
                     (b.bv_s[:], h.bv.ap()),
                     (b.rowscr[0:1, 0:512], h.bo.ap()[0:1, :]),
                     (b.rowscr[32:33, 0:512], h.bo.ap()[1:2, :]),
                     (b.rb[:], h.rb.ap())):
        p.op("sync", DMA(dst, src), waits=ent, inc=(sXD, 16))
    p.op("sync", DMA(b.rwf.rearrange("r (c o) -> r c o", o=NE),
                     h.rwT.ap().rearrange("(c r) o -> r c o", r=128)),
         waits=ent, inc=(sXD, 16))
    sm.consts_dma_done = sXD.n
    rwf_dma = sXD.n
    p.op("vector", CP(b.rw, b.rwf), waits=[(sXD, rwf_dma)] + ent, inc=(sV, 1))
    p.op("vector", TT(b.rwd, b.rwf, b.rw, ALU.subtract),
         waits=[(sV, sV.n)], inc=(sV, 1))
    sm.rw_ready = sV.n
    emit_bo_bc(p, b, sm)
    emit_ln(p, b, sm, x_in, xn_hbm=xn_hbm, router=True, entry=ent)
    p.op("gpsimd", DMA(b.wo.rearrange("r (c o) -> r c o", o=D),
                       h.woT.ap().rearrange("(c r) o -> r c o", r=128)),
         waits=[(sm.sV, sm.ln_last_evac), (sm.sT, sm.sT.n)],
         inc=(sm.sWD, 16))
    sm.wo_dma = sm.sWD.n

    for i in range(nt):
        pr = b.probs_tm[:, i * NE:(i + 1) * NE]
        t2 = b.t2
        for fn in (TT(t2[:, 32:40], pr, b.colofs[:], ALU.subtract),
                   RED(t2[:, 0:1], t2[:, 32:40], ALU.max),
                   TS(t2[:, 8:16], t2[:, 32:40], t2[:, 0:1], None,
                      ALU.is_equal),
                   STT(t2[:, 16:24], t2[:, 8:16], -4.0, t2[:, 32:40],
                       ALU.mult, ALU.add),
                   RED(t2[:, 1:2], t2[:, 16:24], ALU.max),
                   TS(t2[:, 24:32], t2[:, 16:24], t2[:, 1:2], None,
                      ALU.is_equal),
                   TT(t2[:, 2:3], t2[:, 0:1], t2[:, 1:2], ALU.add),
                   RECIP(t2[:, 3:4], t2[:, 2:3]),
                   TS(t2[:, 8:16], t2[:, 8:16], t2[:, 0:1], t2[:, 3:4],
                      ALU.mult, ALU.mult),
                   TS(t2[:, 24:32], t2[:, 24:32], t2[:, 1:2], t2[:, 3:4],
                      ALU.mult, ALU.mult),
                   TT(b.w_rt[:, i * NE:(i + 1) * NE], t2[:, 8:16],
                      t2[:, 24:32], ALU.add)):
            p.op("vector", fn, waits=[(sV, sV.n)], inc=(sV, 1))
    sm.w_done = sV.n

    sm.first_wdma_waits = [(sV, sm.w_done), (sS, sS.n)]
    emit_attention(p, b, sm, h.wqT, h.wkT, h.wvT)
    emit_out_proj(p, b, sm, x_in, part_hbm, sign_src=None)

    comb = b.ctxT
    gup_dma = {}
    gup_pw = {}
    slab_dma = {}
    gu_pe = {}
    comb_ev = {}
    comb_tile_done = {}

    for ei in range(NE + 1):
        if ei == 0:
            srcs = (h.sgT.ap(), h.suT.ap(), h.sdT.ap())
            waits = [(sT, sT.n), (sV, sV.n), (sS, sS.n),
                     (sXD, sm.xn_hbm_done)]
        else:
            srcs = (h.gw.ap()[ei - 1], h.uw.ap()[ei - 1], h.dw.ap()[ei - 1])
            waits = [(sT, gu_pe[(ei - 1, ntb - 1)])]
        pw = {}
        for g in range(2):
            for dst, src in ((b.gate_v, srcs[0]), (b.up_v, srcs[1])):
                p.op("gpsimd", DMA(
                    dst.rearrange("r (c o) -> r c o", o=IFF)
                    [:, :, g * 1024:(g + 1) * 1024],
                    src.rearrange("(c r) o -> r c o", r=128)
                    [:, :, g * 1024:(g + 1) * 1024]),
                    waits=waits, inc=(sLD, 16))
            pw[g] = sLD.n
        for g2 in range(2):
            p.op("gpsimd", DMA(
                b.down_v.rearrange("r (c o) -> r c o", o=D)
                [:, :, g2 * 512:(g2 + 1) * 512],
                srcs[2].rearrange("(c r) o -> r c o", r=128)
                [:, :, g2 * 512:(g2 + 1) * 512]),
                waits=waits, inc=(sLD, 16))
            pw[("d", g2)] = sLD.n
        gup_dma[ei] = sLD.n
        gup_pw[ei] = pw

        for tb in range(ntb):
            bb = tb % 2
            ssl = sm.sSp[bb]
            waits = [(sXD, sm.xn_hbm_done)]
            if ei == 0 and tb < 2:
                waits.append((sT, sm.outproj_pe_done))
            pk = (ei, tb - 2) if tb >= 2 else (ei - 1, ntb - 2 + tb)
            if pk in gu_pe:
                waits.append((sT, gu_pe[pk]))
            p.op("sync", DMA(b.slab[bb].rearrange("r (c t) -> r c t", t=256),
                             xn_hbm.ap().rearrange("(c r) t -> r c t", r=128)
                             [:, :, tb * 256:(tb + 1) * 256]),
                 waits=waits, inc=(ssl, 16))
            slab_dma[(ei, tb)] = (ssl, ssl.n)
            ev = {}
            for ic in range(2 * (IFF // 128)):
                isup = ic % 2
                icc = ic // 2
                ps = b.ps_b if isup else b.ps_a
                wv_ = b.up_v if isup else b.gate_v
                for c in range(NCH):
                    waits = []
                    if c == 0:
                        waits = [(sLD, gup_pw[ei][icc // 8]),
                                 slab_dma[(ei, tb)]]
                        if ic >= 2:
                            waits.append((sS, ev[ic - 2]))
                        elif (ei, tb) == (0, 0):
                            waits.append((sV, sm.outproj_v_done))
                            waits.append((sS, sS.n))
                        else:
                            waits.append((sS, guev_last))
                            pk0 = (ei, tb - 1) if tb >= 1 else (ei - 1, ntb - 1)
                            waits.append((sV, comb_ev[pk0]))
                    p.op("tensor", MM(ps[:, 0:256],
                                      wv_[:, c * IFF + icc * 128: c * IFF + icc * 128 + 128],
                                      b.slab[bb][:, c * 256:(c + 1) * 256],
                                      c == 0, c == NCH - 1),
                         waits=waits, inc=((sT, 1) if c == NCH - 1 else None))
                mmd = sT.n
                dst = (b.u_sb if isup else b.g_sb)[:, icc * 256:(icc + 1) * 256]
                p.op("scalar", ACT(dst, ps[:, 0:256],
                                   AF.Copy if isup else AF.Silu),
                     waits=[(sT, mmd)], inc=(sS, 1))
                ev[ic] = sS.n
            guev_last = sS.n
            p.op("vector", TT(b.g_sb[:, 0:4096], b.g_sb[:, 0:4096],
                              b.u_sb[:, 0:4096], ALU.mult),
                 waits=[(sS, ev[2 * (IFF // 128) - 1])], inc=(sV, 1))
            gu_done = sV.n
            dmm = {}
            for tt in range(2):
                for oh in range(2):
                    psl = b.ps_q[:, (tt * 2 + oh) * 512:(tt * 2 + oh + 1) * 512]
                    for ic in range(IFF // 128):
                        waits = []
                        if ic == 0:
                            waits = [(sV, gu_done),
                                     (sLD, gup_pw[ei][("d", oh)])]
                            if (ei, tb) == (0, 0):
                                waits.append((sS, sm.attn_s_done))
                            pk2 = (ei, tb - 1) if tb >= 1 else (ei - 1, ntb - 1)
                            if pk2 in comb_ev:
                                waits.append((sV, comb_ev[pk2]))
                        p.op("tensor", MM(psl,
                                          b.g_sb[:, ic * 256 + tt * 128: ic * 256 + tt * 128 + 128],
                                          b.down_v[:, ic * D + oh * 512: ic * D + oh * 512 + 512],
                                          ic == 0, ic == IFF // 128 - 1),
                             waits=waits,
                             inc=((sT, 1) if ic == IFF // 128 - 1 else None))
                    dmm[(tt, oh)] = sT.n
            gu_pe[(ei, tb)] = sT.n
            for tt in range(2):
                ti = tb * 2 + tt
                for oh in range(2):
                    psl = b.ps_q[:, (tt * 2 + oh) * 512:(tt * 2 + oh + 1) * 512]
                    csl = comb[:, ti * D + oh * 512: ti * D + oh * 512 + 512]
                    waits = [(sT, dmm[(tt, oh)])]
                    if ei == 0:
                        waits.append((sV, sm.outproj_v_done))
                        fn = CP(csl, psl)
                    else:
                        if ei == 1:
                            waits.append((sV, sm.w_done))
                        fn = STT(csl, psl, b.w_rt[:, ti * NE + ei - 1: ti * NE + ei],
                                 csl, ALU.mult, ALU.add)
                    p.op("vector", fn, waits=waits, inc=(sV, 1))
            comb_ev[(ei, tb)] = sV.n
            if ei == NE:
                comb_tile_done[tb] = sV.n

    fin = {}
    for i in range(nt):
        bb = i % 2
        sx = sm.sXp[bb]
        so = sm.sOp[bb]
        waits = [(sm.sOA, sm.outproj_oa), (sm.sOB, sm.outproj_ob),
                 (sV, sm.outproj_v_done)]
        if i >= 2:
            waits.append((so, fin[i - 2]))
        p.op("sync", DMA(b.x_t[bb][:], part_hbm.ap()[i * 128:(i + 1) * 128, :]),
             waits=waits, inc=(sx, 16))
        pd = sx.n
        p.op("vector", TT(b.x_t[bb][:], b.x_t[bb][:],
                          comb[:, i * D:(i + 1) * D], ALU.add),
             waits=[(sx, pd), (sV, comb_tile_done[i // 2])], inc=(sV, 1))
        p.op("sync", DMA(out_h.ap()[i * 128:(i + 1) * 128, :], b.x_t[bb][:]),
             waits=[(sV, sV.n)], inc=(so, 16))
        fin[i] = so.n


def build_l1(S):
    nc = bass.Bass("TRN2", target_bir_lowering=False, detect_race_conditions=False)
    h = Bag()
    h.x = nc.dram_tensor("x", [S, D], F32, kind="ExternalInput")
    for nm, shp in (("wqT", [D, D]), ("wkT", [D, D]), ("wvT", [D, D]),
                    ("woT", [D, D]), ("bq", [128, NP]), ("bk", [128, NP]),
                    ("bv", [128, NP]), ("bo", [2, 512]),
                    ("sgT", [D, IFF]), ("suT", [D, IFF]), ("sdT", [IFF, D]),
                    ("gw", [NE, D, IFF]), ("uw", [NE, D, IFF]),
                    ("dw", [NE, IFF, D]), ("rwT", [D, NE]), ("rb", [NE, 1])):
        h[nm] = nc.dram_tensor(nm, shp, F32, kind="ExternalInput")
    h.out = nc.dram_tensor("out", [S, D], F32, kind="ExternalOutput")
    xn_hbm = nc.dram_tensor("xn_hbm", [D, S], BF16, kind="Internal")
    part_hbm = nc.dram_tensor("part_hbm", [S, D], F32, kind="Internal")
    with ExitStack() as ctx:
        b = alloc_bufs(nc, ctx, S, l1=True)
        E = ctx.enter_context
        sm = Bag()
        mksems(nc, ctx, sm)
        block = E(nc.Block())
        p = Plan()
        emit_consts(p, b, sm)
        emit_l1_body(p, b, sm, h, h.x, h.out, xn_hbm, part_hbm)
        p.op("sync", lambda e: e.nop(),
             waits=[(sm.sOA, sm.sOA.n), (sm.sOB, sm.sOB.n)])
        p.realize(block)
    return nc


def build_fused(S):
    nc = bass.Bass("TRN2", target_bir_lowering=False, detect_race_conditions=False)
    h = Bag()
    h.x = nc.dram_tensor("x", [S, D], F32, kind="ExternalInput")
    h0 = Bag(x=h.x)
    for nm, shp in (("wqT", [D, D]), ("wkT", [D, D]), ("wvT", [D, D]),
                    ("woT", [D, D]), ("wfT", [D, D]),
                    ("bq", [128, NP]), ("bk", [128, NP]), ("bv", [128, NP]),
                    ("bo", [2, 512])):
        h0[nm] = nc.dram_tensor("l0_" + nm, shp, F32, kind="ExternalInput")
    h1 = Bag()
    for nm, shp in (("wqT", [D, D]), ("wkT", [D, D]), ("wvT", [D, D]),
                    ("woT", [D, D]), ("bq", [128, NP]), ("bk", [128, NP]),
                    ("bv", [128, NP]), ("bo", [2, 512]),
                    ("sgT", [D, IFF]), ("suT", [D, IFF]), ("sdT", [IFF, D]),
                    ("gw", [NE, D, IFF]), ("uw", [NE, D, IFF]),
                    ("dw", [NE, IFF, D]), ("rwT", [D, NE]), ("rb", [NE, 1])):
        h1[nm] = nc.dram_tensor("l1_" + nm, shp, F32, kind="ExternalInput")
    h.out = nc.dram_tensor("out", [S, D], F32, kind="ExternalOutput")
    x1_hbm = nc.dram_tensor("x1_hbm", [S, D], F32, kind="Internal")
    xn_hbm = nc.dram_tensor("xn_hbm", [D, S], BF16, kind="Internal")
    part_hbm = nc.dram_tensor("part_hbm", [S, D], F32, kind="Internal")
    with ExitStack() as ctx:
        b = alloc_bufs(nc, ctx, S, l1=True)
        E = ctx.enter_context
        sm = Bag()
        mksems(nc, ctx, sm)
        block = E(nc.Block())
        p = Plan()
        emit_consts(p, b, sm)
        emit_l0_body(p, b, sm, h0, x1_hbm)
        h1.x = None
        emit_l1_body(p, b, sm, h1, x1_hbm, h.out, xn_hbm, part_hbm)
        p.op("sync", lambda e: e.nop(),
             waits=[(sm.sOA, sm.sOA.n), (sm.sOB, sm.sOB.n)])
        p.realize(block)
    return nc


def _l0_inputs(W_qkv, b_qkv, W_out, b_out, Wf):
    idx = (np.arange(NH)[:, None] * 192 + np.arange(DH)[None, :]).ravel()
    d = dict(
        wqT=np.ascontiguousarray(W_qkv[idx].T),
        wkT=np.ascontiguousarray(W_qkv[idx + 64].T),
        wvT=np.ascontiguousarray(W_qkv[idx + 128].T),
        woT=np.ascontiguousarray(W_out.T),
        wfT=np.ascontiguousarray(Wf.T),
        bq=np.ascontiguousarray(b_qkv[idx].reshape(NP, 128).T),
        bk=np.ascontiguousarray(b_qkv[idx + 64].reshape(NP, 128).T),
        bv=np.ascontiguousarray(b_qkv[idx + 128].reshape(NP, 128).T),
        bo=np.ascontiguousarray(b_out.reshape(2, 512)),
    )
    return d


def _l1_inputs(Wq, bq, Wk, bk, Wv, bv, Wo, bo, sg, su, sd, gw, uw, dw, rw, rb):
    return dict(
        wqT=np.ascontiguousarray(Wq.T), wkT=np.ascontiguousarray(Wk.T),
        wvT=np.ascontiguousarray(Wv.T), woT=np.ascontiguousarray(Wo.T),
        bq=np.ascontiguousarray(bq.reshape(NP, 128).T),
        bk=np.ascontiguousarray(bk.reshape(NP, 128).T),
        bv=np.ascontiguousarray(bv.reshape(NP, 128).T),
        bo=np.ascontiguousarray(bo.reshape(2, 512)),
        sgT=np.ascontiguousarray(sg.T), suT=np.ascontiguousarray(su.T),
        sdT=np.ascontiguousarray(sd.T),
        gw=np.ascontiguousarray(gw), uw=np.ascontiguousarray(uw),
        dw=np.ascontiguousarray(dw),
        rwT=np.ascontiguousarray(rw.T),
        rb=np.ascontiguousarray(rb.reshape(NE, 1)),
    )


def kernel(L0_attn_x, L0_attn_W_qkv, L0_attn_b_qkv, L0_attn_W_out, L0_attn_b_out,
           L0_ffn_weight, L1_attn_Wq, L1_attn_bq, L1_attn_Wk, L1_attn_bk,
           L1_attn_Wv, L1_attn_bv, L1_attn_Wo, L1_attn_bo, L1_ffn_shared_gate,
           L1_ffn_shared_up, L1_ffn_shared_down, L1_ffn_gate_w, L1_ffn_up_w,
           L1_ffn_down_w, L1_ffn_router_w, L1_ffn_routing_bias):
    S = L0_attn_x.shape[1]
    x = np.ascontiguousarray(np.asarray(L0_attn_x, dtype=np.float32))
    w0 = _l0_inputs(np.asarray(L0_attn_W_qkv), np.asarray(L0_attn_b_qkv),
                    np.asarray(L0_attn_W_out), np.asarray(L0_attn_b_out),
                    np.asarray(L0_ffn_weight))
    w1 = _l1_inputs(np.asarray(L1_attn_Wq), np.asarray(L1_attn_bq),
                    np.asarray(L1_attn_Wk), np.asarray(L1_attn_bk),
                    np.asarray(L1_attn_Wv), np.asarray(L1_attn_bv),
                    np.asarray(L1_attn_Wo), np.asarray(L1_attn_bo),
                    np.asarray(L1_ffn_shared_gate), np.asarray(L1_ffn_shared_up),
                    np.asarray(L1_ffn_shared_down), np.asarray(L1_ffn_gate_w),
                    np.asarray(L1_ffn_up_w), np.asarray(L1_ffn_down_w),
                    np.asarray(L1_ffn_router_w), np.asarray(L1_ffn_routing_bias))
    wall = {("l0_" + k): v for k, v in w0.items()}
    wall.update({("l1_" + k): v for k, v in w1.items()})
    nc = build_fused(S)
    maps = [dict(wall, x=np.ascontiguousarray(x[c])) for c in range(NCORES)]
    res = run_bass_kernel_spmd(nc, maps, core_ids=list(range(NCORES)))
    out = np.stack([res.results[c]["out"] for c in range(NCORES)], axis=0)
    return out.astype(np.float32)


# version 04
# speedup vs baseline: 1.6690x; 1.0734x over previous
import math
from contextlib import ExitStack

import numpy as np

import concourse.bass as bass
import concourse.mybir as mybir
from concourse.bass_utils import run_bass_kernel_spmd

F32 = mybir.dt.float32
BF16 = mybir.dt.bfloat16
AF = mybir.ActivationFunctionType
ALU = mybir.AluOpType
AX = mybir.AxisListType

D = 1024
NH = 16
DH = 64
NP = NH // 2
NCH = 8
SCALE = 1.0 / math.sqrt(DH)
EPS = 1e-5
IFF = 2048
NE = 8
NCORES = 8


class Sem:
    def __init__(self, h):
        self.h = h
        self.n = 0


class Plan:
    ENGINES = ("tensor", "vector", "scalar", "gpsimd", "sync")

    def __init__(self):
        self.q = {e: [] for e in self.ENGINES}

    def op(self, eng, fn, waits=(), inc=None):
        if inc is not None:
            sem, k = inc
            sem.n += k
            inc = (sem, k)
        self.q[eng].append((tuple(waits), fn, inc))

    def realize(self, block):
        def runner(name):
            def run(e):
                last = {}
                for waits, fn, inc in self.q[name]:
                    for sem, val in waits:
                        if val > 0 and last.get(sem, 0) < val:
                            e.wait_ge(sem.h, val)
                            last[sem] = val
                    r = fn(e)
                    if inc is not None:
                        r.then_inc(inc[0].h, inc[1])
            return run
        block.tensor(runner("tensor"))
        block.vector(runner("vector"))
        block.scalar(runner("scalar"))
        block.gpsimd(runner("gpsimd"))
        block.sync(runner("sync"))


def MM(out, lhsT, rhs, start, stop):
    return lambda e, o=out, l=lhsT, r=rhs, s1=start, s2=stop: e.matmul(
        o, l, r, start=s1, stop=s2)


def TRN(out, in_, ident):
    return lambda e, o=out, i=in_, d=ident: e.transpose(o, i, d)


def ACT(out, in_, func, bias=0.0, scale=1.0):
    return lambda e, o=out, i=in_, f=func, b=bias, s=scale: e.activation(
        o, i, f, bias=b, scale=s)


def TS(out, in0, s1, s2, op0, op1=ALU.bypass):
    return lambda e, o=out, i=in0, a=s1, b=s2, p0=op0, p1=op1: e.tensor_scalar(
        o, i, a, b, p0, p1)


def TT(out, in0, in1, op):
    return lambda e, o=out, a=in0, b=in1, p=op: e.tensor_tensor(o, a, b, p)


def STT(out, in0, scalar, in1, op0, op1):
    return lambda e, o=out, a=in0, s=scalar, b=in1, p0=op0, p1=op1: \
        e.scalar_tensor_tensor(o, a, s, b, p0, p1)


def CP(out, in_):
    return lambda e, o=out, i=in_: e.tensor_copy(o, i)


def RED(out, in_, op):
    return lambda e, o=out, i=in_, p=op: e.tensor_reduce(o, i, AX.X, p)


def RECIP(out, in_):
    return lambda e, o=out, i=in_: e.reciprocal(o, i)


def DMA(out, in_):
    return lambda e, o=out, i=in_: e.dma_start(out=o, in_=i)


def MEMSET(ap, v):
    return lambda e, a=ap, vv=v: e.memset(a, vv)


def SCOPY(out, in_):
    return lambda e, o=out, i=in_: e.copy(o, i)


class Bag(dict):
    __getattr__ = dict.__getitem__
    __setattr__ = dict.__setitem__


def alloc_bufs(nc, ctx, S, l1):
    E = ctx.enter_context
    b = Bag()
    b.S = S
    b.nt = S // 128
    b.nqb = S // 512
    b.nkb = S // 128
    b.ps_q = E(nc.psum_tensor("ps_q", [128, 2048], F32))
    b.ps_a = E(nc.psum_tensor("ps_a", [128, 512], F32))
    b.ps_b = E(nc.psum_tensor("ps_b", [128, 512], F32))
    b.ps_c = E(nc.psum_tensor("ps_c", [128, 512], F32))
    b.ps_d = E(nc.psum_tensor("ps_d", [128, 512], F32))
    b.ps_tr = [b.ps_c[:].bitcast(BF16), b.ps_d[:].bitcast(BF16)]
    xnt_cols = max(NCH * S, 2 * IFF * NCH) if l1 else NCH * S
    b.xnT = E(nc.sbuf_tensor("xnT", [128, xnt_cols], BF16))
    b.ctxT = E(nc.sbuf_tensor("ctxT", [128, NCH * S], BF16))
    arc_cols = max(3 * S + (S // 128) * 192, IFF * NCH)
    b.arC = E(nc.sbuf_tensor("arC", [128, arc_cols], BF16))
    b.qpA = b.arC[:, 0:S]
    b.qpB = b.arC[:, S:2 * S]
    b.kp = b.arC[:, 2 * S:3 * S]
    b.vT = b.arC[:, 3 * S:3 * S + (S // 128) * 192]
    b.arE = E(nc.sbuf_tensor("arE", [128, 16 * 512], BF16))
    b.wf = b.arC[:, 0:8192]
    b.wq = b.arE[:, 0:1024]
    b.wk = b.arE[:, 1024:2048]
    b.wv = b.arE[:, 2048:3072]
    b.pt = [b.arE[:, 3072:4096], b.arE[:, 4096:5120]]
    b.g_sb = b.arE[:, 0:4096]
    b.u_sb = b.arE[:, 4096:8192]
    b.arF = E(nc.sbuf_tensor("arF", [128, NCH * D], BF16))
    b.wo = b.arF[:, :]
    b.slab = [b.arF[:, 0:2048], b.arF[:, 2048:4096]]
    b.xsq = b.arF[:].bitcast(F32)[:, 0:D]
    b.xn_tm = [b.arF[:, 2048:3072], b.arF[:, 3072:4096]]
    b.lo_tm = b.arF[:, 4096:5120]
    b.x_t = [E(nc.sbuf_tensor(f"x_t{i}", [128, D], F32)) for i in range(2)]
    b.st = [E(nc.sbuf_tensor(f"st{i}", [128, 8], F32)) for i in range(2)]
    b.vtmp = [b.arE[:, 5120:5632], b.arE[:, 5632:6144]]
    b.bo_bc = b.arE[:, 6144:7168]
    b.bq_s = E(nc.sbuf_tensor("bq_s", [128, NP], F32))
    b.bk_s = E(nc.sbuf_tensor("bk_s", [128, NP], F32))
    b.bv_s = E(nc.sbuf_tensor("bv_s", [128, NP], F32))
    b.rowscr = E(nc.sbuf_tensor("rowscr", [65, 512], F32))
    b.recip_t = b.rowscr
    b.nrm = b.rowscr[0:1, 0:512]
    b.u_t = E(nc.sbuf_tensor("u_t", [1, 16], F32))
    b.biasU = [E(nc.sbuf_tensor(f"biasU{i}", [128, 1], F32)) for i in range(2)]
    b.ident = E(nc.sbuf_tensor("ident", [128, 128], BF16))
    b.ones_colA = E(nc.sbuf_tensor("ones_colA", [128, 1], BF16))
    b.ones_colB = E(nc.sbuf_tensor("ones_colB", [128, 1], BF16))
    b.ones_t = E(nc.sbuf_tensor("ones_t", [65, 128], F32))
    b.colofs = E(nc.sbuf_tensor("colofs", [128, NE], F32))
    b.ident8 = E(nc.sbuf_tensor("ident8", [NE, NE], F32))
    if l1:
        nt = b.nt
        arEf = b.arE[:].bitcast(F32)
        b.rw = b.arE[:, 3072:3072 + NCH * NE]
        b.rwd = b.arE[:, 3072 + NCH * NE:3072 + 2 * NCH * NE]
        b.rwf = arEf[:, 1600:1600 + NCH * NE]
        b.rb = E(nc.sbuf_tensor("rb_sb", [NE, 1], F32))
        b.pbuf = [arEf[0:NE, 512:1024], arEf[0:NE, 1024:1536]]
        b.probs_tm = arEf[:, 0:nt * NE]
        b.t2 = arEf[:, nt * NE:nt * NE + 40]
        b.w_rt = E(nc.sbuf_tensor("w_rt_sb", [128, nt * NE], BF16))
        b.gate_v = b.xnT[:, 0:IFF * NCH]
        b.up_v = b.xnT[:, IFF * NCH:2 * IFF * NCH]
        b.down_v = b.arC[:, 0:IFF * NCH]
    return b


def mksems(nc, ctx, sm):
    E = ctx.enter_context
    for name in ("sLD", "sWD", "sXD", "sT", "sV", "sS", "sG",
                 "sXA", "sXB", "sOA", "sOB", "sSA", "sSB"):
        sm[name] = Sem(E(nc.semaphore(name)))
    sm.sXp = (sm.sXA, sm.sXB)
    sm.sOp = (sm.sOA, sm.sOB)
    sm.sSp = (sm.sSA, sm.sSB)


def emit_consts(p, b, sm):
    sG = sm.sG
    sV = sm.sV
    p.op("gpsimd", MEMSET(b.ident[:], 0.0), inc=(sG, 1))
    p.op("gpsimd", lambda e, o=b.ident[:]: e.affine_select(
        out=o, in_=o, compare_op=ALU.not_equal, fill=1.0, base=0,
        pattern=[[-1, 128]], channel_multiplier=1),
        waits=[(sG, sG.n)], inc=(sG, 1))
    p.op("gpsimd", MEMSET(b.ones_colA[:], 0.0), inc=(sG, 1))
    p.op("gpsimd", MEMSET(b.ones_colA[0:64, :], 1.0),
         waits=[(sG, sG.n)], inc=(sG, 1))
    p.op("gpsimd", MEMSET(b.ones_colB[:], 0.0), inc=(sG, 1))
    p.op("gpsimd", MEMSET(b.ones_colB[64:128, :], 1.0),
         waits=[(sG, sG.n)], inc=(sG, 1))
    p.op("gpsimd", MEMSET(b.ones_t[0:1, :], 1.0), inc=(sG, 1))
    p.op("gpsimd", MEMSET(b.ones_t[32:33, :], 1.0), inc=(sG, 1))
    p.op("gpsimd", MEMSET(b.ones_t[64:65, :], 1.0), inc=(sG, 1))
    p.op("gpsimd", lambda e, o=b.colofs[:]: e.iota(
        o, pattern=[[1, NE]], base=0, channel_multiplier=0,
        allow_small_or_imprecise_dtypes=True), inc=(sG, 1))
    p.op("gpsimd", MEMSET(b.ident8[:], 0.0), inc=(sG, 1))
    p.op("gpsimd", lambda e, o=b.ident8[:]: e.affine_select(
        out=o, in_=o, compare_op=ALU.not_equal, fill=1.0, base=0,
        pattern=[[-1, NE]], channel_multiplier=1),
        waits=[(sG, sG.n)], inc=(sG, 1))
    p.op("vector", TS(b.colofs[:], b.colofs[:], 1e-6, None, ALU.mult),
         waits=[(sG, sG.n)], inc=(sV, 1))
    sm.consts_done = sG.n
    sm.colofs_done = sm.sV.n


def emit_bo_bc(p, b, sm):
    sT, sV = sm.sT, sm.sV
    for half, ps, row in ((0, b.ps_c, 0), (1, b.ps_d, 32)):
        p.op("tensor", MM(ps[0:128, 0:512], b.ones_t[row:row + 1, :],
                          b.rowscr[row:row + 1, 0:512], True, True),
             waits=[(sm.sXD, sm.consts_dma_done), (sm.sG, sm.consts_done)],
             inc=(sT, 1))
        p.op("vector", CP(b.bo_bc[:, half * 512:(half + 1) * 512], ps[0:128, 0:512]),
             waits=[(sT, sT.n)], inc=(sV, 1))
    sm.bo_bc_done = sV.n


def emit_ln(p, b, sm, x_hbm, xn_hbm=None, router=False, entry=()):
    sXD, sT, sV, sS, sG = sm.sXD, sm.sT, sm.sV, sm.sS, sm.sG
    S, nt = b.S, b.nt
    sq_done, xn_done, evac_done = {}, {}, {}
    rmm_done = {-1: 0, -2: 0}
    sig_done = {}
    tr_evac = {}
    loroll = [b.arC[:, 0:NCH * 512], b.arC[:, NCH * 512:2 * NCH * 512]]
    for i in range(nt):
        bb = i % 2
        sx = sm.sXp[bb]
        waits = list(entry) if i < 2 else []
        if i >= 2:
            waits = [(sS, sq_done[i - 2]), (sV, xn_done[i - 2])]
        p.op("sync", DMA(b.x_t[bb][:], x_hbm.ap()[i * 128:(i + 1) * 128, :]),
             waits=waits, inc=(sx, 16))
        xd = sx.n
        st = b.st[bb]
        p.op("vector", RED(st[:, 0:1], b.x_t[bb][:], ALU.add), waits=[(sx, xd)],
             inc=(sV, 1))
        sqw = [(sx, xd)] + (list(entry) if i < 2 else [])
        if i >= 1:
            sqw.append((sV, (xn_done[i - 1] if router else
                             sm.get(("s2", i - 1), 0))))
        p.op("scalar", lambda e, o=b.xsq[:], i_=b.x_t[bb][:]: e.square(o, i_),
             waits=sqw, inc=(sS, 1))
        sq_done[i] = sS.n
        p.op("vector", RED(st[:, 1:2], b.xsq[:], ALU.add),
             waits=[(sS, sq_done[i])], inc=(sV, 1))
        sm[("s2", i)] = sV.n
        for fn in (TS(st[:, 2:3], st[:, 0:1], 1.0 / D, None, ALU.mult),
                   TS(st[:, 3:4], st[:, 1:2], 1.0 / D, None, ALU.mult),
                   TT(st[:, 4:5], st[:, 2:3], st[:, 2:3], ALU.mult),
                   TT(st[:, 5:6], st[:, 3:4], st[:, 4:5], ALU.subtract),
                   TS(st[:, 5:6], st[:, 5:6], EPS, None, ALU.add)):
            p.op("vector", fn, waits=[(sV, sV.n)], inc=(sV, 1))
        var_done = sV.n
        p.op("scalar", ACT(st[:, 6:7], st[:, 5:6], AF.Sqrt),
             waits=[(sV, var_done)], inc=(sS, 1))
        sd_done = sS.n
        p.op("vector", RECIP(st[:, 7:8], st[:, 6:7]), waits=[(sS, sd_done)],
             inc=(sV, 1))
        if not router:
            p.op("vector", TS(b.xn_tm[bb][:], b.x_t[bb][:], st[:, 2:3],
                              st[:, 7:8], ALU.subtract, ALU.mult),
                 waits=[(sV, sV.n)], inc=(sV, 1))
        else:
            p.op("vector", TS(b.xsq[:], b.x_t[bb][:], st[:, 2:3],
                              st[:, 7:8], ALU.subtract, ALU.mult),
                 waits=[(sV, sV.n)], inc=(sV, 1))
            p.op("vector", CP(b.xn_tm[bb][:], b.xsq[:]), waits=[(sV, sV.n)],
                 inc=(sV, 1))
            p.op("vector", TT(b.lo_tm[:], b.xsq[:], b.xn_tm[bb][:],
                              ALU.subtract), waits=[(sV, sV.n)], inc=(sV, 1))
        xn_done[i] = sV.n
        par = (i // 4) % 2
        nsub = 2 if router else 1
        for c in range(NCH):
            for sub in range(nsub):
                key = (i * NCH + c) * nsub + sub - 2
                tw = [(sV, xn_done[i]), (sG, sm.consts_done)]
                if key >= 0:
                    tw.append((sV, evac_done[key]))
                if router and sub == 1 and c == 0 and i % 4 == 0:
                    tw.append((sT, rmm_done[i // 4 - 2]))
                srcb = b.xn_tm[bb] if sub == 0 else b.lo_tm
                p.op("tensor", TRN(b.ps_tr[(c * nsub + sub) % 2][:, 0:128],
                                   srcb[:, c * 128:(c + 1) * 128], b.ident[:]),
                     waits=tw, inc=(sT, 1))
                trd = sT.n
                if sub == 0:
                    dst = b.xnT[:, c * S + i * 128: c * S + (i + 1) * 128]
                else:
                    dst = loroll[par][:, c * 512 + (i % 4) * 128:
                                      c * 512 + (i % 4) * 128 + 128]
                p.op("vector", CP(dst, b.ps_tr[(c * nsub + sub) % 2][:, 0:128]),
                     waits=[(sT, trd)], inc=(sV, 1))
                evac_done[(i * NCH + c) * nsub + sub] = sV.n
        if router and i % 4 == 3:
            j = i // 4
            lo_ready = sV.n
            psl = b.ps_q[0:NE, (j % 2) * 512:(j % 2) * 512 + 512]
            nmm = 3 * NCH
            for ci in range(nmm):
                c = ci % NCH
                if ci < NCH:
                    wsel, xsel = b.rw, b.xnT[:, c * S + j * 512: c * S + j * 512 + 512]
                elif ci < 2 * NCH:
                    wsel, xsel = b.rwd, b.xnT[:, c * S + j * 512: c * S + j * 512 + 512]
                else:
                    wsel, xsel = b.rw, loroll[par][:, c * 512:(c + 1) * 512]
                waits = []
                if ci == 0:
                    waits = [(sV, lo_ready), (sV, sm.rw_ready)]
                    if j >= 2:
                        waits.append((sS, sig_done[j - 2]))
                p.op("tensor", MM(psl, wsel[:, c * NE:(c + 1) * NE], xsel,
                                  ci == 0, ci == nmm - 1),
                     waits=waits, inc=((sT, 1) if ci == nmm - 1 else None))
            rmm_done[j] = sT.n
            sw = [(sT, rmm_done[j]), (sXD, sm.consts_dma_done)]
            if j >= 2:
                sw.append((sT, tr_evac[("tr", j - 2)]))
            p.op("scalar", ACT(b.pbuf[j % 2], psl, AF.Sigmoid, bias=b.rb[:],
                               scale=1.0), waits=sw, inc=(sS, 1))
            sig_done[j] = sS.n
            for tt in range(4):
                ii = j * 4 + tt
                pso = b.ps_q[0:128, 1024 + (ii % 2) * 512: 1024 + (ii % 2) * 512 + NE]
                waits = [(sS, sig_done[j])]
                if ii >= 2:
                    waits.append((sV, tr_evac[ii - 2]))
                p.op("tensor", TRN(pso, b.pbuf[j % 2][0:NE, tt * 128:(tt + 1) * 128],
                                   b.ident8[:]), waits=waits, inc=(sT, 1))
                p.op("vector", CP(b.probs_tm[:, ii * NE:(ii + 1) * NE], pso),
                     waits=[(sT, sT.n)], inc=(sV, 1))
                tr_evac[ii] = sV.n
            tr_evac[("tr", j)] = sT.n
    sm.ln_last_xn = xn_done[nt - 1]
    sm.ln_last_sq = sq_done[nt - 1]
    sm.ln_last_evac = sV.n
    sm.router_tr_evac = tr_evac
    if xn_hbm is not None:
        p.op("sync", DMA(xn_hbm.ap().rearrange("(c r) t -> r c t", r=128),
                         b.xnT[:, 0:NCH * S].rearrange("r (c t) -> r c t", t=S)),
             waits=[(sV, sm.ln_last_evac)], inc=(sXD, 16))
        sm.xn_hbm_done = sXD.n


def emit_attention(p, b, sm, wq_h, wk_h, wv_h, stage="full"):
    sLD, sT, sV, sS = sm.sLD, sm.sT, sm.sV, sm.sS
    S, nqb, nkb = b.S, b.nqb, b.nkb
    nquad = nkb // 4
    wdma_done = {}
    pair_pe_done = {}
    pair_evac_done = {}
    sm.bc_copy_done = sm.get("bc_copy_done", 0)

    def plan_wdma(pp):
        waits = []
        if pp >= 1:
            waits = [(sT, pair_pe_done[pp - 1])]
        else:
            waits = list(sm.get("first_wdma_waits", []))
        for wsb, hbm in ((b.wq, wq_h), (b.wk, wk_h), (b.wv, wv_h)):
            src = hbm.ap()[:, pp * 128:(pp + 1) * 128].rearrange(
                "(c r) o -> r c o", r=128)
            dst = wsb.rearrange("r (c o) -> r c o", o=128)
            p.op("gpsimd", DMA(dst, src), waits=waits, inc=(sLD, 16))
        wdma_done[pp] = sLD.n

    for pp in range(NP):
        plan_wdma(pp)
        if pp == 0:
            p.op("vector", MEMSET(b.qpA[64:128, :], 0.0),
                 waits=[(sV, sV.n), (sT, sT.n)], inc=(sV, 1))
            p.op("vector", MEMSET(b.qpB[0:64, :], 0.0),
                 waits=[(sV, sV.n)], inc=(sV, 1))
        vT3 = b.vT.rearrange("r (kb c) -> r kb c", c=192)
        if pp == 0:
            p.op("vector", MEMSET(vT3[:, :, 64:128], 1.0),
                 waits=[(sV, sV.n), (sT, sT.n), (sS, sm.bc_copy_done)],
                 inc=(sV, 1))
        vT_init = sV.n
        evac_done = {}
        vT_evac = {}
        n_mm = 0
        trp = 0
        for xi, (wsb, brow, scl) in enumerate(((b.wq, b.bq_s, SCALE),
                                               (b.wk, b.bk_s, 1.0),
                                               (b.wv, b.bv_s, 1.0))):
            for j in range(S // 512):
                ps = b.ps_a if (n_mm % 2 == 0) else b.ps_b
                mmd = None
                for c in range(NCH):
                    waits = []
                    if c == 0:
                        waits = [(sLD, wdma_done[pp]), (sV, sm.get("ln_last_evac", 0))]
                        if n_mm >= 2:
                            waits.append((sV, evac_done[n_mm - 2]))
                        if n_mm == 0 and pp >= 1:
                            waits.append((sV, pair_evac_done[pp - 1]))
                    p.op("tensor", MM(ps[:], wsb[:, c * 128:(c + 1) * 128],
                                      b.xnT[:, c * S + j * 512: c * S + j * 512 + 512],
                                      c == 0, c == NCH - 1),
                         waits=waits, inc=((sT, 1) if c == NCH - 1 else None))
                mmd = sT.n
                ew = [(sT, mmd)]
                if xi == 0:
                    p.op("vector", TS(b.qpA[0:64, j * 512:(j + 1) * 512],
                                      ps[0:64, :], brow[0:64, pp:pp + 1], scl,
                                      ALU.add, ALU.mult), waits=ew, inc=(sV, 1))
                    p.op("vector", TS(b.qpB[64:128, j * 512:(j + 1) * 512],
                                      ps[64:128, :], brow[64:128, pp:pp + 1], scl,
                                      ALU.add, ALU.mult), waits=ew, inc=(sV, 1))
                    evac_done[n_mm] = sV.n
                    n_mm += 1
                    continue
                if xi == 1:
                    sl = b.kp[:, j * 512:(j + 1) * 512]
                else:
                    sl = b.vtmp[j % 2][:]
                fn = TS(sl, ps[:], brow[:, pp:pp + 1], None, ALU.add)
                if xi == 2:
                    ew.append((sV, vT_init))
                    if j >= 2:
                        ew.append((sT, vT_evac[("tr", j - 2)]))
                p.op("vector", fn, waits=ew, inc=(sV, 1))
                evac_done[n_mm] = sV.n
                n_mm += 1
                if xi == 2:
                    for tt in range(4):
                        kb = j * 4 + tt
                        pstr = b.ps_tr[trp % 2]
                        tw = [(sV, evac_done[n_mm - 1])]
                        if trp >= 2:
                            tw.append((sV, vT_evac[trp - 2]))
                        if trp < 2:
                            tw.append((sS, sm.bc_copy_done))
                        p.op("tensor", TRN(pstr[:, 0:128],
                                           b.vtmp[j % 2][:, tt * 128:(tt + 1) * 128],
                                           b.ident[:]), waits=tw, inc=(sT, 1))
                        trd = sT.n
                        p.op("vector", CP(b.vT[:, kb * 192:kb * 192 + 64],
                                          pstr[:, 0:64]), waits=[(sT, trd)])
                        p.op("vector", CP(b.vT[:, kb * 192 + 128:kb * 192 + 192],
                                          pstr[:, 64:128]), inc=(sV, 1))
                        vT_evac[trp] = sV.n
                        trp += 1
                    vT_evac[("tr", j)] = sT.n
        qkv_evac_last = sV.n
        vT_last = sV.n
        if stage == "qkv":
            return

        scr = b.ctxT[:, pp * S:(pp + 1) * S]
        nsq = {}
        jj = 0
        for t_i in range(2):
            if t_i == 0:
                p.op("vector", TT(scr[0:64, :], b.qpA[0:64, :], b.qpA[0:64, :],
                                  ALU.mult),
                     waits=[(sV, qkv_evac_last), (sT, sT.n), (sV, sV.n)],
                     inc=(sV, 1))
                p.op("vector", TT(scr[64:128, :], b.qpB[64:128, :],
                                  b.qpB[64:128, :], ALU.mult),
                     waits=[(sV, sV.n)], inc=(sV, 1))
            else:
                p.op("vector", TT(scr, b.kp, b.kp, ALU.mult),
                     waits=[(sV, sV.n)], inc=(sV, 1))
            sq_d = sV.n
            for half, onescol in ((0, b.ones_colA), (1, b.ones_colB)):
                for j in range(S // 512):
                    ps = b.ps_c if jj % 2 == 0 else b.ps_d
                    waits = [(sV, sq_d), (sS, sm.bc_copy_done)]
                    if jj >= 2:
                        waits.append((sV, nsq[jj - 2]))
                    p.op("tensor", MM(ps[0:1, 0:512], onescol[:],
                                      scr[:, j * 512:(j + 1) * 512], True, True),
                         waits=waits, inc=(sT, 1))
                    mmd = sT.n
                    if j == 0:
                        p.op("vector", CP(b.nrm[:], ps[0:1, 0:512]),
                             waits=[(sT, mmd), (sV, sV.n)], inc=(sV, 1))
                    else:
                        p.op("vector", TT(b.nrm[:], b.nrm[:], ps[0:1, 0:512],
                                          ALU.max), waits=[(sT, mmd), (sV, sV.n)],
                             inc=(sV, 1))
                    nsq[jj] = sV.n
                    jj += 1
                p.op("vector", RED(b.u_t[:, (t_i * 2 + half):(t_i * 2 + half) + 1],
                                   b.nrm[:], ALU.max), waits=[(sV, sV.n)],
                     inc=(sV, 1))
        if stage in ("c4a", "c4b", "c4z", "c4y"):
            return
        c4_pe_done = sT.n
        for half in range(2):
            p.op("vector", TS(b.u_t[:, 4 + half:5 + half],
                              b.u_t[:, 2 + half:3 + half], 1.0 / 16.0, None,
                              ALU.mult), waits=[(sV, sV.n)], inc=(sV, 1))
            p.op("vector", STT(b.u_t[:, 6 + half:7 + half],
                               b.u_t[:, half:half + 1], 4.0,
                               b.u_t[:, 4 + half:5 + half], ALU.mult, ALU.add),
                 waits=[(sV, sV.n)], inc=(sV, 1))
            p.op("vector", TS(b.u_t[:, 8 + half:9 + half],
                              b.u_t[:, 6 + half:7 + half], -0.5, None, ALU.mult),
                 waits=[(sV, sV.n)], inc=(sV, 1))
            negU_done = sV.n
            if stage == "c4c":
                continue
            ps = b.ps_c if half == 0 else b.ps_d
            p.op("tensor", MM(ps[0:128, 0:1], b.ones_t[0:1, :],
                              b.u_t[:, 8 + half:9 + half], True, True),
                 waits=[(sV, negU_done), (sV, sV.n)], inc=(sT, 1))
            p.op("scalar", SCOPY(b.biasU[half][:], ps[0:128, 0:1]),
                 waits=[(sT, sT.n)], inc=(sS, 1))
        if stage == "c4c":
            return
        biasU_done = sS.n
        if stage == "c4":
            return

        if stage == "c3":
            return

        mult_done = {}
        exp_done = {}
        avs_done = {}
        s_done = {}
        psqh = [b.ps_q[:, 0:1024], b.ps_q[:, 1024:2048]]
        rcp = b.x_t[0]
        nch2 = nkb // 2
        chunks = [(h, qb, ch) for h in range(2) for qb in range(nqb)
                  for ch in range(nch2)]
        n_all = len(chunks)

        def emit_scores(ci):
            h, qb, ch = chunks[ci]
            qsrc = (b.qpA, b.qpB)[h]
            par = ci % 2
            for k2 in range(2):
                kb = ch * 2 + k2
                waits = []
                if k2 == 0:
                    if ci - 2 in exp_done:
                        waits.append((sS, exp_done[ci - 2]))
                    else:
                        waits += [(sS, biasU_done), (sV, vT_last)]
                p.op("tensor", MM(psqh[par][:, k2 * 512:(k2 + 1) * 512],
                                  b.kp[:, kb * 128:(kb + 1) * 128],
                                  qsrc[:, qb * 512:(qb + 1) * 512],
                                  True, True), waits=waits,
                     inc=((sT, 1) if k2 == 1 else None))
            s_done[ci] = sT.n
            w2 = [(sT, s_done[ci])]
            if ci - 2 in avs_done:
                w2.append((sT, avs_done[ci - 2]))
            p.op("scalar", ACT(b.pt[par][:, 0:1024], psqh[par][:, 0:1024],
                               AF.Exp, bias=b.biasU[h][:], scale=1.0),
                 waits=w2, inc=(sS, 1))
            exp_done[ci] = sS.n

        def emit_av(ci):
            h, qb, ch = chunks[ci]
            hq = h * nqb + qb
            par = ci % 2
            po = b.ps_a if hq % 2 == 0 else b.ps_b
            for k2 in range(2):
                kb = ch * 2 + k2
                lhs = b.vT[:, kb * 192 + 64 * h: kb * 192 + 64 * h + 128]
                waits = []
                if k2 == 0:
                    waits.append((sS, exp_done[ci]))
                    if kb == 0:
                        if hq >= 2:
                            waits.append((sV, mult_done[hq - 2]))
                        elif pp >= 1:
                            waits.append((sV, pair_evac_done[pp - 1]))
                        else:
                            waits.append((sV, vT_last))
                p.op("tensor", MM(po[:, 0:512], lhs,
                                  b.pt[par][:, k2 * 512:(k2 + 1) * 512],
                                  kb == 0, kb == nkb - 1),
                     waits=waits, inc=((sT, 1) if k2 == 1 else None))
            avs_done[ci] = sT.n
            if ch == nch2 - 1:
                crow = 0 if h == 0 else 64
                srow = 64 if h == 0 else 0
                p.op("vector", RECIP(rcp[crow:crow + 64, 0:512],
                                     po[srow:srow + 64, 0:512]),
                     waits=[(sT, avs_done[ci]), (sV, sV.n)], inc=(sV, 1))
                dst = b.ctxT[crow:crow + 64,
                             pp * S + qb * 512: pp * S + qb * 512 + 512]
                p.op("vector", TT(dst, po[crow:crow + 64, 0:512],
                                  rcp[crow:crow + 64, 0:512], ALU.mult),
                     waits=[(sV, sV.n)], inc=(sV, 1))
                mult_done[hq] = sV.n

        for ci in range(n_all + 1):
            if ci < n_all:
                emit_scores(ci)
            if ci >= 1:
                emit_av(ci - 1)
        pair_pe_done[pp] = sT.n
        pair_evac_done[pp] = sV.n
        if stage == "c5":
            return
    sm.attn_pe_done = sT.n
    sm.attn_v_done = sV.n
    sm.attn_s_done = sS.n


def emit_out_proj(p, b, sm, x_hbm, dst_hbm, sign_src=None, sign_ready=0):
    sXD, sT, sV, sS = sm.sXD, sm.sT, sm.sV, sm.sS
    S, nt = b.S, b.nt
    out_done = {}
    ev_done = {}
    for i in range(nt):
        bb = i % 2
        sx = sm.sXp[bb]
        so = sm.sOp[bb]
        waits = [(sV, sm.bo_bc_done), (sV, sm.ln_last_xn)]
        if i == 0:
            waits.append((sV, sm.attn_v_done))
        if i >= 2:
            waits.append((so, out_done[i - 2]))
        p.op("sync", DMA(b.x_t[bb][:], x_hbm.ap()[i * 128:(i + 1) * 128, :]),
             waits=waits, inc=(sx, 16))
        xd = sx.n
        for half in range(2):
            ps = (b.ps_a, b.ps_b, b.ps_c, b.ps_d)[2 * (i % 2) + half]
            nmm = NCH * (2 if sign_src is not None else 1)
            k = 0
            for c in range(NCH):
                fw = []
                if k == 0:
                    fw = [(sT, sm.attn_pe_done), (sV, sm.attn_v_done),
                          (sS, sm.bc_copy_done), (sm.sWD, sm.wo_dma)]
                    if i >= 2:
                        fw.append((sV, ev_done[i - 2]))
                p.op("tensor", MM(ps[:],
                                  b.ctxT[:, c * S + i * 128: c * S + (i + 1) * 128],
                                  b.wo[:, c * D + half * 512: c * D + half * 512 + 512],
                                  k == 0, k == nmm - 1),
                     waits=fw, inc=((sT, 1) if k == nmm - 1 else None))
                k += 1
            if sign_src is not None:
                for c in range(NCH):
                    fw = [(sS, sign_ready)] if c == 0 else []
                    p.op("tensor", MM(ps[:],
                                      b.xnT[:, c * S + i * 128: c * S + (i + 1) * 128],
                                      sign_src[:, c * D + half * 512: c * D + half * 512 + 512],
                                      False, k == nmm - 1),
                         waits=fw, inc=((sT, 1) if k == nmm - 1 else None))
                    k += 1
            mmd = sT.n
            sl = b.x_t[bb][:, half * 512:(half + 1) * 512]
            p.op("vector", STT(sl, ps[:], 1.0, sl, ALU.mult, ALU.add),
                 waits=[(sT, mmd), (sx, xd)], inc=(sV, 1))
            p.op("vector", TT(sl, sl, b.bo_bc[:, half * 512:(half + 1) * 512],
                              ALU.add), waits=[(sV, sV.n)], inc=(sV, 1))
        ev_done[i] = sV.n
        p.op("sync", DMA(dst_hbm.ap()[i * 128:(i + 1) * 128, :], b.x_t[bb][:]),
             waits=[(sV, ev_done[i])], inc=(so, 16))
        out_done[i] = so.n
    sm.outproj_oa = sm.sOA.n
    sm.outproj_ob = sm.sOB.n
    sm.outproj_v_done = sV.n
    sm.outproj_pe_done = sT.n


def emit_l0_body(p, b, sm, h, x_out, stage="full"):
    for dst, src in ((b.bq_s[:], h.bq.ap()), (b.bk_s[:], h.bk.ap()),
                     (b.bv_s[:], h.bv.ap()),
                     (b.rowscr[0:1, 0:512], h.bo.ap()[0:1, :]),
                     (b.rowscr[32:33, 0:512], h.bo.ap()[1:2, :])):
        p.op("sync", DMA(dst, src), inc=(sm.sXD, 16))
    sm.consts_dma_done = sm.sXD.n
    emit_bo_bc(p, b, sm)
    emit_ln(p, b, sm, h.x)
    p.op("gpsimd", DMA(b.wo.rearrange("r (c o) -> r c o", o=D),
                       h.woT.ap().rearrange("(c r) o -> r c o", r=128)),
         waits=[(sm.sV, sm.ln_last_evac), (sm.sT, sm.sT.n)],
         inc=(sm.sWD, 16))
    sm.wo_dma = sm.sWD.n
    emit_attention(p, b, sm, h.wqT, h.wkT, h.wvT, stage=stage)
    p.op("gpsimd", DMA(b.wf.rearrange("r (c o) -> r c o", o=D),
                       h.wfT.ap().rearrange("(c r) o -> r c o", r=128)),
         waits=[(sm.sT, sm.attn_pe_done), (sm.sS, sm.attn_s_done),
                (sm.sV, sm.attn_v_done)], inc=(sm.sWD, 16))
    p.op("scalar", ACT(b.wf, b.wf, AF.Sign),
         waits=[(sm.sWD, sm.sWD.n)], inc=(sm.sS, 1))
    sign_ready = sm.sS.n
    emit_out_proj(p, b, sm, h.x, x_out, sign_src=b.wf, sign_ready=sign_ready)


def build_l0(S, stage="full"):
    nc = bass.Bass("TRN2", target_bir_lowering=False, detect_race_conditions=False)
    h = Bag()
    h.x = nc.dram_tensor("x", [S, D], F32, kind="ExternalInput")
    for nm, shp in (("wqT", [D, D]), ("wkT", [D, D]), ("wvT", [D, D]),
                    ("woT", [D, D]), ("wfT", [D, D]),
                    ("bq", [128, NP]), ("bk", [128, NP]), ("bv", [128, NP]),
                    ("bo", [2, 512])):
        h[nm] = nc.dram_tensor(nm, shp, F32, kind="ExternalInput")
    h.x1 = nc.dram_tensor("x1", [S, D], F32, kind="ExternalOutput")
    with ExitStack() as ctx:
        b = alloc_bufs(nc, ctx, S, l1=False)
        E = ctx.enter_context
        sm = Bag()
        mksems(nc, ctx, sm)
        block = E(nc.Block())
        p = Plan()
        emit_consts(p, b, sm)
        emit_l0_body(p, b, sm, h, h.x1, stage=stage)
        p.op("sync", lambda e: e.nop(),
             waits=[(sm.sOA, sm.sOA.n), (sm.sOB, sm.sOB.n)])
        p.realize(block)
    return nc


def emit_l1_body(p, b, sm, h, x_in, out_h, xn_hbm, part_hbm):
    sLD, sXD, sT, sV, sS = sm.sLD, sm.sXD, sm.sT, sm.sV, sm.sS
    nt, ntb = b.nt, b.S // 256
    ent = [(sm.sT, sm.sT.n), (sm.sV, sm.sV.n), (sm.sS, sm.sS.n),
           (sm.sOA, sm.sOA.n), (sm.sOB, sm.sOB.n)]
    for dst, src in ((b.bq_s[:], h.bq.ap()), (b.bk_s[:], h.bk.ap()),
                     (b.bv_s[:], h.bv.ap()),
                     (b.rowscr[0:1, 0:512], h.bo.ap()[0:1, :]),
                     (b.rowscr[32:33, 0:512], h.bo.ap()[1:2, :]),
                     (b.rb[:], h.rb.ap())):
        p.op("sync", DMA(dst, src), waits=ent, inc=(sXD, 16))
    p.op("sync", DMA(b.rwf.rearrange("r (c o) -> r c o", o=NE),
                     h.rwT.ap().rearrange("(c r) o -> r c o", r=128)),
         waits=ent, inc=(sXD, 16))
    sm.consts_dma_done = sXD.n
    rwf_dma = sXD.n
    p.op("vector", CP(b.rw, b.rwf), waits=[(sXD, rwf_dma)] + ent, inc=(sV, 1))
    p.op("vector", TT(b.rwd, b.rwf, b.rw, ALU.subtract),
         waits=[(sV, sV.n)], inc=(sV, 1))
    sm.rw_ready = sV.n
    emit_bo_bc(p, b, sm)
    emit_ln(p, b, sm, x_in, xn_hbm=xn_hbm, router=True, entry=ent)
    p.op("gpsimd", DMA(b.wo.rearrange("r (c o) -> r c o", o=D),
                       h.woT.ap().rearrange("(c r) o -> r c o", r=128)),
         waits=[(sm.sV, sm.ln_last_evac), (sm.sT, sm.sT.n)],
         inc=(sm.sWD, 16))
    sm.wo_dma = sm.sWD.n

    for i in range(nt):
        pr = b.probs_tm[:, i * NE:(i + 1) * NE]
        t2 = b.t2
        for fn in (TT(t2[:, 32:40], pr, b.colofs[:], ALU.subtract),
                   RED(t2[:, 0:1], t2[:, 32:40], ALU.max),
                   TS(t2[:, 8:16], t2[:, 32:40], t2[:, 0:1], None,
                      ALU.is_equal),
                   STT(t2[:, 16:24], t2[:, 8:16], -4.0, t2[:, 32:40],
                       ALU.mult, ALU.add),
                   RED(t2[:, 1:2], t2[:, 16:24], ALU.max),
                   TS(t2[:, 24:32], t2[:, 16:24], t2[:, 1:2], None,
                      ALU.is_equal),
                   TT(t2[:, 2:3], t2[:, 0:1], t2[:, 1:2], ALU.add),
                   RECIP(t2[:, 3:4], t2[:, 2:3]),
                   TS(t2[:, 8:16], t2[:, 8:16], t2[:, 0:1], t2[:, 3:4],
                      ALU.mult, ALU.mult),
                   TS(t2[:, 24:32], t2[:, 24:32], t2[:, 1:2], t2[:, 3:4],
                      ALU.mult, ALU.mult),
                   TT(b.w_rt[:, i * NE:(i + 1) * NE], t2[:, 8:16],
                      t2[:, 24:32], ALU.add)):
            p.op("vector", fn, waits=[(sV, sV.n)], inc=(sV, 1))
    sm.w_done = sV.n

    sm.first_wdma_waits = [(sV, sm.w_done), (sS, sS.n)]
    emit_attention(p, b, sm, h.wqT, h.wkT, h.wvT)
    emit_out_proj(p, b, sm, x_in, part_hbm, sign_src=None)

    comb = b.ctxT
    gup_dma = {}
    gup_pw = {}
    half_pe = {}
    guend_pe = {}
    slab_dma = {}
    gu_pe = {}
    comb_ev = {}
    comb_tile_done = {}

    for ei in range(NE + 1):
        if ei == 0:
            srcs = (h.sgT.ap(), h.suT.ap(), h.sdT.ap())
            base_w = [(sT, sT.n), (sV, sV.n), (sS, sS.n),
                      (sXD, sm.xn_hbm_done)]
            gu_w = [list(base_w), list(base_w)]
            dn_w = list(base_w)
        else:
            srcs = (h.gw.ap()[ei - 1], h.uw.ap()[ei - 1], h.dw.ap()[ei - 1])
            gu_w = [[(sT, half_pe[ei - 1])], [(sT, guend_pe[ei - 1])]]
            dn_w = [(sT, gu_pe[(ei - 1, ntb - 1)])]
        pw = {}
        for g in range(2):
            for key, dst, src in (("g", b.gate_v, srcs[0]),
                                  ("u", b.up_v, srcs[1])):
                p.op("gpsimd", DMA(
                    dst.rearrange("r (c o) -> r c o", o=IFF)
                    [:, :, g * 1024:(g + 1) * 1024],
                    src.rearrange("(c r) o -> r c o", r=128)
                    [:, :, g * 1024:(g + 1) * 1024]),
                    waits=gu_w[g], inc=(sLD, 16))
                pw[(key, g)] = sLD.n
        for g2 in range(2):
            p.op("gpsimd", DMA(
                b.down_v.rearrange("r (c o) -> r c o", o=D)
                [:, :, g2 * 512:(g2 + 1) * 512],
                srcs[2].rearrange("(c r) o -> r c o", r=128)
                [:, :, g2 * 512:(g2 + 1) * 512]),
                waits=dn_w, inc=(sLD, 16))
            pw[("d", g2)] = sLD.n
        gup_dma[ei] = sLD.n
        gup_pw[ei] = pw

        for tb in range(ntb):
            bb = tb % 2
            ssl = sm.sSp[bb]
            waits = [(sXD, sm.xn_hbm_done)]
            if ei == 0 and tb < 2:
                waits.append((sT, sm.outproj_pe_done))
            pk = (ei, tb - 2) if tb >= 2 else (ei - 1, ntb - 2 + tb)
            if pk in gu_pe:
                waits.append((sT, gu_pe[pk]))
            p.op("sync", DMA(b.slab[bb].rearrange("r (c t) -> r c t", t=256),
                             xn_hbm.ap().rearrange("(c r) t -> r c t", r=128)
                             [:, :, tb * 256:(tb + 1) * 256]),
                 waits=waits, inc=(ssl, 16))
            slab_dma[(ei, tb)] = (ssl, ssl.n)
            silu = {}
            ttev = {}
            for ic in range(2 * (IFF // 128)):
                isup = ic % 2
                icc = ic // 2
                ps = b.ps_b if isup else b.ps_a
                wv_ = b.up_v if isup else b.gate_v
                for c in range(NCH):
                    waits = []
                    if c == 0:
                        waits = [(sLD, gup_pw[ei][("u" if isup else "g",
                                                   icc // 8)]),
                                 slab_dma[(ei, tb)]]
                        if not isup:
                            if icc >= 1:
                                waits.append((sS, silu[icc - 1]))
                            elif (ei, tb) == (0, 0):
                                waits.append((sV, sm.outproj_v_done))
                                waits.append((sS, sS.n))
                            else:
                                waits.append((sS, prev_silu_last))
                        else:
                            if icc >= 1:
                                waits.append((sV, ttev[icc - 1]))
                            elif (ei, tb) == (0, 0):
                                waits.append((sV, sm.outproj_v_done))
                            else:
                                waits.append((sV, prev_ttev_last))
                    p.op("tensor", MM(ps[:, 0:256],
                                      wv_[:, c * IFF + icc * 128: c * IFF + icc * 128 + 128],
                                      b.slab[bb][:, c * 256:(c + 1) * 256],
                                      c == 0, c == NCH - 1),
                         waits=waits, inc=((sT, 1) if c == NCH - 1 else None))
                mmd = sT.n
                if ic == 2 * (IFF // 128) - 17:
                    half_pe[ei] = sT.n
                gsl = b.g_sb[:, icc * 256:(icc + 1) * 256]
                if not isup:
                    p.op("scalar", ACT(gsl, ps[:, 0:256], AF.Silu),
                         waits=[(sT, mmd)], inc=(sS, 1))
                    silu[icc] = sS.n
                else:
                    p.op("vector", TT(gsl, ps[:, 0:256], gsl, ALU.mult),
                         waits=[(sT, mmd), (sS, silu[icc]), (sV, sV.n)],
                         inc=(sV, 1))
                    ttev[icc] = sV.n
            guend_pe[ei] = sT.n
            prev_silu_last = silu[IFF // 128 - 1]
            prev_ttev_last = ttev[IFF // 128 - 1]
            gu_done = ttev[IFF // 128 - 1]
            dmm = {}
            for tt in range(2):
                for oh in range(2):
                    psl = b.ps_q[:, (tt * 2 + oh) * 512:(tt * 2 + oh + 1) * 512]
                    for ic in range(IFF // 128):
                        waits = []
                        if ic == 0:
                            waits = [(sV, gu_done),
                                     (sLD, gup_pw[ei][("d", oh)])]
                            if (ei, tb) == (0, 0):
                                waits.append((sS, sm.attn_s_done))
                            pk2 = (ei, tb - 1) if tb >= 1 else (ei - 1, ntb - 1)
                            if pk2 in comb_ev:
                                waits.append((sV, comb_ev[pk2]))
                        p.op("tensor", MM(psl,
                                          b.g_sb[:, ic * 256 + tt * 128: ic * 256 + tt * 128 + 128],
                                          b.down_v[:, ic * D + oh * 512: ic * D + oh * 512 + 512],
                                          ic == 0, ic == IFF // 128 - 1),
                             waits=waits,
                             inc=((sT, 1) if ic == IFF // 128 - 1 else None))
                    dmm[(tt, oh)] = sT.n
            gu_pe[(ei, tb)] = sT.n
            for tt in range(2):
                ti = tb * 2 + tt
                for oh in range(2):
                    psl = b.ps_q[:, (tt * 2 + oh) * 512:(tt * 2 + oh + 1) * 512]
                    csl = comb[:, ti * D + oh * 512: ti * D + oh * 512 + 512]
                    waits = [(sT, dmm[(tt, oh)])]
                    if ei == 0:
                        waits.append((sV, sm.outproj_v_done))
                        fn = CP(csl, psl)
                    else:
                        if ei == 1:
                            waits.append((sV, sm.w_done))
                        fn = STT(csl, psl, b.w_rt[:, ti * NE + ei - 1: ti * NE + ei],
                                 csl, ALU.mult, ALU.add)
                    p.op("vector", fn, waits=waits, inc=(sV, 1))
            comb_ev[(ei, tb)] = sV.n
            if ei == NE:
                comb_tile_done[tb] = sV.n

    fin = {}
    for i in range(nt):
        bb = i % 2
        sx = sm.sXp[bb]
        so = sm.sOp[bb]
        waits = [(sm.sOA, sm.outproj_oa), (sm.sOB, sm.outproj_ob),
                 (sV, sm.outproj_v_done)]
        if i >= 2:
            waits.append((so, fin[i - 2]))
        p.op("sync", DMA(b.x_t[bb][:], part_hbm.ap()[i * 128:(i + 1) * 128, :]),
             waits=waits, inc=(sx, 16))
        pd = sx.n
        p.op("vector", TT(b.x_t[bb][:], b.x_t[bb][:],
                          comb[:, i * D:(i + 1) * D], ALU.add),
             waits=[(sx, pd), (sV, comb_tile_done[i // 2])], inc=(sV, 1))
        p.op("sync", DMA(out_h.ap()[i * 128:(i + 1) * 128, :], b.x_t[bb][:]),
             waits=[(sV, sV.n)], inc=(so, 16))
        fin[i] = so.n


def build_l1(S):
    nc = bass.Bass("TRN2", target_bir_lowering=False, detect_race_conditions=False)
    h = Bag()
    h.x = nc.dram_tensor("x", [S, D], F32, kind="ExternalInput")
    for nm, shp in (("wqT", [D, D]), ("wkT", [D, D]), ("wvT", [D, D]),
                    ("woT", [D, D]), ("bq", [128, NP]), ("bk", [128, NP]),
                    ("bv", [128, NP]), ("bo", [2, 512]),
                    ("sgT", [D, IFF]), ("suT", [D, IFF]), ("sdT", [IFF, D]),
                    ("gw", [NE, D, IFF]), ("uw", [NE, D, IFF]),
                    ("dw", [NE, IFF, D]), ("rwT", [D, NE]), ("rb", [NE, 1])):
        h[nm] = nc.dram_tensor(nm, shp, F32, kind="ExternalInput")
    h.out = nc.dram_tensor("out", [S, D], F32, kind="ExternalOutput")
    xn_hbm = nc.dram_tensor("xn_hbm", [D, S], BF16, kind="Internal")
    part_hbm = nc.dram_tensor("part_hbm", [S, D], F32, kind="Internal")
    with ExitStack() as ctx:
        b = alloc_bufs(nc, ctx, S, l1=True)
        E = ctx.enter_context
        sm = Bag()
        mksems(nc, ctx, sm)
        block = E(nc.Block())
        p = Plan()
        emit_consts(p, b, sm)
        emit_l1_body(p, b, sm, h, h.x, h.out, xn_hbm, part_hbm)
        p.op("sync", lambda e: e.nop(),
             waits=[(sm.sOA, sm.sOA.n), (sm.sOB, sm.sOB.n)])
        p.realize(block)
    return nc


def build_fused(S):
    nc = bass.Bass("TRN2", target_bir_lowering=False, detect_race_conditions=False)
    h = Bag()
    h.x = nc.dram_tensor("x", [S, D], F32, kind="ExternalInput")
    h0 = Bag(x=h.x)
    for nm, shp in (("wqT", [D, D]), ("wkT", [D, D]), ("wvT", [D, D]),
                    ("woT", [D, D]), ("wfT", [D, D]),
                    ("bq", [128, NP]), ("bk", [128, NP]), ("bv", [128, NP]),
                    ("bo", [2, 512])):
        h0[nm] = nc.dram_tensor("l0_" + nm, shp, F32, kind="ExternalInput")
    h1 = Bag()
    for nm, shp in (("wqT", [D, D]), ("wkT", [D, D]), ("wvT", [D, D]),
                    ("woT", [D, D]), ("bq", [128, NP]), ("bk", [128, NP]),
                    ("bv", [128, NP]), ("bo", [2, 512]),
                    ("sgT", [D, IFF]), ("suT", [D, IFF]), ("sdT", [IFF, D]),
                    ("gw", [NE, D, IFF]), ("uw", [NE, D, IFF]),
                    ("dw", [NE, IFF, D]), ("rwT", [D, NE]), ("rb", [NE, 1])):
        h1[nm] = nc.dram_tensor("l1_" + nm, shp, F32, kind="ExternalInput")
    h.out = nc.dram_tensor("out", [S, D], F32, kind="ExternalOutput")
    x1_hbm = nc.dram_tensor("x1_hbm", [S, D], F32, kind="Internal")
    xn_hbm = nc.dram_tensor("xn_hbm", [D, S], BF16, kind="Internal")
    part_hbm = nc.dram_tensor("part_hbm", [S, D], F32, kind="Internal")
    with ExitStack() as ctx:
        b = alloc_bufs(nc, ctx, S, l1=True)
        E = ctx.enter_context
        sm = Bag()
        mksems(nc, ctx, sm)
        block = E(nc.Block())
        p = Plan()
        emit_consts(p, b, sm)
        emit_l0_body(p, b, sm, h0, x1_hbm)
        h1.x = None
        emit_l1_body(p, b, sm, h1, x1_hbm, h.out, xn_hbm, part_hbm)
        p.op("sync", lambda e: e.nop(),
             waits=[(sm.sOA, sm.sOA.n), (sm.sOB, sm.sOB.n)])
        p.realize(block)
    return nc


def _l0_inputs(W_qkv, b_qkv, W_out, b_out, Wf):
    idx = (np.arange(NH)[:, None] * 192 + np.arange(DH)[None, :]).ravel()
    d = dict(
        wqT=np.ascontiguousarray(W_qkv[idx].T),
        wkT=np.ascontiguousarray(W_qkv[idx + 64].T),
        wvT=np.ascontiguousarray(W_qkv[idx + 128].T),
        woT=np.ascontiguousarray(W_out.T),
        wfT=np.ascontiguousarray(Wf.T),
        bq=np.ascontiguousarray(b_qkv[idx].reshape(NP, 128).T),
        bk=np.ascontiguousarray(b_qkv[idx + 64].reshape(NP, 128).T),
        bv=np.ascontiguousarray(b_qkv[idx + 128].reshape(NP, 128).T),
        bo=np.ascontiguousarray(b_out.reshape(2, 512)),
    )
    return d


def _l1_inputs(Wq, bq, Wk, bk, Wv, bv, Wo, bo, sg, su, sd, gw, uw, dw, rw, rb):
    return dict(
        wqT=np.ascontiguousarray(Wq.T), wkT=np.ascontiguousarray(Wk.T),
        wvT=np.ascontiguousarray(Wv.T), woT=np.ascontiguousarray(Wo.T),
        bq=np.ascontiguousarray(bq.reshape(NP, 128).T),
        bk=np.ascontiguousarray(bk.reshape(NP, 128).T),
        bv=np.ascontiguousarray(bv.reshape(NP, 128).T),
        bo=np.ascontiguousarray(bo.reshape(2, 512)),
        sgT=np.ascontiguousarray(sg.T), suT=np.ascontiguousarray(su.T),
        sdT=np.ascontiguousarray(sd.T),
        gw=np.ascontiguousarray(gw), uw=np.ascontiguousarray(uw),
        dw=np.ascontiguousarray(dw),
        rwT=np.ascontiguousarray(rw.T),
        rb=np.ascontiguousarray(rb.reshape(NE, 1)),
    )


def kernel(L0_attn_x, L0_attn_W_qkv, L0_attn_b_qkv, L0_attn_W_out, L0_attn_b_out,
           L0_ffn_weight, L1_attn_Wq, L1_attn_bq, L1_attn_Wk, L1_attn_bk,
           L1_attn_Wv, L1_attn_bv, L1_attn_Wo, L1_attn_bo, L1_ffn_shared_gate,
           L1_ffn_shared_up, L1_ffn_shared_down, L1_ffn_gate_w, L1_ffn_up_w,
           L1_ffn_down_w, L1_ffn_router_w, L1_ffn_routing_bias):
    S = L0_attn_x.shape[1]
    x = np.ascontiguousarray(np.asarray(L0_attn_x, dtype=np.float32))
    w0 = _l0_inputs(np.asarray(L0_attn_W_qkv), np.asarray(L0_attn_b_qkv),
                    np.asarray(L0_attn_W_out), np.asarray(L0_attn_b_out),
                    np.asarray(L0_ffn_weight))
    w1 = _l1_inputs(np.asarray(L1_attn_Wq), np.asarray(L1_attn_bq),
                    np.asarray(L1_attn_Wk), np.asarray(L1_attn_bk),
                    np.asarray(L1_attn_Wv), np.asarray(L1_attn_bv),
                    np.asarray(L1_attn_Wo), np.asarray(L1_attn_bo),
                    np.asarray(L1_ffn_shared_gate), np.asarray(L1_ffn_shared_up),
                    np.asarray(L1_ffn_shared_down), np.asarray(L1_ffn_gate_w),
                    np.asarray(L1_ffn_up_w), np.asarray(L1_ffn_down_w),
                    np.asarray(L1_ffn_router_w), np.asarray(L1_ffn_routing_bias))
    wall = {("l0_" + k): v for k, v in w0.items()}
    wall.update({("l1_" + k): v for k, v in w1.items()})
    nc = build_fused(S)
    maps = [dict(wall, x=np.ascontiguousarray(x[c])) for c in range(NCORES)]
    res = run_bass_kernel_spmd(nc, maps, core_ids=list(range(NCORES)))
    out = np.stack([res.results[c]["out"] for c in range(NCORES)], axis=0)
    return out.astype(np.float32)
